# version 10
# baseline (speedup 1.0000x reference)
import contextlib
import numpy as np
import concourse.bass as bass
import concourse.mybir as mybir
from concourse.bass_utils import run_bass_kernel_spmd

F32 = mybir.dt.float32
BF16 = mybir.dt.bfloat16
I32 = mybir.dt.int32
AF = mybir.ActivationFunctionType
ALU = mybir.AluOpType
AX = mybir.AxisListType

BATCH = 2
DM = 1024
DI = 2048
DST = 16
GRP = 4
KD = DM // 128
KI = DI // 128
MI = 2 * DI // 128
MF = 4096 // 128
KH = 4096 // 128
HALO = 3
DXW = DI + 2 * DST
DXP = DI + 128


class Ctr:
    def __init__(self, h):
        self.h = h
        self.v = 0


def build_program(T_total=8192, CH=512, debug=False, vec_limit=None, bar_tier=0):
    NCHUNK = T_total // CH
    RNDS = NCHUNK // GRP
    RT = GRP * CH
    SCH = RT // 2
    TLOC = RNDS * CH
    CHH = CH + HALO
    DQ = DI // GRP
    NDT = DQ // 128
    NSLOT = 3
    assert CH <= 512

    nc = bass.Bass("TRN2", target_bir_lowering=False,
                   detect_race_conditions=False, dynamic_dma_scratch_size=16384)

    def EIN(name, shape, dt=F32):
        return nc.dram_tensor(name, shape, dt, kind="ExternalInput")

    x_fm = EIN("x_fm", [DM, RNDS * CHH])
    WinT = EIN("WinT", [DM, 2 * DI])
    WxT = EIN("WxT", [DI, DXW])
    WdtT = EIN("WdtT", [DI, DI])
    WoutT = EIN("WoutT", [DI, DM])
    w1T_myk = EIN("w1T_myk", [128, 4096])
    w2T_myk = EIN("w2T_myk", [512, DM])
    w1_rows = EIN("w1_rows", [512, DM])
    w2_rows = EIN("w2_rows", [128, 4096])
    bconv_t = EIN("bconv_t", [128, KI])
    bdt_t = EIN("bdt_t", [128, KI])
    wcv_t = EIN("wcv_t", [128, KI * 4])
    Alog_my = EIN("Alog_my", [128, NDT * DST])
    D_my = EIN("D_my", [128, NDT])
    b1_t = EIN("b1_t", [128, MF])
    b2_t = EIN("b2_t", [128, KD])
    rank_in = EIN("rank_in", [1, 4], I32)

    out_fm = nc.dram_tensor("out_fm", [DM, TLOC], F32, kind="ExternalOutput")

    dbg = {}
    if debug:
        for nm, n, dt in [("dsend0", DI * CH, BF16), ("xbc0", DXP * CH, BF16),
                          ("y0", DQ * RT, BF16), ("sc", 64, F32),
                          ("h0", 4096 * TLOC, BF16), ("g0", DM * TLOC, BF16),
                          ("s0", RNDS * DM * CH, BF16), ("xn0", RNDS * DM * CHH, BF16),
                          ("wq", 8 * (128 * 4096 + 512 * DM), BF16), ("xq", 2 * DI * CH, BF16),
                          ("q1", 128 * 2 * MF, F32),
                          ("wqin", 128 * 4096 + 512 * DM, BF16),
                          ("swq", 8 * 1280, F32), ("ar1", 160, F32),
                          ("xm", 128 * 16, F32), ("hq0", 128 * 512, BF16),
                          ("q2", 128 * 2 * KD, F32),
                          ("carry", 128 * NDT * DST, F32)]:
            dbg[nm] = nc.dram_tensor("dbg_" + nm, [n], dt, kind="ExternalOutput")

    wbf_in = nc.dram_tensor("wbf_in", [DM * 2 * DI], BF16)
    wbf_x = nc.dram_tensor("wbf_x", [DI * DXW], BF16)
    wbf_dt = nc.dram_tensor("wbf_dt", [DI * DI], BF16)
    wbf_out = nc.dram_tensor("wbf_out", [DI * DM], BF16)
    z_dram = nc.dram_tensor("z_dram", [RNDS * DI * CH], BF16)
    xn_dram = nc.dram_tensor("xn_dram", [RNDS * DM * CHH], BF16)
    h_dram = nc.dram_tensor("h_dram", [4096 * TLOC], BF16)
    g_dram = nc.dram_tensor("g_dram", [DM * TLOC], BF16)
    s_dram = nc.dram_tensor("s_dram", [RNDS * DM * CH], BF16)
    rowD = nc.dram_tensor("rowD", [RNDS * 2 * CHH], F32)
    sclD = nc.dram_tensor("sclD", [64], F32)
    swq_in = nc.dram_tensor("swq_in", [1280], F32)
    swq_out = nc.dram_tensor("swq_out", [8 * 1280], F32)
    WQBLK = 128 * 4096 + 512 * DM
    wq_in = nc.dram_tensor("wq_in", [WQBLK], BF16)
    wq_out = nc.dram_tensor("wq_out", [8 * WQBLK], BF16)
    mx_in = nc.dram_tensor("mx_in", [16], F32)
    mx1_out = nc.dram_tensor("mx1_out", [16], F32)
    mx2_out = nc.dram_tensor("mx2_out", [16], F32)
    flipD = nc.dram_tensor("flipD", [128], F32)
    dSend = nc.dram_tensor("dSend", [DI * CH], BF16)
    xbcSend = nc.dram_tensor("xbcSend", [DXP * CH], BF16)
    ySend = nc.dram_tensor("ySend", [DQ * RT], BF16)
    dAGo = [nc.dram_tensor(f"dAGo{i}", [GRP * DI * CH], BF16) for i in range(2)]
    xbcAGo = [nc.dram_tensor(f"xbcAGo{i}", [GRP * DXP * CH], BF16) for i in range(2)]
    yAGo = [nc.dram_tensor(f"yAGo{i}", [GRP * DQ * RT], BF16) for i in range(2)]

    G4 = [[0, 1, 2, 3], [4, 5, 6, 7]]
    G8 = [[0, 1, 2, 3, 4, 5, 6, 7]]

    plan = {k: [] for k in ("sync", "tensor", "vector", "scalar", "gpsimd")}

    def put(eng, fn):
        plan[eng].append(fn)

    def mark(eng, fn, ctr, n=1):
        ctr.v += n
        plan[eng].append(lambda e, f=fn, c=ctr, m=n: f(e).then_inc(c.h, m))
        return ctr.v

    stack = contextlib.ExitStack()

    def sbuf(name, shape, dt):
        return stack.enter_context(nc.sbuf_tensor(name, shape, dt))

    def mksem(name):
        return Ctr(stack.enter_context(nc.semaphore(name)))

    dSY = Ctr(None)
    dGP = Ctr(None)
    dSC = Ctr(None)
    cc = mksem("cc")
    tM = mksem("tM")
    vM = mksem("vM")
    sM = mksem("sM")
    SEMS = {"dSY": dSY, "dGP": dGP, "dSC": dSC, "cc": cc, "tM": tM, "vM": vM, "sM": sM}

    RQ = 8

    class DmaTrack:
        def __init__(self, name, engname, ctr):
            self.eng = engname
            self.ctr = ctr
            self.S = [mksem(f"{name}S{r}").h for r in range(RQ)]
            self.F = mksem(f"{name}F").h
            self.n = 0
            self.slot_cum = []
            self.uses = [0] * RQ
            self.covered = 0
            self.fval = 0
            self.covers = []

    trS = DmaTrack("qS", "sync", dSY)
    trG = DmaTrack("qG", "gpsimd", dGP)
    trC = DmaTrack("qC", "scalar", dSC)
    TRACKS = {"dSY": trS, "dGP": trG, "dSC": trC}

    def ensure_relay(tr, idx):
        idx = min(idx, tr.n)
        if idx <= tr.covered:
            return
        targets = {}
        for j in range(tr.covered + 1, idx + 1):
            r, cnt = tr.slot_cum[j - 1]
            targets[r] = cnt
        for r, cnt in targets.items():
            plan[tr.eng].append(lambda e, s=tr.S[r], w=16 * cnt: e.wait_ge(s, w))
        tr.fval += 1
        plan[tr.eng].append(lambda e, s=tr.F: e.nop(nofuse=True).then_inc(s, 1))
        tr.covered = idx
        tr.covers.append((idx, tr.fval))

    def fval_for(tr, idx):
        ensure_relay(tr, idx)
        for hi, fv in tr.covers:
            if hi >= idx:
                return fv
        raise AssertionError("relay lookup failed")

    def dma_issue(tr, fn):
        if tr.n + 1 > RQ:
            ensure_relay(tr, tr.n + 1 - RQ)
        tr.n += 1
        r = (tr.n - 1) % RQ
        tr.uses[r] += 1
        tr.slot_cum.append((r, tr.uses[r]))
        plan[tr.eng].append(lambda e, f=fn, s=tr.S[r]: f(e).then_inc(s, 16))
        tr.ctr.v += 16
        return tr.ctr.v

    BAR = mksem("BAR")
    nbar = [0]
    ENGS = ("sync", "tensor", "vector", "scalar", "gpsimd")

    def quiesce(tier=1):
        if tier > bar_tier and tier >= 0:
            return
        for tr in (trS, trG, trC):
            ensure_relay(tr, tr.n)
        if cc.v > 0:
            plan["gpsimd"].append(lambda e, c=cc, w=cc.v: e.wait_ge(c.h, w))
        nbar[0] += 1
        tgt = 5 * nbar[0]
        for eng in ENGS:
            plan[eng].append(lambda e, s=BAR.h: e.nop(nofuse=True).then_inc(s, 1))
            plan[eng].append(lambda e, s=BAR.h, w=tgt: e.wait_ge(s, w))

    def wait(eng, sem, v=None):
        name = sem if isinstance(sem, str) else None
        ctr = SEMS[sem] if isinstance(sem, str) else sem
        vv = ctr.v if v is None else v
        if vv <= 0:
            return vv
        if name in TRACKS:
            tr = TRACKS[name]
            idx = (vv + 15) // 16
            fv = fval_for(tr, idx)
            plan[eng].append(lambda e, s=tr.F, w=fv: e.wait_ge(s, w))
        else:
            plan[eng].append(lambda e, c=ctr, w=vv: e.wait_ge(c.h, w))
        return vv

    onesc_b = sbuf("onesc_b", [128, 1], BF16)
    rank_sb = sbuf("rank_sb", [1, 4], I32)
    bconv_sb = sbuf("bconv_sb", [128, KI], F32)
    bdt_sb = sbuf("bdt_sb", [128, KI], F32)
    wcv_sb = sbuf("wcv_sb", [128, KI * 4], F32)
    A_sb = sbuf("A_sb", [128, NDT * DST], F32)
    D_sb = sbuf("D_sb", [128, NDT], F32)
    b1_sb = sbuf("b1_sb", [128, MF], F32)
    b2_sb = sbuf("b2_sb", [128, KD], F32)
    sw1_sb = sbuf("sw1_sb", [128, MF], F32)
    sw2_sb = sbuf("sw2_sb", [128, KD], F32)
    scale1_sb = sbuf("scale1_sb", [128, MF], F32)
    bq1_sb = sbuf("bq1_sb", [128, MF], F32)
    scale2_sb = sbuf("scale2_sb", [128, KD], F32)
    bq2_sb = sbuf("bq2_sb", [128, KD], F32)
    scl_sb = sbuf("scl_sb", [128, 4], F32)
    qs_sb = sbuf("qs_sb", [128, 24], F32)
    srow = sbuf("srow", [1, 6 * CHH], F32)
    carry = sbuf("carry", [128, NDT * DST], F32)
    xmax_sb = sbuf("xmax_sb", [128, 2], F32)
    r16 = sbuf("r16", [1, 16], F32)
    rcp5_sb = sbuf("rcp5_sb", [128, 5], F32)
    rcp1_sb = sbuf("rcp1_sb", [1, 1], F32)
    dbgx_sb = sbuf("dbgx_sb", [128, 16], F32)

    xf32 = sbuf("xf32", [128, 2 * CHH], F32)
    xsqb = sbuf("xsqb", [128, 2 * CHH], BF16)
    xb_sb = sbuf("xb_sb", [128, KD * CHH], BF16)
    xn_sb = sbuf("xn_sb", [128, KD * CHH], BF16)
    rbc_sb = sbuf("rbc_sb", [128, CHH], BF16)
    mbc_sb = sbuf("mbc_sb", [128, CHH], BF16)
    WF = max(2 * 2048, 2 * SCH)
    wf32 = sbuf("wf32", [128, WF], F32)
    wbf = sbuf("wbf", [128, 2 * 2048], BF16)
    xin_sb = sbuf("xin_sb", [128, 3 * CHH], BF16)
    xcv_sb = sbuf("xcv_sb", [128, KI * CH], BF16)
    drw_sb = sbuf("drw_sb", [128, KI * CH], BF16)
    stg_sb = sbuf("stg_sb", [128, 2 * CH], BF16)
    stg2_sb = sbuf("stg2_sb", [128, 2 * CH], BF16)
    bcstg = sbuf("bcstg", [32, CH], BF16)

    dR_sb = sbuf("dR_sb", [128, NSLOT * NDT * SCH], BF16)
    xcR_sb = sbuf("xcR_sb", [128, NSLOT * NDT * SCH], BF16)
    b_sb = sbuf("b_sb", [128, SCH], BF16)
    st_sb = sbuf("st_sb", [128, SCH], F32)
    pr_sb = sbuf("pr_sb", [128, SCH], BF16)
    Bex_sb = sbuf("Bex_sb", [128, 2 * SCH], BF16)
    Cex_sb = sbuf("Cex_sb", [128, 2 * SCH], BF16)
    yacc_sb = sbuf("yacc_sb", [128, NDT * SCH], BF16)
    yfin_sb = sbuf("yfin_sb", [128, 2 * SCH], BF16)

    yt_sb = sbuf("yt_sb", [128, KI * CH], BF16)
    szt_sb = sbuf("szt_sb", [128, 2 * CH], BF16)

    iq_sb = sbuf("iq_sb", [128, max(2 * CH, 1024)], I32)
    wqb_sb = sbuf("wqb_sb", [128, 1024], BF16)
    invbc_sb = wf32[:, 1024:2048]
    wqi_sb = iq_sb
    xro_sb = sbuf("xro_sb", [128, CH], F32)
    out_sb = sbuf("out_sb", [128, CH], F32)

    ps = [stack.enter_context(nc.psum_tensor(f"ps{i}", [128, 512], F32)) for i in range(8)]

    regs = {}

    def dmaS(out, in_):
        return dma_issue(trS, lambda e, o=out, i=in_: e.dma_start(out=o, in_=i))

    def dma_slot(eng, buf, slot, out, in_):
        ctr = buf.sems[slot]
        ctr.v += 16
        plan[eng].append(lambda e, o=out, i=in_, c=ctr: e.dma_start(out=o, in_=i)
                         .then_inc(c.h, 16))
        return ctr, ctr.v

    def dmaG(out, in_):
        return dma_issue(trG, lambda e, o=out, i=in_: e.dma_start(out=o, in_=i))

    def dmaC(out, in_):
        return dma_issue(trC, lambda e, o=out, i=in_: e.dma_start(out=o, in_=i))

    def dmaS_nc(out, in_):
        return dma_issue(trS, lambda e, o=out, i=in_: e.dma_start(
            out=o, in_=i, allow_slow_non_contiguous=True))

    def dmaG_nc(out, in_):
        return dma_issue(trG, lambda e, o=out, i=in_: e.dma_start(
            out=o, in_=i, allow_slow_non_contiguous=True))

    class Buf:
        def __init__(self, n, name=None):
            self.g = [[] for _ in range(n)]
            self.sems = [mksem(f"{name}{i}") for i in range(n)] if name else None

        def acquire(self, eng, slot):
            for s, v in self.g[slot]:
                wait(eng, s, v)
            self.g[slot] = []

        def consumed(self, slot, sem_name, v):
            self.g[slot].append((sem_name, v))

    xf32B = Buf(2, "bXF")
    wbfB = Buf(2, "bWB")
    wf32B = Buf(2, "bWF")
    stgB = Buf(2)
    stg2B = Buf(2)
    xinB = Buf(3)
    aB = Buf(2)
    BexB = Buf(2)
    CexB = Buf(2)
    yfinB = Buf(2)
    iqB = Buf(2)
    xroB = Buf(1)
    outB = Buf(1)
    sztB = Buf(2)

    bank_ev = [("sM", 0)] * 8

    def te_take(bk):
        s, v = bank_ev[bk]
        if v > 0:
            wait("tensor", s, v)

    def matgroup(emitters):
        for f in emitters[:-1]:
            put("tensor", f)
        return mark("tensor", emitters[-1], tM)

    dmaS(rank_sb[:], rank_in.ap())
    dmaS(bconv_sb[:], bconv_t.ap())
    dmaS(bdt_sb[:], bdt_t.ap())
    dmaS(wcv_sb[:], wcv_t.ap())
    dmaS(A_sb[:], Alog_my.ap())
    dmaS(D_sb[:], D_my.ap())
    dmaS(b1_sb[:], b1_t.ap())
    ld_setup = dmaS(b2_sb[:], b2_t.ap())

    wait("sync", "dSY", 16)

    def _ld_rank(e):
        regs["q"] = e.value_load(rank_sb[0:1, 0:1])
        regs["qCH"] = e.value_load(rank_sb[0:1, 1:2])
        regs["qn4"] = e.value_load(rank_sb[0:1, 2:3])
    put("sync", _ld_rank)

    put("vector", lambda e: e.memset(onesc_b[:], 1.0))
    put("vector", lambda e: e.memset(carry[:], 0.0))
    put("vector", lambda e: e.memset(r16[:], 0.0))
    put("vector", lambda e: e.memset(srow[0:1, 4 * CHH:4 * CHH + 1], 1e-5))
    mark("vector", lambda e: e.memset(xmax_sb[:], -1e30), vM)
    m_z = mark("vector", lambda e: e.memset(yacc_sb[0:96, 0:CH], 0.0), vM)
    wait("gpsimd", "vM", m_z)
    g_pad = dmaG(bass.AP(xbcSend, DXW * CH, [[CH, 96], [1, CH]]), yacc_sb[0:96, 0:CH])
    wait("vector", "dGP", g_pad)

    wait("scalar", "dSY", ld_setup)
    put("scalar", lambda e: e.activation(A_sb[:], A_sb[:], AF.Exp))
    mark("scalar", lambda e: e.mul(A_sb[:], A_sb[:], -1.0), sM)

    for i4 in range(4):
        sl = i4 % 2
        ld = dmaS(wf32[:, sl * 2048: sl * 2048 + DM], w1_rows.ap()[i4 * 128:(i4 + 1) * 128, :])
        wait("vector", "dSY", ld)
        put("vector", lambda e, i=i4, s=sl: e.tensor_reduce(
            qs_sb[:, i:i + 1], wf32[:, s * 2048: s * 2048 + DM],
            axis=AX.X, op=ALU.max, apply_absolute_value=True))
    ld = dmaS(wf32[:, 0:2048], w2_rows.ap()[:, 0:2048])
    ld = dmaS(wf32[:, 2048:4096], w2_rows.ap()[:, 2048:4096])
    wait("vector", "dSY", ld)
    put("vector", lambda e: e.tensor_reduce(qs_sb[:, 4:5], wf32[:, 0:2048],
                                            axis=AX.X, op=ALU.max, apply_absolute_value=True))
    v_hz = mark("vector", lambda e: e.tensor_reduce(qs_sb[:, 5:6], wf32[:, 2048:4096],
                                                    axis=AX.X, op=ALU.max,
                                                    apply_absolute_value=True), vM)
    wait("vector", "vM", v_hz)
    v_hz = mark("vector", lambda e: e.tensor_max(qs_sb[:, 4:5], qs_sb[:, 4:5],
                                                 qs_sb[:, 5:6]), vM)
    wait("vector", "vM", v_hz)
    put("vector", lambda e: e.tensor_scalar_mul(qs_sb[:, 8:13], qs_sb[:, 0:5], 1.0 / 127.0))
    v_hz = mark("vector", lambda e: e.reciprocal(rcp5_sb[:], qs_sb[:, 0:5]), vM)
    wait("vector", "vM", v_hz)
    v_sw = mark("vector", lambda e: e.tensor_scalar_mul(qs_sb[:, 16:21], rcp5_sb[:],
                                                        127.0), vM)
    wait("gpsimd", "vM", v_sw)
    dmaG_nc(bass.AP(swq_in, 0, [[1, 128], [128, 4]]), qs_sb[:, 8:12])
    dmaG_nc(bass.AP(swq_in, 512, [[1, 128], [128, 4]]), qs_sb[:, 16:20])
    dmaG_nc(bass.AP(swq_in, 1024, [[1, 128], [128, 1]]), qs_sb[:, 12:13])
    g_pack = dmaG_nc(bass.AP(swq_in, 1152, [[1, 128], [128, 1]]), qs_sb[:, 20:21])
    wait("gpsimd", "dGP", g_pack)
    cc_sw = mark("gpsimd", lambda e: e.collective_compute(
        "AllGather", ALU.bypass, ins=[swq_in.ap()], outs=[swq_out.ap()],
        replica_groups=G8), cc)

    wait("sync", "cc", cc_sw)
    for r in range(8):
        dmaS_nc(sw1_sb[:, 4 * r:4 * r + 4], bass.AP(swq_out, r * 1280, [[1, 128], [128, 4]]))
    dmaS_nc(sw2_sb[:, 0:KD], bass.AP(swq_out, 1024, [[1, 128], [1280, KD]]))

    v_q_prev = 0
    g_prev = 0
    for p4 in range(4):
        if v_q_prev > 0:
            wait("sync", "vM", v_q_prev)
        bcv = dmaS(wf32[:, 1024:2048], bass.AP(swq_out, 2 * p4 * 1280 + 512,
                                               [[0, 128], [1280, 2], [1, 512]]))
        off = (p4 % 2) * 2048
        ldw = dmaS(wf32[:, off:off + 1024], w1T_myk.ap()[:, p4 * 1024:(p4 + 1) * 1024])
        wait("vector", "dSY", max(bcv, ldw))
        v_q = mark("vector", lambda e, o=off: e.tensor_tensor(
            wf32[:, o:o + 1024], wf32[:, o:o + 1024], wf32[:, 1024:2048], ALU.mult), vM)
        v_q_prev = v_q
        wait("scalar", "vM", v_q)
        put("scalar", lambda e, o=off: e.copy(wqi_sb[:, 0:1024], wf32[:, o:o + 1024]))
        if g_prev > 0:
            wait("scalar", "dGP", g_prev)
        s_q = mark("scalar", lambda e: e.copy(wqb_sb[:], wqi_sb[:]), sM)
        wait("gpsimd", "sM", s_q)
        g_prev = dmaG(bass.AP(wq_in, p4 * 1024, [[4096, 128], [1, 1024]]), wqb_sb[:])
    wait("sync", "vM", v_q_prev)
    bcv2 = dmaS(wf32[:, 1024:2048], bass.AP(swq_out, 1152, [[0, 128], [1280, 8], [1, 128]]))
    for i4 in range(4):
        if i4 >= 2:
            wait("sync", "vM", v_q_prev)
        off = (i4 % 2) * 2048
        ldw = dmaS(wf32[:, off:off + 1024], w2T_myk.ap()[i4 * 128:(i4 + 1) * 128, :])
        wait("vector", "dSY", max(bcv2, ldw))
        v_q = mark("vector", lambda e, o=off: e.tensor_tensor(
            wf32[:, o:o + 1024], wf32[:, o:o + 1024], wf32[:, 1024:2048], ALU.mult), vM)
        v_q_prev = v_q
        wait("scalar", "vM", v_q)
        put("scalar", lambda e, o=off: e.copy(wqi_sb[:, 0:1024], wf32[:, o:o + 1024]))
        wait("scalar", "dGP", g_prev)
        s_q = mark("scalar", lambda e: e.copy(wqb_sb[:], wqi_sb[:]), sM)
        wait("gpsimd", "sM", s_q)
        g_prev = dmaG(bass.AP(wq_in, 128 * 4096 + i4 * 131072, [[1024, 128], [1, 1024]]),
                      wqb_sb[:])
    wf32B.consumed(0, "vM", v_q_prev)
    wf32B.consumed(1, "vM", v_q_prev)
    wait("gpsimd", "dGP", dGP.v)
    cc_wq = mark("gpsimd", lambda e: e.collective_compute(
        "AllGather", ALU.bypass, ins=[wq_in.ap()], outs=[wq_out.ap()],
        replica_groups=G8), cc)

    for m in range(KD):
        sl = m % 2
        wf32B.acquire("sync", sl)
        ldw = dmaS(wf32[:, sl * 2048:sl * 2048 + 2048],
                   bass.AP(WoutT.ap().tensor, m * 128, [[DM, 128], [128 * DM, KI], [1, 128]]))
        wbfB.acquire("scalar", sl)
        wait("scalar", "dSY", ldw)
        s_c = mark("scalar", lambda e, s=sl: e.copy(
            wbf[:, s * 2048:s * 2048 + 2048], wf32[:, s * 2048:s * 2048 + 2048]), sM)
        wf32B.consumed(sl, "sM", s_c)
        wait("gpsimd", "sM", s_c)
        g_sp = dmaG(bass.AP(wbf_out, m * 128, [[DM, 128], [128 * DM, KI], [1, 128]]),
                    wbf[:, sl * 2048:sl * 2048 + 2048])
        wbfB.consumed(sl, "dGP", g_sp)
    quiesce(tier=-1)
    setup_v = vM.v

    recv_half = {}
    cm_last = {}
    scan_seq = []
    scan_mk = {}
    a_seq = []
    bex_marks = {}
    bex_seq = []
    yg_consumed = {}
    ag_mark = {}
    ysend_mark = {}
    xcsend_mark = {}
    prev = {"t_stats": 0, "v_xn": 0, "win_last": 0, "wx_last": 0, "wdt_last": 0,
            "xn_spill": 0, "conv_last": 0, "wout_last": 0, "v_yg": 0,
            "bex_round_last": {}, "recv_last": {}, "rows_dma": 0, "v_xm": 0}
    conv_mark = {}
    zmark = {}

    def w_stream(j, f32_t, bf_t, row_stride, m, mw, kk, off=None):
        sl = m % 2
        n = kk * mw
        if off is None:
            off = m * mw
        dst = wbf[:, sl * 2048: sl * 2048 + n]
        if j == 0:
            wf32B.acquire("sync", sl)
            lc, lv = dma_slot("sync", wf32B, sl, wf32[:, sl * 2048:sl * 2048 + n],
                              bass.AP(f32_t, off,
                                      [[row_stride, 128], [128 * row_stride, kk],
                                       [1, mw]]))
            wbfB.acquire("scalar", sl)
            wait("scalar", lc, lv)
            s_c = mark("scalar", lambda e, s=sl, n=n: e.copy(
                wbf[:, s * 2048:s * 2048 + n], wf32[:, s * 2048:s * 2048 + n]), sM)
            wf32B.consumed(sl, "sM", s_c)
            wait("gpsimd", "sM", s_c)
            g_sp = dmaG(bass.AP(bf_t, off,
                                [[row_stride, 128], [128 * row_stride, kk], [1, mw]]), dst)
            wbfB.consumed(sl, "dGP", g_sp)
            return dst, ("sM", s_c), sl
        wbfB.acquire("sync", sl)
        lc, lv = dma_slot("sync", wbfB, sl, dst,
                          bass.AP(bf_t, off,
                                  [[row_stride, 128], [128 * row_stride, kk], [1, mw]]))
        return dst, (lc, lv), sl

    def emit_recv(j):
        jb = j % 2
        wait("sync", "cc", ag_mark[("d", j)])
        wait("sync", "cc", ag_mark[("x", j)])
        dview = dAGo[jb].ap().rearrange("(r n p t) -> p n r t", r=GRP, p=128, t=CH)
        xview = xbcAGo[jb].ap().rearrange("(r n p t) -> p n r t", r=GRP, p=128, t=CH)
        for h in range(2):
            gslot = (2 * j + h) % NSLOT
            if 2 * j + h >= NSLOT:
                pidx = 2 * j + h - NSLOT
                pj, ph = pidx // 2, pidx % 2
                wait("sync", "vM", cm_last[(pj, ph, DST - 1)])
                wait("sync", "vM", yfin_last.get((pj, ph), 0))
            for rr2 in range(2):
                def fd(e, h=h, gs=gslot, dv=dview, rr2=rr2):
                    o = dR_sb[:, gs * NDT * SCH:(gs + 1) * NDT * SCH].rearrange(
                        "p (n x) -> p n x", n=NDT, x=SCH)[:, :, rr2 * CH:(rr2 + 1) * CH]
                    return e.dma_start(
                        out=o, in_=dv[:, bass.ds(regs["qn4"], NDT),
                                      2 * h + rr2:2 * h + rr2 + 1, :])
                dma_issue(trS, fd)
                def fx(e, h=h, gs=gslot, xv=xview, rr2=rr2):
                    o = xcR_sb[:, gs * NDT * SCH:(gs + 1) * NDT * SCH].rearrange(
                        "p (n x) -> p n x", n=NDT, x=SCH)[:, :, rr2 * CH:(rr2 + 1) * CH]
                    return e.dma_start(
                        out=o, in_=xv[:, bass.ds(regs["qn4"], NDT),
                                      2 * h + rr2:2 * h + rr2 + 1, :])
                recv_half[(j, h)] = dma_issue(trS, fx)
        prev["recv_last"][j] = recv_half[(j, 1)]

    yfin_last = {}

    def emit_scan_half_gen(j, h):
        jb = j % 2
        gslot = (2 * j + h) % NSLOT
        wait("scalar", "dSY", recv_half[(j, h)])
        wait("vector", "dSY", recv_half[(j, h)])
        for s in range(DST):
            eb = len(bex_seq) % 2
            if len(bex_seq) >= 2:
                pk = bex_seq[-2]
                BexB.acquire("scalar", eb)
                CexB.acquire("scalar", eb)
            dmaC(Bex_sb[:, eb * SCH:(eb + 1) * SCH],
                 bass.AP(xbcAGo[jb], (2 * h) * DXP * CH + (DI + s) * CH,
                         [[0, 128], [DXP * CH, 2], [1, CH]]))
            bx = dmaC(Cex_sb[:, eb * SCH:(eb + 1) * SCH],
                      bass.AP(xbcAGo[jb], (2 * h) * DXP * CH + (DI + DST + s) * CH,
                              [[0, 128], [DXP * CH, 2], [1, CH]]))
            bex_marks[(j, h, s)] = bx
            bex_seq.append((j, h, s))
            wait("vector", "dSC", bx)
            for dt in range(NDT):
                ai = len(a_seq) % 2
                aB.acquire("scalar", ai)
                src = dR_sb[:, (gslot * NDT + dt) * SCH:(gslot * NDT + dt + 1) * SCH]
                av = wf32[:, ai * SCH:(ai + 1) * SCH]
                a_v = mark("scalar", lambda e, src=src, av=av, dt=dt, s=s: e.activation(
                    av, src, AF.Exp, scale=A_sb[:, dt * DST + s:dt * DST + s + 1]), sM)
                a_seq.append((j, h, s, dt))
                xc = xcR_sb[:, (gslot * NDT + dt) * SCH:(gslot * NDT + dt + 1) * SCH]
                Bx = Bex_sb[:, eb * SCH:(eb + 1) * SCH]
                Cx = Cex_sb[:, eb * SCH:(eb + 1) * SCH]
                wait("vector", "sM", a_v)
                bbv = mark("vector", lambda e, xc=xc, Bx=Bx: e.tensor_mul(b_sb[:], xc, Bx), vM)
                sc_v = mark("vector", lambda e, av=av, dt=dt, s=s: e.tensor_tensor_scan(
                    st_sb[:], av, b_sb[:], carry[:, dt * DST + s:dt * DST + s + 1],
                    ALU.mult, ALU.add), vM)
                scan_mk[(j, h, s, dt)] = sc_v
                aB.consumed(ai, "vM", sc_v)
                wait("vector", "vM", sc_v)
                put("vector", lambda e, dt=dt, s=s: e.tensor_copy(
                    carry[:, dt * DST + s:dt * DST + s + 1], st_sb[:, SCH - 1:SCH]))
                cmv = mark("vector", lambda e, Cx=Cx: e.tensor_mul(pr_sb[:], st_sb[:], Cx), vM)
                if dt == NDT - 1:
                    cm_last[(j, h, s)] = cmv
                    BexB.consumed(eb, "vM", cmv)
                    CexB.consumed(eb, "vM", cmv)
                ya = yacc_sb[:, dt * SCH:(dt + 1) * SCH]
                if s == 0:
                    put("vector", lambda e, ya=ya: e.tensor_copy(ya, pr_sb[:]))
                else:
                    put("vector", lambda e, ya=ya: e.tensor_add(ya, pr_sb[:], ya))
            yield
        if h == 0 and j >= 1:
            wait("gpsimd", "cc", ag_mark[("y", j - 1)])
        for dt in range(NDT):
            xc = xcR_sb[:, (gslot * NDT + dt) * SCH:(gslot * NDT + dt + 1) * SCH]
            ya = yacc_sb[:, dt * SCH:(dt + 1) * SCH]
            ysl = dt % 2
            yfinB.acquire("vector", ysl)
            yf = yfin_sb[:, ysl * SCH:(ysl + 1) * SCH]
            v_yf = mark("vector", lambda e, xc=xc, ya=ya, yf=yf, dt=dt:
                        e.scalar_tensor_tensor(yf, xc, D_sb[:, dt:dt + 1], ya,
                                               ALU.mult, ALU.add), vM)
            yfin_last[(j, h)] = v_yf
            wait("gpsimd", "vM", v_yf)
            g_ys = dmaG(bass.AP(ySend, dt * 128 * RT + h * SCH, [[RT, 128], [1, SCH]]), yf)
            yfinB.consumed(ysl, "dGP", g_ys)
        if h == 1:
            prev["bex_round_last"][j] = dSC.v
            wait("gpsimd", "dGP", dGP.v)
            if j >= 2:
                wait("gpsimd", "vM", yg_consumed[j - 2])
            ag_mark[("y", j)] = mark("gpsimd", lambda e, jb=jb: e.collective_compute(
                "AllGather", ALU.bypass, ins=[ySend.ap()], outs=[yAGo[jb].ap()],
                replica_groups=G4), cc)
        yield

    def emit_scan_half(j, h):
        for _ in emit_scan_half_gen(j, h):
            pass

    def emit_gating(j):
        jb = j % 2
        wait("sync", "cc", ag_mark[("y", j)])
        if prev["wout_last"] > 0:
            wait("sync", "tM", prev["wout_last"])
            wait("sync", "vM", prev["v_yg"])
        yv = yAGo[jb].ap().rearrange("(r n p t) -> p r n t", r=GRP, p=128, t=RT)
        y_ld = 0
        for r in range(GRP):
            def fy(e, r=r, yv=yv):
                return e.dma_start(
                    out=yt_sb[:, r * NDT * CH:(r + 1) * NDT * CH].rearrange(
                        "p (r n t) -> p r n t", r=1, n=NDT, t=CH),
                    in_=yv[:, r:r + 1, :, bass.ds(regs["qCH"], CH)])
            y_ld = dma_issue(trS, fy)
        wait("sync", "dGP", zmark[j])
        for kt in range(KI):
            zsl = kt % 2
            sztB.acquire("sync", zsl)
            ldz = dmaS(szt_sb[:, zsl * CH:(zsl + 1) * CH],
                       bass.AP(z_dram, (j * DI + kt * 128) * CH, [[CH, 128], [1, CH]]))
            wait("scalar", "dSY", ldz)
            s_z = mark("scalar", lambda e, zs=zsl: e.activation(
                szt_sb[:, zs * CH:(zs + 1) * CH], szt_sb[:, zs * CH:(zs + 1) * CH],
                AF.Silu), sM)
            wait("vector", "sM", s_z)
            if kt == 0:
                wait("vector", "dSY", y_ld)
            v_g = mark("vector", lambda e, kt=kt, zs=zsl: e.tensor_mul(
                yt_sb[:, kt * CH:(kt + 1) * CH], yt_sb[:, kt * CH:(kt + 1) * CH],
                szt_sb[:, zs * CH:(zs + 1) * CH]), vM)
            sztB.consumed(zsl, "vM", v_g)
        v_yg = vM.v
        yg_consumed[j] = v_yg
        prev["v_yg"] = v_yg
        for m in range(KD):
            sl = m % 2
            wbfB.acquire("sync", sl)
            lc3, lv3 = dma_slot("sync", wbfB, sl, wbf[:, sl * 2048:sl * 2048 + 2048],
                                bass.AP(wbf_out, m * 128,
                                        [[DM, 128], [128 * DM, KI], [1, 128]]))
            bk = 5 if m % 2 == 0 else 6
            te_take(bk)
            wait("tensor", lc3, lv3)
            wait("tensor", "vM", v_yg)
            t_mm = matgroup([
                (lambda e, k=k, b=bk, sl=sl: e.matmul(
                    ps[b][:, 0:CH], wbf[:, sl * 2048 + k * 128:sl * 2048 + (k + 1) * 128],
                    yt_sb[:, k * CH:(k + 1) * CH], start=(k == 0), stop=(k == KI - 1)))
                for k in range(KI)])
            wbfB.consumed(sl, "tM", t_mm)
            wait("scalar", "tM", t_mm)
            ssl = m % 2
            stgB.acquire("scalar", ssl)
            s_ev = mark("scalar", lambda e, b=bk, ss=ssl: e.copy(
                stg_sb[:, ss * CH:(ss + 1) * CH], ps[b][:, 0:CH]), sM)
            bank_ev[bk] = ("sM", s_ev)
            wait("gpsimd", "sM", s_ev)
            g_sp = dmaG(bass.AP(s_dram, (j * DM + m * 128) * CH, [[CH, 128], [1, CH]]),
                        stg_sb[:, ssl * CH:(ssl + 1) * CH])
            stgB.consumed(ssl, "dGP", g_sp)
            prev["wout_last"] = t_mm

    first_wload = True
    for j in range(RNDS):
        jb = j % 2
        co = j * CHH

        if j >= 1:
            emit_recv(j - 1)

        t_stats = 0
        for k in range(KD):
            sl = k % 2
            xf32B.acquire("sync", sl)
            lcx, lvx = dma_slot("sync", xf32B, sl, xf32[:, sl * CHH:(sl + 1) * CHH],
                                x_fm.ap()[k * 128:(k + 1) * 128, co:co + CHH])
            wait("scalar", lcx, lvx)
            if k == 0:
                wait("scalar", "tM", prev["t_stats"])
                wait("scalar", "vM", prev["v_xn"])
            put("scalar", lambda e, s=sl, k=k: e.copy(
                xb_sb[:, k * CHH:(k + 1) * CHH], xf32[:, s * CHH:(s + 1) * CHH]))
            s_k = mark("scalar", lambda e, s=sl: e.activation(
                xsqb[:, s * CHH:(s + 1) * CHH], xf32[:, s * CHH:(s + 1) * CHH],
                AF.Square), sM)
            xf32B.consumed(sl, "sM", s_k)
            wait("tensor", "sM", s_k)
            if k == 0:
                te_take(2), te_take(3), te_take(4), te_take(5)
            put("tensor", lambda e, k=k: e.matmul(
                ps[2][0:1, 0:CH], onesc_b[:, 0:1], xb_sb[:, k * CHH + HALO:(k + 1) * CHH],
                start=(k == 0), stop=(k == KD - 1)))
            put("tensor", lambda e, k=k, s=sl: e.matmul(
                ps[3][0:1, 0:CH], onesc_b[:, 0:1], xsqb[:, s * CHH + HALO:(s + 1) * CHH],
                start=(k == 0), stop=(k == KD - 1)))
            put("tensor", lambda e, k=k: e.matmul(
                ps[4][0:1, 0:HALO], onesc_b[:, 0:1], xb_sb[:, k * CHH:k * CHH + HALO],
                start=(k == 0), stop=(k == KD - 1)))
            t_stats = mark("tensor", lambda e, k=k, s=sl: e.matmul(
                ps[5][0:1, 0:HALO], onesc_b[:, 0:1], xsqb[:, s * CHH:s * CHH + HALO],
                start=(k == 0), stop=(k == KD - 1)), tM)
        prev["t_stats"] = t_stats
        wait("scalar", "tM", t_stats)
        put("scalar", lambda e: e.copy(srow[0:1, HALO:CHH], ps[2][0:1, 0:CH]))
        put("scalar", lambda e: e.copy(srow[0:1, CHH + HALO:2 * CHH], ps[3][0:1, 0:CH]))
        put("scalar", lambda e: e.copy(srow[0:1, 0:HALO], ps[4][0:1, 0:HALO]))
        put("scalar", lambda e: e.copy(srow[0:1, CHH:CHH + HALO], ps[5][0:1, 0:HALO]))
        put("scalar", lambda e: e.mul(srow[0:1, 2 * CHH:3 * CHH], srow[0:1, 0:CHH], 1.0 / DM))
        s_st = mark("scalar", lambda e: e.mul(srow[0:1, CHH:2 * CHH],
                                              srow[0:1, CHH:2 * CHH], 1.0 / DM), sM)
        bank_ev[2] = bank_ev[3] = bank_ev[4] = bank_ev[5] = ("sM", s_st)
        wait("vector", "sM", s_st)
        put("vector", lambda e: e.tensor_mul(srow[0:1, 5 * CHH:6 * CHH],
                                             srow[0:1, 2 * CHH:3 * CHH],
                                             srow[0:1, 2 * CHH:3 * CHH]))
        v_var = mark("vector", lambda e: e.tensor_sub(srow[0:1, CHH:2 * CHH],
                                                      srow[0:1, CHH:2 * CHH],
                                                      srow[0:1, 5 * CHH:6 * CHH]), vM)
        wait("scalar", "vM", v_var)
        put("scalar", lambda e: e.activation(srow[0:1, 3 * CHH:4 * CHH],
                                             srow[0:1, CHH:2 * CHH], AF.Ln,
                                             bias=srow[0:1, 4 * CHH:4 * CHH + 1]))
        s_r = mark("scalar", lambda e: e.activation(srow[0:1, 3 * CHH:4 * CHH],
                                                    srow[0:1, 3 * CHH:4 * CHH], AF.Exp,
                                                    scale=-0.5), sM)
        wait("sync", "sM", s_r)
        wait("sync", "dSY", prev["rows_dma"])
        dmaS(bass.AP(rowD, j * 2 * CHH, [[CHH, 1], [1, CHH]]), srow[0:1, 3 * CHH:4 * CHH])
        rr = dmaS(bass.AP(rowD, j * 2 * CHH + CHH, [[CHH, 1], [1, CHH]]),
                  srow[0:1, 2 * CHH:3 * CHH])
        if prev["v_xn"] > 0:
            wait("gpsimd", "vM", prev["v_xn"])
        wait("gpsimd", "dSY", rr)
        rb = dmaG(rbc_sb[:], bass.AP(rowD, j * 2 * CHH, [[0, 128], [1, CHH]]))
        mb = dmaG(mbc_sb[:], bass.AP(rowD, j * 2 * CHH + CHH, [[0, 128], [1, CHH]]))
        prev["rows_dma"] = rr
        wait("vector", "dGP", mb)
        if prev["win_last"] > 0:
            wait("vector", "tM", prev["win_last"])
        if prev["xn_spill"] > 0:
            wait("vector", "dGP", prev["xn_spill"])
        if j == 0:
            wait("vector", "vM", setup_v)
        for k in range(KD):
            put("vector", lambda e, k=k: e.tensor_sub(
                xn_sb[:, k * CHH:(k + 1) * CHH], xb_sb[:, k * CHH:(k + 1) * CHH], mbc_sb[:]))
            put("vector", lambda e, k=k: e.tensor_mul(
                xn_sb[:, k * CHH:(k + 1) * CHH], xn_sb[:, k * CHH:(k + 1) * CHH], rbc_sb[:]))
        for k in range(KD):
            v_hz = mark("vector", lambda e, k=k: e.tensor_reduce(
                qs_sb[:, 22:23], xn_sb[:, k * CHH + HALO:(k + 1) * CHH],
                axis=AX.X, op=ALU.max), vM)
            wait("vector", "vM", v_hz)
            put("vector", lambda e: e.tensor_max(xmax_sb[:, 0:1], xmax_sb[:, 0:1],
                                                 qs_sb[:, 22:23]))
        if debug and j == 0:
            put("vector", lambda e: e.tensor_copy(dbgx_sb[:, 0:2], xmax_sb[:, 0:2]))
            put("vector", lambda e: e.tensor_copy(dbgx_sb[:, 6:7], qs_sb[:, 22:23]))
        if debug and j == RNDS - 1:
            put("vector", lambda e: e.tensor_copy(dbgx_sb[:, 2:4], xmax_sb[:, 0:2]))
        v_xn = mark("vector", lambda e: e.tensor_copy(qs_sb[:, 22:23], qs_sb[:, 22:23]), vM)
        prev["v_xn"] = v_xn
        prev["v_xm"] = v_xn
        wait("gpsimd", "vM", v_xn)
        prev["xn_spill"] = dmaG(
            bass.AP(xn_dram, j * DM * CHH, [[CHH, 128], [128 * CHH, KD], [1, CHH]]), xn_sb[:])
        quiesce(tier=2)

        if first_wload:
            wait("sync", "vM", setup_v)
            first_wload = False
        win_last = 0
        for m in range(MI):
            wsl, wready, slot = w_stream(j, WinT.ap().tensor, wbf_in, 2 * DI, m, 128, KD)
            bk = m % 2
            te_take(bk)
            wait("tensor", wready[0], wready[1])
            wait("tensor", "vM", v_xn)
            t_mm = matgroup([
                (lambda e, k=k, w=wsl, b=bk: e.matmul(
                    ps[b][:, 0:CH], w[:, k * 128:(k + 1) * 128],
                    xn_sb[:, k * CHH + HALO:(k + 1) * CHH],
                    start=(k == 0), stop=(k == KD - 1)))
                for k in range(KD)])
            t_last = t_mm
            if m < KI:
                te_take(6)
                t_last = matgroup([
                    (lambda e, k=k, w=wsl, m=m: e.matmul(
                        ps[6][:, 3 * m:3 * m + HALO], w[:, k * 128:(k + 1) * 128],
                        xn_sb[:, k * CHH:k * CHH + HALO],
                        start=(k == 0), stop=(k == KD - 1)))
                    for k in range(KD)])
            wbfB.consumed(slot, "tM", t_last)
            win_last = t_last
            wait("scalar", "tM", t_last)
            if m < KI:
                xsl = m % 3
                xinB.acquire("scalar", xsl)
                put("scalar", lambda e, xs=xsl, b=bk: e.copy(
                    xin_sb[:, xs * CHH + HALO:(xs + 1) * CHH], ps[b][:, 0:CH]))
                s_ev = mark("scalar", lambda e, m=m, xs=xsl: e.copy(
                    xin_sb[:, xs * CHH:xs * CHH + HALO], ps[6][:, 3 * m:3 * m + HALO]), sM)
                bank_ev[6] = ("sM", s_ev)
                wait("vector", "sM", s_ev)
                if m == 0:
                    if prev["wx_last"] > 0:
                        wait("vector", "tM", prev["wx_last"])
                    if xcsend_mark.get(j - 1, 0) > 0:
                        wait("vector", "dGP", xcsend_mark[j - 1])
                xi = xin_sb[:, xsl * CHH:(xsl + 1) * CHH]
                xc = xcv_sb[:, m * CH:(m + 1) * CH]
                put("vector", lambda e, xi=xi, xc=xc, d=m: e.tensor_scalar(
                    xc, xi[:, 0:CH], wcv_sb[:, d * 4:d * 4 + 1], None, ALU.mult))
                for kk in range(1, 4):
                    put("vector", lambda e, xi=xi, xc=xc, d=m, kk=kk: e.scalar_tensor_tensor(
                        xc, xi[:, kk:kk + CH], wcv_sb[:, d * 4 + kk:d * 4 + kk + 1], xc,
                        ALU.mult, ALU.add))
                v_cv = mark("vector", lambda e, xc=xc: e.tensor_copy(xc[:, 0:1], xc[:, 0:1]),
                            vM)
                conv_mark[(j, m)] = v_cv
                xinB.consumed(xsl, "vM", v_cv)
                wait("scalar", "vM", v_cv)
                s_ev = mark("scalar", lambda e, xc=xc, d=m: e.activation(
                    xc, xc, AF.Silu, bias=bconv_sb[:, d:d + 1]), sM)
            else:
                zsl = m % 2
                stgB.acquire("scalar", zsl)
                s_ev = mark("scalar", lambda e, b=bk, zs=zsl: e.copy(
                    stg_sb[:, zs * CH:(zs + 1) * CH], ps[b][:, 0:CH]), sM)
                wait("gpsimd", "sM", s_ev)
                g_z = dmaG(bass.AP(z_dram, (j * DI + (m - KI) * 128) * CH,
                                   [[CH, 128], [1, CH]]),
                           stg_sb[:, zsl * CH:(zsl + 1) * CH])
                zmark[j] = g_z
                stgB.consumed(zsl, "dGP", g_z)
            bank_ev[bk] = ("sM", s_ev)
        prev["win_last"] = win_last
        xc_ready = sM.v

        wait("gpsimd", "sM", xc_ready)
        if j >= 1:
            wait("gpsimd", "cc", ag_mark[("x", j - 1)])
        for kk in range(KI):
            xcsend_mark[j] = dmaG(bass.AP(xbcSend, kk * 128 * CH, [[CH, 128], [1, CH]]),
                                  xcv_sb[:, kk * CH:(kk + 1) * CH])

        gen0 = emit_scan_half_gen(j - 1, 0) if j >= 1 else iter(())

        wx_last = 0
        for m in range(KI + 1):
            next(gen0, None)
            mw = 128 if m < KI else 2 * DST
            wsl, wready, slot = w_stream(j, WxT.ap().tensor, wbf_x, DXW, m, mw, KI,
                                         off=m * 128)
            bk = m % 2
            te_take(bk)
            wait("tensor", wready[0], wready[1])
            wait("tensor", "sM", xc_ready)
            t_mm = matgroup([
                (lambda e, k=k, w=wsl, b=bk, mw=mw: e.matmul(
                    ps[b][0:mw, 0:CH], w[:, k * mw:(k + 1) * mw],
                    xcv_sb[:, k * CH:(k + 1) * CH], start=(k == 0), stop=(k == KI - 1)))
                for k in range(KI)])
            wbfB.consumed(slot, "tM", t_mm)
            wx_last = t_mm
            wait("scalar", "tM", t_mm)
            if m < KI:
                if m == 0 and prev["wdt_last"] > 0:
                    wait("scalar", "tM", prev["wdt_last"])
                s_ev = mark("scalar", lambda e, m=m, b=bk: e.copy(
                    drw_sb[:, m * CH:(m + 1) * CH], ps[b][:, 0:CH]), sM)
            else:
                s_ev = mark("scalar", lambda e, b=bk: e.copy(
                    bcstg[:, 0:CH], ps[b][0:32, 0:CH]), sM)
                wait("gpsimd", "sM", s_ev)
                dmaG(bass.AP(xbcSend, DI * CH, [[CH, 32], [1, CH]]), bcstg[:, 0:CH])
            bank_ev[bk] = ("sM", s_ev)
        for _ in gen0:
            pass
        prev["wx_last"] = wx_last
        drw_ready = sM.v
        wait("gpsimd", "dGP", dGP.v)
        if j >= 2:
            wait("gpsimd", "dSY", prev["recv_last"].get(j - 2, 0))
            wait("gpsimd", "dSC", prev["bex_round_last"].get(j - 2, 0))
        ag_mark[("x", j)] = mark("gpsimd", lambda e, jb=jb: e.collective_compute(
            "AllGather", ALU.bypass, ins=[xbcSend.ap()], outs=[xbcAGo[jb].ap()],
            replica_groups=G4), cc)
        quiesce(tier=2)

        gen1 = emit_scan_half_gen(j - 1, 1) if j >= 1 else iter(())
        wdt_last = 0
        for m in range(KI):
            next(gen1, None)
            next(gen1, None)
            wsl, wready, slot = w_stream(j, WdtT.ap().tensor, wbf_dt, DI, m, 128, KI)
            bk = m % 2
            te_take(bk)
            wait("tensor", wready[0], wready[1])
            wait("tensor", "sM", drw_ready)
            t_mm = matgroup([
                (lambda e, k=k, w=wsl, b=bk: e.matmul(
                    ps[b][:, 0:CH], w[:, k * 128:(k + 1) * 128],
                    drw_sb[:, k * CH:(k + 1) * CH], start=(k == 0), stop=(k == KI - 1)))
                for k in range(KI)])
            wbfB.consumed(slot, "tM", t_mm)
            wdt_last = t_mm
            wait("scalar", "tM", t_mm)
            esl = m % 2
            stg2B.acquire("scalar", esl)
            put("scalar", lambda e, b=bk, es=esl, m=m: e.activation(
                stg2_sb[:, es * CH:(es + 1) * CH], ps[b][:, 0:CH], AF.Exp,
                bias=bdt_sb[:, m:m + 1]))
            s_ev = mark("scalar", lambda e, es=esl: e.activation(
                stg2_sb[:, es * CH:(es + 1) * CH], stg2_sb[:, es * CH:(es + 1) * CH],
                AF.Ln, bias=1.0), sM)
            bank_ev[bk] = ("sM", s_ev)
            wait("gpsimd", "sM", s_ev)
            if m == 0 and j >= 1:
                wait("gpsimd", "cc", ag_mark[("d", j - 1)])
            g_d = dmaG(bass.AP(dSend, m * 128 * CH, [[CH, 128], [1, CH]]),
                       stg2_sb[:, esl * CH:(esl + 1) * CH])
            stg2B.consumed(esl, "dGP", g_d)
        prev["wdt_last"] = wdt_last
        wait("gpsimd", "dGP", dGP.v)
        if j >= 2:
            wait("gpsimd", "dSY", prev["recv_last"].get(j - 2, 0))
        ag_mark[("d", j)] = mark("gpsimd", lambda e, jb=jb: e.collective_compute(
            "AllGather", ALU.bypass, ins=[dSend.ap()], outs=[dAGo[jb].ap()],
            replica_groups=G4), cc)

        if debug and j == 0:
            wait("gpsimd", "cc", ag_mark[("d", 0)])
            dmaG(dbg["dsend0"].ap(), dSend.ap())
            dmaG(dbg["xbc0"].ap(), xbcSend.ap())

        quiesce(tier=2)

        for _ in gen1:
            pass
        if j >= 2:
            emit_gating(j - 2)
        quiesce(tier=2)

    emit_recv(RNDS - 1)
    genp0 = emit_scan_half_gen(RNDS - 1, 0)
    genp1 = emit_scan_half_gen(RNDS - 1, 1)
    quiesce()

    wait("vector", "vM", prev["v_xm"])
    if debug:
        put("vector", lambda e: e.tensor_copy(dbgx_sb[:, 4:6], xmax_sb[:, 0:2]))
        put("vector", lambda e: e.tensor_copy(dbgx_sb[:, 8:16], qs_sb[:, 16:24]))
    v_fl = mark("vector", lambda e: e.tensor_copy(qs_sb[:, 23:24], xmax_sb[:, 0:1]), vM)
    wait("sync", "vM", v_fl)
    w_flip = dmaS(flipD.ap(), qs_sb[:, 23:24])
    wait("sync", "dSY", w_flip)
    fl2 = dmaS(srow[0:1, 0:128], flipD.ap()[None, :])
    wait("vector", "dSY", fl2)
    v_m1 = mark("vector", lambda e: e.tensor_reduce(
        r16[0:1, 0:1], srow[0:1, 0:128], axis=AX.X, op=ALU.max), vM)
    wait("sync", "vM", v_m1)
    ar1_in = dmaS(mx_in.ap()[None, :], r16[0:1, 0:16])
    wait("gpsimd", "dSY", ar1_in)
    cc_ar1 = mark("gpsimd", lambda e: e.collective_compute(
        "AllReduce", ALU.max, ins=[mx_in.ap()], outs=[mx1_out.ap()],
        replica_groups=G8), cc)
    wait("sync", "cc", cc_ar1)
    if debug:
        wait("gpsimd", "cc", cc_ar1)
        dmaG(bass.AP(dbg["ar1"].ap().tensor, 0, [[1, 128]]), flipD.ap())
        dmaG(bass.AP(dbg["ar1"].ap().tensor, 128, [[1, 16]]), mx_in.ap())
        dmaG(bass.AP(dbg["ar1"].ap().tensor, 144, [[1, 16]]), mx1_out.ap())
    m1_ld = dmaS(r16[0:1, 0:16], mx1_out.ap()[None, :])
    wait("vector", "dSY", m1_ld)
    put("vector", lambda e: e.tensor_scalar_max(r16[0:1, 0:1], r16[0:1, 0:1], 1e-8))
    put("vector", lambda e: e.tensor_scalar_mul(r16[0:1, 1:2], r16[0:1, 0:1], 1.0 / 255.0))
    v_hz = mark("vector", lambda e: e.reciprocal(rcp1_sb[:], r16[0:1, 0:1]), vM)
    wait("vector", "vM", v_hz)
    v_s1 = mark("vector", lambda e: e.tensor_scalar_mul(r16[0:1, 2:3], rcp1_sb[:],
                                                        255.0), vM)
    wait("sync", "vM", v_s1)
    w_scl = dmaS(bass.AP(sclD, 0, [[4, 1], [1, 4]]), r16[0:1, 0:4])
    wait("sync", "dSY", w_scl)
    s1b = dmaS(scl_sb[:, 0:1], bass.AP(sclD, 2, [[0, 128], [1, 1]]))
    s1c = dmaS(scl_sb[:, 2:3], bass.AP(sclD, 1, [[0, 128], [1, 1]]))
    wait("vector", "dSY", s1c)
    v_hz = mark("vector", lambda e: e.tensor_scalar(scale1_sb[:], sw1_sb[:],
                                                    scl_sb[:, 2:3], None, ALU.mult), vM)
    wait("vector", "vM", v_hz)
    v_hz = mark("vector", lambda e: e.reciprocal(bq1_sb[:], scale1_sb[:]), vM)
    wait("vector", "vM", v_hz)
    v_bq = mark("vector", lambda e: e.tensor_mul(bq1_sb[:], b1_sb[:], bq1_sb[:]), vM)
    wait("vector", "vM", v_bq)
    wait("scalar", "vM", v_bq)
    s_hz2 = mark("scalar", lambda e: e.copy(wqi_sb[:, 0:MF], bq1_sb[:]), sM)
    wait("scalar", "sM", s_hz2)
    s_bq = mark("scalar", lambda e: e.copy(bq1_sb[:], wqi_sb[:, 0:MF]), sM)
    wait("vector", "sM", s_bq)
    v_bq1 = mark("vector", lambda e: e.tensor_mul(bq1_sb[:], bq1_sb[:], scale1_sb[:]), vM)

    quiesce()

    def xq_tile(idx):
        base = xcv_sb if idx < KI else drw_sb
        i = idx % KI
        return base[:, i * CH:(i + 1) * CH]

    wait("scalar", "tM", prev["wx_last"])
    wait("scalar", "tM", prev["wdt_last"])
    wait("scalar", "dGP", xcsend_mark[RNDS - 1])
    wait("scalar", "dSY", s1b)
    wait("sync", "dGP", prev["xn_spill"])
    wait("sync", "cc", cc_wq)
    xq_last = 0
    for j in range(RNDS):
        for k in range(KD):
            xsl = k % 2
            stg2B.acquire("sync", xsl)
            ldxn = dmaS(stg2_sb[:, xsl * CH:(xsl + 1) * CH],
                        bass.AP(xn_dram, j * DM * CHH + k * 128 * CHH + HALO,
                                [[CHH, 128], [1, CH]]))
            wait("scalar", "dSY", ldxn)
            iqB.acquire("scalar", xsl)
            s_hz = mark("scalar", lambda e, xs=xsl: e.activation(
                iq_sb[:, xs * CH:(xs + 1) * CH], stg2_sb[:, xs * CH:(xs + 1) * CH],
                AF.Copy, scale=scl_sb[:, 0:1]), sM)
            wait("scalar", "sM", s_hz)
            s_i = mark("scalar", lambda e, j=j, k=k, xs=xsl: e.copy(
                xq_tile(j * KD + k), iq_sb[:, xs * CH:(xs + 1) * CH]), sM)
            stg2B.consumed(xsl, "sM", s_i)
            wait("vector", "sM", s_i)
            xq_last = mark("vector", lambda e, j=j, k=k: e.tensor_scalar_max(
                xq_tile(j * KD + k), xq_tile(j * KD + k), 0.0), vM)

    quiesce()

    wait("tensor", "cc", cc_wq)
    wait("tensor", "vM", xq_last)
    wait("scalar", "vM", v_bq1)
    hmax_v = 0
    for m in range(MF):
        if m < 17:
            next(genp0, None)
        else:
            next(genp1, None)
            next(genp1, None)
        sl = m % 2
        wbfB.acquire("sync", sl)
        lc1, lv1 = dma_slot("sync", wbfB, sl, wbf[:, sl * 2048:sl * 2048 + KD * 128],
                            bass.AP(wq_out, m * 128, [[4096, 128], [WQBLK, KD], [1, 128]]))
        wait("tensor", lc1, lv1)
        t_grp = 0
        for j in range(RNDS):
            bk = (m * RNDS + j) % 2
            te_take(bk)
            t_grp = matgroup([
                (lambda e, k=k, j=j, b=bk, s=sl: e.matmul(
                    ps[b][:, 0:CH], wbf[:, s * 2048 + k * 128:s * 2048 + (k + 1) * 128],
                    xq_tile(j * KD + k), start=(k == 0), stop=(k == KD - 1)))
                for k in range(KD)])
            wait("scalar", "tM", t_grp)
            hsl = (m * RNDS + j) % 2
            stgB.acquire("scalar", hsl)
            s_h = mark("scalar", lambda e, b=bk, hs=hsl, m=m: e.activation(
                stg_sb[:, hs * CH:(hs + 1) * CH], ps[b][:, 0:CH], AF.Relu,
                bias=bq1_sb[:, m:m + 1], scale=scale1_sb[:, m:m + 1]), sM)
            bank_ev[bk] = ("sM", s_h)
            wait("gpsimd", "sM", s_h)
            g_h = dmaG(bass.AP(h_dram, (m * 128) * TLOC + j * CH, [[TLOC, 128], [1, CH]]),
                       stg_sb[:, hsl * CH:(hsl + 1) * CH])
            stgB.consumed(hsl, "dGP", g_h)
            wait("vector", "sM", s_h)
            v_hz = mark("vector", lambda e, hs=hsl: e.tensor_reduce(
                qs_sb[:, 22:23], stg_sb[:, hs * CH:(hs + 1) * CH], axis=AX.X,
                op=ALU.max), vM)
            wait("vector", "vM", v_hz)
            hmax_v = mark("vector", lambda e: e.tensor_max(
                xmax_sb[:, 1:2], xmax_sb[:, 1:2], qs_sb[:, 22:23]), vM)
            stgB.consumed(hsl, "vM", hmax_v)
        wbfB.consumed(sl, "tM", t_grp)

    for _ in genp0:
        pass
    for _ in genp1:
        pass
    quiesce()
    h_done = dGP.v

    if RNDS >= 2:
        emit_gating(RNDS - 2)

    quiesce()

    wait("vector", "vM", hmax_v)
    put("vector", lambda e: e.tensor_scalar_max(xmax_sb[:, 1:2], xmax_sb[:, 1:2], 0.0))
    v_fl2 = mark("vector", lambda e: e.tensor_copy(qs_sb[:, 23:24], xmax_sb[:, 1:2]), vM)
    wait("sync", "vM", v_fl2)
    w_flip2 = dmaS(flipD.ap(), qs_sb[:, 23:24])
    wait("sync", "dSY", w_flip2)
    fl3 = dmaS(srow[0:1, 0:128], flipD.ap()[None, :])
    wait("vector", "dSY", fl3)
    v_m2 = mark("vector", lambda e: e.tensor_reduce(
        r16[0:1, 0:1], srow[0:1, 0:128], axis=AX.X, op=ALU.max), vM)
    wait("sync", "vM", v_m2)
    ar2_in = dmaS(mx_in.ap()[None, :], r16[0:1, 0:16])
    wait("gpsimd", "dSY", ar2_in)
    cc_ar2 = mark("gpsimd", lambda e: e.collective_compute(
        "AllReduce", ALU.max, ins=[mx_in.ap()], outs=[mx2_out.ap()],
        replica_groups=G8), cc)
    wait("sync", "cc", cc_ar2)
    m2_ld = dmaS(r16[0:1, 0:16], mx2_out.ap()[None, :])
    wait("vector", "dSY", m2_ld)
    put("vector", lambda e: e.tensor_scalar_max(r16[0:1, 0:1], r16[0:1, 0:1], 1e-8))
    put("vector", lambda e: e.tensor_scalar_mul(r16[0:1, 1:2], r16[0:1, 0:1], 1.0 / 255.0))
    v_hz = mark("vector", lambda e: e.reciprocal(rcp1_sb[:], r16[0:1, 0:1]), vM)
    wait("vector", "vM", v_hz)
    v_s2 = mark("vector", lambda e: e.tensor_scalar_mul(r16[0:1, 2:3], rcp1_sb[:],
                                                        255.0), vM)
    wait("sync", "vM", v_s2)
    w_scl2 = dmaS(bass.AP(sclD, 8, [[4, 1], [1, 4]]), r16[0:1, 0:4])
    wait("sync", "dSY", w_scl2)
    s2b = dmaS(scl_sb[:, 1:2], bass.AP(sclD, 10, [[0, 128], [1, 1]]))
    s2c = dmaS(scl_sb[:, 3:4], bass.AP(sclD, 9, [[0, 128], [1, 1]]))
    wait("vector", "dSY", s2c)
    v_hz = mark("vector", lambda e: e.tensor_scalar(scale2_sb[:], sw2_sb[:],
                                                    scl_sb[:, 3:4], None, ALU.mult), vM)
    wait("vector", "vM", v_hz)
    v_hz = mark("vector", lambda e: e.reciprocal(bq2_sb[:], scale2_sb[:]), vM)
    wait("vector", "vM", v_hz)
    v_bq2a = mark("vector", lambda e: e.tensor_mul(bq2_sb[:], b2_sb[:], bq2_sb[:]), vM)
    wait("vector", "vM", v_bq2a)
    wait("scalar", "vM", v_bq2a)
    s_hz2 = mark("scalar", lambda e: e.copy(wqi_sb[:, 0:KD], bq2_sb[:]), sM)
    wait("scalar", "sM", s_hz2)
    s_bq2 = mark("scalar", lambda e: e.copy(bq2_sb[:], wqi_sb[:, 0:KD]), sM)
    wait("vector", "sM", s_bq2)
    v_bq2 = mark("vector", lambda e: e.tensor_mul(bq2_sb[:], bq2_sb[:], scale2_sb[:]), vM)

    def hq_tile(idx):
        base = dR_sb if idx < KI else xcR_sb
        i = idx % KI
        return base[:, i * CH:(i + 1) * CH]

    quiesce()
    last_cm = cm_last[(RNDS - 1, 1, DST - 1)]
    last_yf = yfin_last[(RNDS - 1, 1)]
    wait("scalar", "vM", max(last_cm, last_yf))
    wait("scalar", "vM", v_bq2)
    wait("scalar", "dSY", s2b)
    wait("sync", "dGP", h_done)
    for j in range(RNDS):
        hq_last = 0
        for k in range(KH):
            hsl = k % 2
            stg2B.acquire("sync", hsl)
            ldh = dmaS(stg2_sb[:, hsl * CH:(hsl + 1) * CH],
                       bass.AP(h_dram, (k * 128) * TLOC + j * CH, [[TLOC, 128], [1, CH]]))
            wait("scalar", "dSY", ldh)
            iqB.acquire("scalar", hsl)
            s_hz = mark("scalar", lambda e, hs=hsl: e.activation(
                iq_sb[:, hs * CH:(hs + 1) * CH], stg2_sb[:, hs * CH:(hs + 1) * CH],
                AF.Copy, scale=scl_sb[:, 1:2]), sM)
            wait("scalar", "sM", s_hz)
            s_hq = mark("scalar", lambda e, k=k, hs=hsl: e.copy(
                hq_tile(k), iq_sb[:, hs * CH:(hs + 1) * CH]), sM)
            stg2B.consumed(hsl, "sM", s_hq)
            wait("vector", "sM", s_hq)
            hq_last = mark("vector", lambda e, k=k: e.tensor_scalar_max(
                hq_tile(k), hq_tile(k), 0.0), vM)
        if debug and j == 0:
            wait("gpsimd", "vM", hq_last)
            dmaG(bass.AP(dbg["hq0"].ap().tensor, 0, [[4 * CH, 128], [1, 4 * CH]]),
                 dR_sb[:, 0:4 * CH])
        wait("tensor", "vM", hq_last)
        for m in range(KD):
            sl = m % 2
            wbfB.acquire("sync", 0)
            wbfB.acquire("sync", 1)
            for r in range(8):
                lc2, lv2 = dma_slot("sync", wbfB, 0, wbf[:, r * 512:(r + 1) * 512],
                                    bass.AP(wq_out, r * WQBLK + 128 * 4096 + m * 128,
                                            [[DM, 128], [128 * DM, 4], [1, 128]]))
            te_take(m)
            wait("tensor", lc2, lv2)
            t_mm = matgroup([
                (lambda e, k=k, m=m: e.matmul(
                    ps[m][:, 0:CH], wbf[:, k * 128:(k + 1) * 128],
                    hq_tile(k), start=(k == 0), stop=(k == KH - 1)))
                for k in range(KH)])
            wbfB.consumed(0, "tM", t_mm)
            wbfB.consumed(1, "tM", t_mm)
            wait("scalar", "tM", t_mm)
            gsl = m % 2
            stgB.acquire("scalar", gsl)
            s_g = mark("scalar", lambda e, m=m, gs=gsl: e.activation(
                stg_sb[:, gs * CH:(gs + 1) * CH], ps[m][:, 0:CH], AF.Silu,
                bias=bq2_sb[:, m:m + 1], scale=scale2_sb[:, m:m + 1]), sM)
            bank_ev[m] = ("sM", s_g)
            wait("gpsimd", "sM", s_g)
            g_g = dmaG(bass.AP(g_dram, (m * 128) * TLOC + j * CH, [[TLOC, 128], [1, CH]]),
                       stg_sb[:, gsl * CH:(gsl + 1) * CH])
            stgB.consumed(gsl, "dGP", g_g)
    g_done = dGP.v

    quiesce()

    emit_gating(RNDS - 1)
    quiesce()
    s_done = dGP.v

    wait("sync", "dGP", max(g_done, s_done))
    out_last = 0
    for j in range(RNDS):
        for kt in range(KD):
            osl = kt % 2
            xroB.acquire("sync", 0)
            ldx = dmaS(xro_sb[:],
                       x_fm.ap()[kt * 128:(kt + 1) * 128, j * CHH + HALO:(j + 1) * CHH])
            stgB.acquire("sync", osl)
            lds = dmaS(stg_sb[:, osl * CH:(osl + 1) * CH],
                       bass.AP(s_dram, (j * DM + kt * 128) * CH, [[CH, 128], [1, CH]]))
            stg2B.acquire("sync", osl)
            ldg = dmaS(stg2_sb[:, osl * CH:(osl + 1) * CH],
                       bass.AP(g_dram, (kt * 128) * TLOC + j * CH, [[TLOC, 128], [1, CH]]))
            wait("vector", "dSY", ldg)
            v_p = mark("vector", lambda e, os=osl: e.tensor_mul(
                stg_sb[:, os * CH:(os + 1) * CH], stg_sb[:, os * CH:(os + 1) * CH],
                stg2_sb[:, os * CH:(os + 1) * CH]), vM)
            stg2B.consumed(osl, "vM", v_p)
            outB.acquire("vector", 0)
            v_o = mark("vector", lambda e, os=osl: e.tensor_add(
                out_sb[:], xro_sb[:],
                stg_sb[:, os * CH:(os + 1) * CH]), vM)
            stgB.consumed(osl, "vM", v_o)
            xroB.consumed(0, "vM", v_o)
            wait("sync", "vM", v_o)
            out_last = dmaS(out_fm.ap()[kt * 128:(kt + 1) * 128, j * CH:(j + 1) * CH],
                            out_sb[:])
            outB.consumed(0, "dSY", out_last)

    if debug:
        wait("gpsimd", "dGP", dGP.v)
        wait("gpsimd", "dSY", out_last)
        dmaG(dbg["y0"].ap(), ySend.ap())
        dmaG(dbg["h0"].ap(), h_dram.ap())
        dmaG(dbg["g0"].ap(), g_dram.ap())
        dmaG(dbg["s0"].ap(), s_dram.ap())
        dmaG(dbg["xn0"].ap(), xn_dram.ap())
        dmaG(bass.AP(dbg["sc"].ap().tensor, 0, [[1, 4]]), bass.AP(sclD, 0, [[1, 4]]))
        dmaG(bass.AP(dbg["sc"].ap().tensor, 8, [[1, 4]]), bass.AP(sclD, 8, [[1, 4]]))
        dmaG(dbg["wq"].ap(), wq_out.ap())
        dmaG(bass.AP(dbg["carry"].ap().tensor, 0, [[NDT * DST, 128], [1, NDT * DST]]),
             carry[:])
        dmaG(bass.AP(dbg["xm"].ap().tensor, 0, [[16, 128], [1, 16]]), dbgx_sb[:])
        dmaG(dbg["wqin"].ap(), wq_in.ap())
        dmaG(dbg["swq"].ap(), swq_out.ap())
        dmaG(bass.AP(dbg["xq"].ap().tensor, 0, [[KI * CH, 128], [1, KI * CH]]), xcv_sb[:])
        dmaG(bass.AP(dbg["xq"].ap().tensor, DI * CH, [[KI * CH, 128], [1, KI * CH]]), drw_sb[:])
        dmaG(bass.AP(dbg["q1"].ap().tensor, 0, [[2 * MF, 128], [1, MF]]), scale1_sb[:])
        dmaG(bass.AP(dbg["q1"].ap().tensor, MF, [[2 * MF, 128], [1, MF]]), bq1_sb[:])
        dmaG(bass.AP(dbg["q2"].ap().tensor, 0, [[2 * KD, 128], [1, KD]]), scale2_sb[:])
        dmaG(bass.AP(dbg["q2"].ap().tensor, KD, [[2 * KD, 128], [1, KD]]), bq2_sb[:])


    wait("sync", "dSY", out_last)
    wait("gpsimd", "dSY", out_last)
    wait("gpsimd", "dGP", dGP.v)

    if vec_limit is not None:
        if isinstance(vec_limit, dict):
            for k, n in vec_limit.items():
                plan[k] = plan[k][:n]
        else:
            plan["vector"] = plan["vector"][:vec_limit]

    with nc.Block() as block:
        @block.sync
        def _(e):
            for f in plan["sync"]:
                f(e)

        @block.tensor
        def _(e):
            for f in plan["tensor"]:
                f(e)

        @block.vector
        def _(e):
            for f in plan["vector"]:
                f(e)

        @block.scalar
        def _(e):
            for f in plan["scalar"]:
                f(e)

        @block.gpsimd
        def _(e):
            for f in plan["gpsimd"]:
                f(e)

    stack.close()
    return nc


def _layout_inputs(inputs, T_total, CH):
    NCHUNK = T_total // CH
    RNDS = NCHUNK // GRP
    CHH = CH + HALO
    x = np.asarray(inputs["L0_attn_x"], np.float32)
    WinT = np.ascontiguousarray(np.asarray(inputs["L0_attn_W_in"], np.float32).T)
    WxT = np.ascontiguousarray(np.asarray(inputs["L0_attn_W_x"], np.float32).T)
    WdtT = np.ascontiguousarray(np.asarray(inputs["L0_attn_W_dt"], np.float32).T)
    WoutT = np.ascontiguousarray(np.asarray(inputs["L0_attn_W_out"], np.float32).T)
    w1 = np.asarray(inputs["L0_ffn_w1"], np.float32)
    w2 = np.asarray(inputs["L0_ffn_w2"], np.float32)
    w1T = np.ascontiguousarray(w1.T)
    w2T = np.ascontiguousarray(w2.T)
    bconv = np.ascontiguousarray(
        np.asarray(inputs["L0_attn_b_conv"], np.float32).reshape(KI, 128).T)
    bdt = np.ascontiguousarray(
        np.asarray(inputs["L0_attn_b_dt"], np.float32).reshape(KI, 128).T)
    wcv = np.ascontiguousarray(
        np.asarray(inputs["L0_attn_w_conv"], np.float32).reshape(KI, 128, 4)
        .transpose(1, 0, 2).reshape(128, KI * 4))
    Alog = np.asarray(inputs["L0_attn_A_log"], np.float32)
    Dv = np.asarray(inputs["L0_attn_D"], np.float32)
    b1 = np.ascontiguousarray(np.asarray(inputs["L0_ffn_b1"], np.float32).reshape(MF, 128).T)
    b2 = np.ascontiguousarray(np.asarray(inputs["L0_ffn_b2"], np.float32).reshape(KD, 128).T)

    xp = np.pad(x, ((0, 0), (HALO, 0), (0, 0)))
    DQ = DI // GRP
    NDT = DQ // 128
    in_maps = []
    for c in range(8):
        b, q = divmod(c, GRP)
        cols = []
        for j in range(RNDS):
            g = q + GRP * j
            cols.append(xp[b, g * CH:g * CH + CHH, :].T)
        x_c = np.ascontiguousarray(np.concatenate(cols, axis=1), np.float32)
        dq0 = q * DQ
        Alog_my = np.ascontiguousarray(
            Alog[dq0:dq0 + DQ].reshape(NDT, 128, DST).transpose(1, 0, 2)
            .reshape(128, NDT * DST))
        D_my = np.ascontiguousarray(Dv[dq0:dq0 + DQ].reshape(NDT, 128).T)
        in_maps.append({
            "x_fm": x_c,
            "WinT": WinT, "WxT": WxT, "WdtT": WdtT, "WoutT": WoutT,
            "w1T_myk": np.ascontiguousarray(w1T[c * 128:(c + 1) * 128]),
            "w2T_myk": np.ascontiguousarray(w2T[c * 512:(c + 1) * 512]),
            "w1_rows": np.ascontiguousarray(w1[c * 512:(c + 1) * 512]),
            "w2_rows": np.ascontiguousarray(w2[c * 128:(c + 1) * 128]),
            "bconv_t": bconv, "bdt_t": bdt, "wcv_t": wcv,
            "Alog_my": Alog_my, "D_my": D_my,
            "b1_t": b1, "b2_t": b2,
            "rank_in": np.array([[q, q * CH, q * (DI // GRP // 128), 0]], np.int32),
        })
    return in_maps


def kernel(**inputs):
    T_total = int(np.asarray(inputs["L0_attn_x"]).shape[1])
    CH = 512 if T_total >= GRP * 512 else T_total // (GRP * 4)
    nc = build_program(T_total=T_total, CH=CH)
    in_maps = _layout_inputs(inputs, T_total, CH)
    res = run_bass_kernel_spmd(nc, in_maps, core_ids=list(range(8)))
    RNDS = T_total // CH // GRP
    out = np.empty((BATCH, T_total, DM), np.float32)
    for c in range(8):
        b, q = divmod(c, GRP)
        o = np.asarray(res.results[c]["out_fm"])
        for j in range(RNDS):
            g = q + GRP * j
            out[b, g * CH:(g + 1) * CH, :] = o[:, j * CH:(j + 1) * CH].T
    return out


# version 11
# speedup vs baseline: 1.0120x; 1.0120x over previous
import contextlib
import numpy as np
import concourse.bass as bass
import concourse.mybir as mybir
from concourse.bass_utils import run_bass_kernel_spmd

F32 = mybir.dt.float32
BF16 = mybir.dt.bfloat16
I32 = mybir.dt.int32
AF = mybir.ActivationFunctionType
ALU = mybir.AluOpType
AX = mybir.AxisListType

BATCH = 2
DM = 1024
DI = 2048
DST = 16
GRP = 4
KD = DM // 128
KI = DI // 128
MI = 2 * DI // 128
MF = 4096 // 128
KH = 4096 // 128
HALO = 3
DXW = DI + 2 * DST
DXP = DI + 128


class Ctr:
    def __init__(self, h):
        self.h = h
        self.v = 0


def build_program(T_total=8192, CH=512, debug=False, vec_limit=None, bar_tier=0):
    NCHUNK = T_total // CH
    RNDS = NCHUNK // GRP
    RT = GRP * CH
    SCH = RT // 2
    TLOC = RNDS * CH
    CHH = CH + HALO
    DQ = DI // GRP
    NDT = DQ // 128
    NSLOT = 3
    assert CH <= 512

    nc = bass.Bass("TRN2", target_bir_lowering=False,
                   detect_race_conditions=False, dynamic_dma_scratch_size=16384)

    def EIN(name, shape, dt=F32):
        return nc.dram_tensor(name, shape, dt, kind="ExternalInput")

    x_fm = EIN("x_fm", [DM, RNDS * CHH])
    WinT = EIN("WinT", [DM, 2 * DI])
    WxT = EIN("WxT", [DI, DXW])
    WdtT = EIN("WdtT", [DI, DI])
    WoutT = EIN("WoutT", [DI, DM])
    w1T_myk = EIN("w1T_myk", [128, 4096])
    w2T_myk = EIN("w2T_myk", [512, DM])
    w1_rows = EIN("w1_rows", [512, DM])
    w2_rows = EIN("w2_rows", [128, 4096])
    bconv_t = EIN("bconv_t", [128, KI])
    bdt_t = EIN("bdt_t", [128, KI])
    wcv_t = EIN("wcv_t", [128, KI * 4])
    Alog_my = EIN("Alog_my", [128, NDT * DST])
    D_my = EIN("D_my", [128, NDT])
    b1_t = EIN("b1_t", [128, MF])
    b2_t = EIN("b2_t", [128, KD])
    rank_in = EIN("rank_in", [1, 4], I32)

    out_fm = nc.dram_tensor("out_fm", [DM, TLOC], F32, kind="ExternalOutput")

    dbg = {}
    if debug:
        for nm, n, dt in [("dsend0", DI * CH, BF16), ("xbc0", DXP * CH, BF16),
                          ("y0", DQ * RT, BF16), ("sc", 64, F32),
                          ("h0", 4096 * TLOC, BF16), ("g0", DM * TLOC, BF16),
                          ("s0", RNDS * DM * CH, BF16), ("xn0", RNDS * DM * CHH, BF16),
                          ("wq", 8 * (128 * 4096 + 512 * DM), BF16), ("xq", 2 * DI * CH, BF16),
                          ("q1", 128 * 2 * MF, F32),
                          ("wqin", 128 * 4096 + 512 * DM, BF16),
                          ("swq", 8 * 1280, F32), ("ar1", 160, F32),
                          ("xm", 128 * 16, F32), ("hq0", 128 * 512, BF16),
                          ("q2", 128 * 2 * KD, F32),
                          ("carry", 128 * NDT * DST, F32)]:
            dbg[nm] = nc.dram_tensor("dbg_" + nm, [n], dt, kind="ExternalOutput")

    wbf_in = nc.dram_tensor("wbf_in", [DM * 2 * DI], BF16)
    wbf_x = nc.dram_tensor("wbf_x", [DI * DXW], BF16)
    wbf_dt = nc.dram_tensor("wbf_dt", [DI * DI], BF16)
    wbf_out = nc.dram_tensor("wbf_out", [DI * DM], BF16)
    z_dram = nc.dram_tensor("z_dram", [RNDS * DI * CH], BF16)
    xn_dram = nc.dram_tensor("xn_dram", [RNDS * DM * CHH], BF16)
    h_dram = nc.dram_tensor("h_dram", [4096 * TLOC], BF16)
    g_dram = nc.dram_tensor("g_dram", [DM * TLOC], BF16)
    s_dram = nc.dram_tensor("s_dram", [RNDS * DM * CH], BF16)
    rowD = nc.dram_tensor("rowD", [RNDS * 2 * CHH], F32)
    sclD = nc.dram_tensor("sclD", [64], F32)
    swq_in = nc.dram_tensor("swq_in", [1280], F32)
    swq_out = nc.dram_tensor("swq_out", [8 * 1280], F32)
    WQBLK = 128 * 4096 + 512 * DM
    wq_in = nc.dram_tensor("wq_in", [WQBLK], BF16)
    wq_out = nc.dram_tensor("wq_out", [8 * WQBLK], BF16)
    mx_in = nc.dram_tensor("mx_in", [16], F32)
    mx1_out = nc.dram_tensor("mx1_out", [16], F32)
    mx2_out = nc.dram_tensor("mx2_out", [16], F32)
    flipD = nc.dram_tensor("flipD", [128], F32)
    dSend = nc.dram_tensor("dSend", [DI * CH], BF16)
    xbcSend = nc.dram_tensor("xbcSend", [DXP * CH], BF16)
    ySend = nc.dram_tensor("ySend", [DQ * RT], BF16)
    dAGo = [nc.dram_tensor(f"dAGo{i}", [GRP * DI * CH], BF16) for i in range(2)]
    xbcAGo = [nc.dram_tensor(f"xbcAGo{i}", [GRP * DXP * CH], BF16) for i in range(2)]
    yAGo = [nc.dram_tensor(f"yAGo{i}", [GRP * DQ * RT], BF16) for i in range(2)]

    G4 = [[0, 1, 2, 3], [4, 5, 6, 7]]
    G8 = [[0, 1, 2, 3, 4, 5, 6, 7]]

    plan = {k: [] for k in ("sync", "tensor", "vector", "scalar", "gpsimd")}

    def put(eng, fn):
        plan[eng].append(fn)

    def mark(eng, fn, ctr, n=1):
        ctr.v += n
        plan[eng].append(lambda e, f=fn, c=ctr, m=n: f(e).then_inc(c.h, m))
        return ctr.v

    stack = contextlib.ExitStack()

    def sbuf(name, shape, dt):
        return stack.enter_context(nc.sbuf_tensor(name, shape, dt))

    def mksem(name):
        return Ctr(stack.enter_context(nc.semaphore(name)))

    dSY = Ctr(None)
    dGP = Ctr(None)
    dSC = Ctr(None)
    cc = mksem("cc")
    tM = mksem("tM")
    vM = mksem("vM")
    sM = mksem("sM")
    SEMS = {"dSY": dSY, "dGP": dGP, "dSC": dSC, "cc": cc, "tM": tM, "vM": vM, "sM": sM}

    RQ = 8

    class DmaTrack:
        def __init__(self, name, engname, ctr):
            self.eng = engname
            self.ctr = ctr
            self.S = [mksem(f"{name}S{r}").h for r in range(RQ)]
            self.F = mksem(f"{name}F").h
            self.n = 0
            self.slot_cum = []
            self.uses = [0] * RQ
            self.covered = 0
            self.fval = 0
            self.covers = []

    trS = DmaTrack("qS", "sync", dSY)
    trG = DmaTrack("qG", "gpsimd", dGP)
    trC = DmaTrack("qC", "scalar", dSC)
    TRACKS = {"dSY": trS, "dGP": trG, "dSC": trC}

    def ensure_relay(tr, idx):
        idx = min(idx, tr.n)
        if idx <= tr.covered:
            return
        targets = {}
        for j in range(tr.covered + 1, idx + 1):
            r, cnt = tr.slot_cum[j - 1]
            targets[r] = cnt
        for r, cnt in targets.items():
            plan[tr.eng].append(lambda e, s=tr.S[r], w=16 * cnt: e.wait_ge(s, w))
        tr.fval += 1
        plan[tr.eng].append(lambda e, s=tr.F: e.nop(nofuse=True).then_inc(s, 1))
        tr.covered = idx
        tr.covers.append((idx, tr.fval))

    def fval_for(tr, idx):
        ensure_relay(tr, idx)
        for hi, fv in tr.covers:
            if hi >= idx:
                return fv
        raise AssertionError("relay lookup failed")

    def dma_issue(tr, fn):
        if tr.n + 1 > RQ:
            ensure_relay(tr, tr.n + 1 - RQ)
        tr.n += 1
        r = (tr.n - 1) % RQ
        tr.uses[r] += 1
        tr.slot_cum.append((r, tr.uses[r]))
        plan[tr.eng].append(lambda e, f=fn, s=tr.S[r]: f(e).then_inc(s, 16))
        tr.ctr.v += 16
        return tr.ctr.v

    BAR = mksem("BAR")
    nbar = [0]
    ENGS = ("sync", "tensor", "vector", "scalar", "gpsimd")

    def quiesce(tier=1):
        if tier > bar_tier and tier >= 0:
            return
        for tr in (trS, trG, trC):
            ensure_relay(tr, tr.n)
        if cc.v > 0:
            plan["gpsimd"].append(lambda e, c=cc, w=cc.v: e.wait_ge(c.h, w))
        nbar[0] += 1
        tgt = 5 * nbar[0]
        for eng in ENGS:
            plan[eng].append(lambda e, s=BAR.h: e.nop(nofuse=True).then_inc(s, 1))
            plan[eng].append(lambda e, s=BAR.h, w=tgt: e.wait_ge(s, w))

    def wait(eng, sem, v=None):
        name = sem if isinstance(sem, str) else None
        ctr = SEMS[sem] if isinstance(sem, str) else sem
        vv = ctr.v if v is None else v
        if vv <= 0:
            return vv
        if name in TRACKS:
            tr = TRACKS[name]
            idx = (vv + 15) // 16
            fv = fval_for(tr, idx)
            plan[eng].append(lambda e, s=tr.F, w=fv: e.wait_ge(s, w))
        else:
            plan[eng].append(lambda e, c=ctr, w=vv: e.wait_ge(c.h, w))
        return vv

    onesc_b = sbuf("onesc_b", [128, 1], BF16)
    rank_sb = sbuf("rank_sb", [1, 4], I32)
    bconv_sb = sbuf("bconv_sb", [128, KI], F32)
    bdt_sb = sbuf("bdt_sb", [128, KI], F32)
    wcv_sb = sbuf("wcv_sb", [128, KI * 4], F32)
    A_sb = sbuf("A_sb", [128, NDT * DST], F32)
    D_sb = sbuf("D_sb", [128, NDT], F32)
    b1_sb = sbuf("b1_sb", [128, MF], F32)
    b2_sb = sbuf("b2_sb", [128, KD], F32)
    sw1_sb = sbuf("sw1_sb", [128, MF], F32)
    sw2_sb = sbuf("sw2_sb", [128, KD], F32)
    scale1_sb = sbuf("scale1_sb", [128, MF], F32)
    bq1_sb = sbuf("bq1_sb", [128, MF], F32)
    scale2_sb = sbuf("scale2_sb", [128, KD], F32)
    bq2_sb = sbuf("bq2_sb", [128, KD], F32)
    scl_sb = sbuf("scl_sb", [128, 4], F32)
    qs_sb = sbuf("qs_sb", [128, 24], F32)
    srow = sbuf("srow", [1, 6 * CHH], F32)
    carry = sbuf("carry", [128, NDT * DST], F32)
    xmax_sb = sbuf("xmax_sb", [128, 2], F32)
    r16 = sbuf("r16", [1, 16], F32)
    rcp5_sb = sbuf("rcp5_sb", [128, 5], F32)
    rcp1_sb = sbuf("rcp1_sb", [1, 1], F32)
    dbgx_sb = sbuf("dbgx_sb", [128, 16], F32)

    xf32 = sbuf("xf32", [128, 2 * CHH], F32)
    xsqb = sbuf("xsqb", [128, 2 * CHH], BF16)
    xb_sb = sbuf("xb_sb", [128, KD * CHH], BF16)
    xn_sb = sbuf("xn_sb", [128, KD * CHH], BF16)
    rbc_sb = sbuf("rbc_sb", [128, CHH], BF16)
    mbc_sb = sbuf("mbc_sb", [128, CHH], BF16)
    WF = max(2 * 2048, 2 * SCH)
    wf32 = sbuf("wf32", [128, WF], F32)
    wbf = sbuf("wbf", [128, 2 * 2048], BF16)
    xin_sb = sbuf("xin_sb", [128, 3 * CHH], BF16)
    xcv_sb = sbuf("xcv_sb", [128, KI * CH], BF16)
    drw_sb = sbuf("drw_sb", [128, KI * CH], BF16)
    stg_sb = sbuf("stg_sb", [128, 2 * CH], BF16)
    stg2_sb = sbuf("stg2_sb", [128, 2 * CH], BF16)
    bcstg = sbuf("bcstg", [32, CH], BF16)

    dR_sb = sbuf("dR_sb", [128, NSLOT * NDT * SCH], BF16)
    xcR_sb = sbuf("xcR_sb", [128, NSLOT * NDT * SCH], BF16)
    b_sb = sbuf("b_sb", [128, SCH], BF16)
    st_sb = sbuf("st_sb", [128, SCH], F32)
    pr_sb = sbuf("pr_sb", [128, SCH], BF16)
    Bex_sb = sbuf("Bex_sb", [128, 2 * SCH], BF16)
    Cex_sb = sbuf("Cex_sb", [128, 2 * SCH], BF16)
    yacc_sb = sbuf("yacc_sb", [128, NDT * SCH], BF16)
    yfin_sb = sbuf("yfin_sb", [128, 2 * SCH], BF16)

    yt_sb = sbuf("yt_sb", [128, KI * CH], BF16)
    szt_sb = sbuf("szt_sb", [128, 2 * CH], BF16)

    iq_sb = sbuf("iq_sb", [128, max(2 * CH, 1024)], I32)
    wqb_sb = sbuf("wqb_sb", [128, 1024], BF16)
    invbc_sb = wf32[:, 1024:2048]
    wqi_sb = iq_sb
    xro_sb = sbuf("xro_sb", [128, CH], F32)
    out_sb = sbuf("out_sb", [128, CH], F32)

    ps = [stack.enter_context(nc.psum_tensor(f"ps{i}", [128, 512], F32)) for i in range(8)]

    regs = {}

    def dmaS(out, in_):
        return dma_issue(trS, lambda e, o=out, i=in_: e.dma_start(out=o, in_=i))

    def dma_slot(eng, buf, slot, out, in_):
        ctr = buf.sems[slot]
        ctr.v += 16
        plan[eng].append(lambda e, o=out, i=in_, c=ctr: e.dma_start(out=o, in_=i)
                         .then_inc(c.h, 16))
        return ctr, ctr.v

    def dmaG(out, in_):
        return dma_issue(trG, lambda e, o=out, i=in_: e.dma_start(out=o, in_=i))

    def dmaC(out, in_):
        return dma_issue(trC, lambda e, o=out, i=in_: e.dma_start(out=o, in_=i))

    def dmaS_nc(out, in_):
        return dma_issue(trS, lambda e, o=out, i=in_: e.dma_start(
            out=o, in_=i, allow_slow_non_contiguous=True))

    def dmaG_nc(out, in_):
        return dma_issue(trG, lambda e, o=out, i=in_: e.dma_start(
            out=o, in_=i, allow_slow_non_contiguous=True))

    class Buf:
        def __init__(self, n, name=None):
            self.g = [[] for _ in range(n)]
            self.sems = [mksem(f"{name}{i}") for i in range(n)] if name else None

        def acquire(self, eng, slot):
            for s, v in self.g[slot]:
                wait(eng, s, v)
            self.g[slot] = []

        def consumed(self, slot, sem_name, v):
            self.g[slot].append((sem_name, v))

    xf32B = Buf(2, "bXF")
    wbfB = Buf(2, "bWB")
    wf32B = Buf(2, "bWF")
    stgB = Buf(2)
    stg2B = Buf(2)
    xinB = Buf(3)
    aB = Buf(4)
    BexB = Buf(2)
    CexB = Buf(2)
    yfinB = Buf(2)
    iqB = Buf(2)
    xroB = Buf(1)
    outB = Buf(1)
    sztB = Buf(2)

    bank_ev = [("sM", 0)] * 8

    def te_take(bk):
        s, v = bank_ev[bk]
        if v > 0:
            wait("tensor", s, v)

    def matgroup(emitters):
        for f in emitters[:-1]:
            put("tensor", f)
        return mark("tensor", emitters[-1], tM)

    dmaS(rank_sb[:], rank_in.ap())
    dmaS(bconv_sb[:], bconv_t.ap())
    dmaS(bdt_sb[:], bdt_t.ap())
    dmaS(wcv_sb[:], wcv_t.ap())
    dmaS(A_sb[:], Alog_my.ap())
    dmaS(D_sb[:], D_my.ap())
    dmaS(b1_sb[:], b1_t.ap())
    ld_setup = dmaS(b2_sb[:], b2_t.ap())

    wait("sync", "dSY", 16)

    def _ld_rank(e):
        regs["q"] = e.value_load(rank_sb[0:1, 0:1])
        regs["qCH"] = e.value_load(rank_sb[0:1, 1:2])
        regs["qn4"] = e.value_load(rank_sb[0:1, 2:3])
    put("sync", _ld_rank)

    put("vector", lambda e: e.memset(onesc_b[:], 1.0))
    put("vector", lambda e: e.memset(carry[:], 0.0))
    put("vector", lambda e: e.memset(r16[:], 0.0))
    put("vector", lambda e: e.memset(srow[0:1, 4 * CHH:4 * CHH + 1], 1e-5))
    mark("vector", lambda e: e.memset(xmax_sb[:], -1e30), vM)
    m_z = mark("vector", lambda e: e.memset(yacc_sb[0:96, 0:CH], 0.0), vM)
    wait("gpsimd", "vM", m_z)
    g_pad = dmaG(bass.AP(xbcSend, DXW * CH, [[CH, 96], [1, CH]]), yacc_sb[0:96, 0:CH])
    wait("vector", "dGP", g_pad)

    wait("scalar", "dSY", ld_setup)
    put("scalar", lambda e: e.activation(A_sb[:], A_sb[:], AF.Exp))
    mark("scalar", lambda e: e.mul(A_sb[:], A_sb[:], -1.0), sM)

    for i4 in range(4):
        sl = i4 % 2
        ld = dmaS(wf32[:, sl * 2048: sl * 2048 + DM], w1_rows.ap()[i4 * 128:(i4 + 1) * 128, :])
        wait("vector", "dSY", ld)
        put("vector", lambda e, i=i4, s=sl: e.tensor_reduce(
            qs_sb[:, i:i + 1], wf32[:, s * 2048: s * 2048 + DM],
            axis=AX.X, op=ALU.max, apply_absolute_value=True))
    ld = dmaS(wf32[:, 0:2048], w2_rows.ap()[:, 0:2048])
    ld = dmaS(wf32[:, 2048:4096], w2_rows.ap()[:, 2048:4096])
    wait("vector", "dSY", ld)
    put("vector", lambda e: e.tensor_reduce(qs_sb[:, 4:5], wf32[:, 0:2048],
                                            axis=AX.X, op=ALU.max, apply_absolute_value=True))
    v_hz = mark("vector", lambda e: e.tensor_reduce(qs_sb[:, 5:6], wf32[:, 2048:4096],
                                                    axis=AX.X, op=ALU.max,
                                                    apply_absolute_value=True), vM)
    wait("vector", "vM", v_hz)
    v_hz = mark("vector", lambda e: e.tensor_max(qs_sb[:, 4:5], qs_sb[:, 4:5],
                                                 qs_sb[:, 5:6]), vM)
    wait("vector", "vM", v_hz)
    put("vector", lambda e: e.tensor_scalar_mul(qs_sb[:, 8:13], qs_sb[:, 0:5], 1.0 / 127.0))
    v_hz = mark("vector", lambda e: e.reciprocal(rcp5_sb[:], qs_sb[:, 0:5]), vM)
    wait("vector", "vM", v_hz)
    v_sw = mark("vector", lambda e: e.tensor_scalar_mul(qs_sb[:, 16:21], rcp5_sb[:],
                                                        127.0), vM)
    wait("gpsimd", "vM", v_sw)
    dmaG_nc(bass.AP(swq_in, 0, [[1, 128], [128, 4]]), qs_sb[:, 8:12])
    dmaG_nc(bass.AP(swq_in, 512, [[1, 128], [128, 4]]), qs_sb[:, 16:20])
    dmaG_nc(bass.AP(swq_in, 1024, [[1, 128], [128, 1]]), qs_sb[:, 12:13])
    g_pack = dmaG_nc(bass.AP(swq_in, 1152, [[1, 128], [128, 1]]), qs_sb[:, 20:21])
    wait("gpsimd", "dGP", g_pack)
    cc_sw = mark("gpsimd", lambda e: e.collective_compute(
        "AllGather", ALU.bypass, ins=[swq_in.ap()], outs=[swq_out.ap()],
        replica_groups=G8), cc)

    wait("sync", "cc", cc_sw)
    for r in range(8):
        dmaS_nc(sw1_sb[:, 4 * r:4 * r + 4], bass.AP(swq_out, r * 1280, [[1, 128], [128, 4]]))
    dmaS_nc(sw2_sb[:, 0:KD], bass.AP(swq_out, 1024, [[1, 128], [1280, KD]]))

    v_q_prev = 0
    g_prev = 0
    for p4 in range(4):
        if v_q_prev > 0:
            wait("sync", "vM", v_q_prev)
        bcv = dmaS(wf32[:, 1024:2048], bass.AP(swq_out, 2 * p4 * 1280 + 512,
                                               [[0, 128], [1280, 2], [1, 512]]))
        off = (p4 % 2) * 2048
        ldw = dmaS(wf32[:, off:off + 1024], w1T_myk.ap()[:, p4 * 1024:(p4 + 1) * 1024])
        wait("vector", "dSY", max(bcv, ldw))
        v_q = mark("vector", lambda e, o=off: e.tensor_tensor(
            wf32[:, o:o + 1024], wf32[:, o:o + 1024], wf32[:, 1024:2048], ALU.mult), vM)
        v_q_prev = v_q
        wait("scalar", "vM", v_q)
        put("scalar", lambda e, o=off: e.copy(wqi_sb[:, 0:1024], wf32[:, o:o + 1024]))
        if g_prev > 0:
            wait("scalar", "dGP", g_prev)
        s_q = mark("scalar", lambda e: e.copy(wqb_sb[:], wqi_sb[:]), sM)
        wait("gpsimd", "sM", s_q)
        g_prev = dmaG(bass.AP(wq_in, p4 * 1024, [[4096, 128], [1, 1024]]), wqb_sb[:])
    wait("sync", "vM", v_q_prev)
    bcv2 = dmaS(wf32[:, 1024:2048], bass.AP(swq_out, 1152, [[0, 128], [1280, 8], [1, 128]]))
    for i4 in range(4):
        if i4 >= 2:
            wait("sync", "vM", v_q_prev)
        off = (i4 % 2) * 2048
        ldw = dmaS(wf32[:, off:off + 1024], w2T_myk.ap()[i4 * 128:(i4 + 1) * 128, :])
        wait("vector", "dSY", max(bcv2, ldw))
        v_q = mark("vector", lambda e, o=off: e.tensor_tensor(
            wf32[:, o:o + 1024], wf32[:, o:o + 1024], wf32[:, 1024:2048], ALU.mult), vM)
        v_q_prev = v_q
        wait("scalar", "vM", v_q)
        put("scalar", lambda e, o=off: e.copy(wqi_sb[:, 0:1024], wf32[:, o:o + 1024]))
        wait("scalar", "dGP", g_prev)
        s_q = mark("scalar", lambda e: e.copy(wqb_sb[:], wqi_sb[:]), sM)
        wait("gpsimd", "sM", s_q)
        g_prev = dmaG(bass.AP(wq_in, 128 * 4096 + i4 * 131072, [[1024, 128], [1, 1024]]),
                      wqb_sb[:])
    wf32B.consumed(0, "vM", v_q_prev)
    wf32B.consumed(1, "vM", v_q_prev)
    wait("gpsimd", "dGP", dGP.v)
    cc_wq = mark("gpsimd", lambda e: e.collective_compute(
        "AllGather", ALU.bypass, ins=[wq_in.ap()], outs=[wq_out.ap()],
        replica_groups=G8), cc)

    for m in range(KD):
        sl = m % 2
        wf32B.acquire("sync", sl)
        ldw = dmaS(wf32[:, sl * 2048:sl * 2048 + 2048],
                   bass.AP(WoutT.ap().tensor, m * 128, [[DM, 128], [128 * DM, KI], [1, 128]]))
        wbfB.acquire("scalar", sl)
        wait("scalar", "dSY", ldw)
        s_c = mark("scalar", lambda e, s=sl: e.copy(
            wbf[:, s * 2048:s * 2048 + 2048], wf32[:, s * 2048:s * 2048 + 2048]), sM)
        wf32B.consumed(sl, "sM", s_c)
        wait("gpsimd", "sM", s_c)
        g_sp = dmaG(bass.AP(wbf_out, m * 128, [[DM, 128], [128 * DM, KI], [1, 128]]),
                    wbf[:, sl * 2048:sl * 2048 + 2048])
        wbfB.consumed(sl, "dGP", g_sp)
    quiesce(tier=-1)
    setup_v = vM.v

    recv_half = {}
    cm_last = {}
    scan_seq = []
    scan_mk = {}
    a_seq = []
    bex_marks = {}
    bex_seq = []
    yg_consumed = {}
    ag_mark = {}
    ysend_mark = {}
    xcsend_mark = {}
    prev = {"t_stats": 0, "v_xn": 0, "win_last": 0, "wx_last": 0, "wdt_last": 0,
            "xn_spill": 0, "conv_last": 0, "wout_last": 0, "v_yg": 0,
            "bex_round_last": {}, "recv_last": {}, "rows_dma": 0, "v_xm": 0}
    conv_mark = {}
    zmark = {}

    def w_stream(j, f32_t, bf_t, row_stride, m, mw, kk, off=None):
        sl = m % 2
        n = kk * mw
        if off is None:
            off = m * mw
        dst = wbf[:, sl * 2048: sl * 2048 + n]
        if j == 0:
            wf32B.acquire("sync", sl)
            lc, lv = dma_slot("sync", wf32B, sl, wf32[:, sl * 2048:sl * 2048 + n],
                              bass.AP(f32_t, off,
                                      [[row_stride, 128], [128 * row_stride, kk],
                                       [1, mw]]))
            wbfB.acquire("scalar", sl)
            wait("scalar", lc, lv)
            s_c = mark("scalar", lambda e, s=sl, n=n: e.copy(
                wbf[:, s * 2048:s * 2048 + n], wf32[:, s * 2048:s * 2048 + n]), sM)
            wf32B.consumed(sl, "sM", s_c)
            wait("gpsimd", "sM", s_c)
            g_sp = dmaG(bass.AP(bf_t, off,
                                [[row_stride, 128], [128 * row_stride, kk], [1, mw]]), dst)
            wbfB.consumed(sl, "dGP", g_sp)
            return dst, ("sM", s_c), sl
        wbfB.acquire("sync", sl)
        lc, lv = dma_slot("sync", wbfB, sl, dst,
                          bass.AP(bf_t, off,
                                  [[row_stride, 128], [128 * row_stride, kk], [1, mw]]))
        return dst, (lc, lv), sl

    def emit_recv(j):
        jb = j % 2
        wait("sync", "cc", ag_mark[("d", j)])
        wait("sync", "cc", ag_mark[("x", j)])
        dview = dAGo[jb].ap().rearrange("(r n p t) -> p n r t", r=GRP, p=128, t=CH)
        xview = xbcAGo[jb].ap().rearrange("(r n p t) -> p n r t", r=GRP, p=128, t=CH)
        for h in range(2):
            gslot = (2 * j + h) % NSLOT
            if 2 * j + h >= NSLOT:
                pidx = 2 * j + h - NSLOT
                pj, ph = pidx // 2, pidx % 2
                wait("sync", "vM", cm_last[(pj, ph, DST - 1)])
                wait("sync", "vM", yfin_last.get((pj, ph), 0))
            for rr2 in range(2):
                def fd(e, h=h, gs=gslot, dv=dview, rr2=rr2):
                    o = dR_sb[:, gs * NDT * SCH:(gs + 1) * NDT * SCH].rearrange(
                        "p (n x) -> p n x", n=NDT, x=SCH)[:, :, rr2 * CH:(rr2 + 1) * CH]
                    return e.dma_start(
                        out=o, in_=dv[:, bass.ds(regs["qn4"], NDT),
                                      2 * h + rr2:2 * h + rr2 + 1, :])
                dma_issue(trS, fd)
                def fx(e, h=h, gs=gslot, xv=xview, rr2=rr2):
                    o = xcR_sb[:, gs * NDT * SCH:(gs + 1) * NDT * SCH].rearrange(
                        "p (n x) -> p n x", n=NDT, x=SCH)[:, :, rr2 * CH:(rr2 + 1) * CH]
                    return e.dma_start(
                        out=o, in_=xv[:, bass.ds(regs["qn4"], NDT),
                                      2 * h + rr2:2 * h + rr2 + 1, :])
                recv_half[(j, h)] = dma_issue(trS, fx)
        prev["recv_last"][j] = recv_half[(j, 1)]

    yfin_last = {}

    def emit_scan_half_gen(j, h):
        jb = j % 2
        gslot = (2 * j + h) % NSLOT
        wait("scalar", "dSY", recv_half[(j, h)])
        wait("vector", "dSY", recv_half[(j, h)])
        for s in range(DST):
            eb = len(bex_seq) % 2
            if len(bex_seq) >= 2:
                pk = bex_seq[-2]
                BexB.acquire("scalar", eb)
                CexB.acquire("scalar", eb)
            dmaC(Bex_sb[:, eb * SCH:(eb + 1) * SCH],
                 bass.AP(xbcAGo[jb], (2 * h) * DXP * CH + (DI + s) * CH,
                         [[0, 128], [DXP * CH, 2], [1, CH]]))
            bx = dmaC(Cex_sb[:, eb * SCH:(eb + 1) * SCH],
                      bass.AP(xbcAGo[jb], (2 * h) * DXP * CH + (DI + DST + s) * CH,
                              [[0, 128], [DXP * CH, 2], [1, CH]]))
            bex_marks[(j, h, s)] = bx
            bex_seq.append((j, h, s))
            wait("vector", "dSC", bx)
            for dt in range(NDT):
                ai = len(a_seq) % 4
                aB.acquire("scalar", ai)
                src = dR_sb[:, (gslot * NDT + dt) * SCH:(gslot * NDT + dt + 1) * SCH]
                av = wf32[:, ai * SCH:(ai + 1) * SCH]
                a_v = mark("scalar", lambda e, src=src, av=av, dt=dt, s=s: e.activation(
                    av, src, AF.Exp, scale=A_sb[:, dt * DST + s:dt * DST + s + 1]), sM)
                a_seq.append((j, h, s, dt))
                xc = xcR_sb[:, (gslot * NDT + dt) * SCH:(gslot * NDT + dt + 1) * SCH]
                Bx = Bex_sb[:, eb * SCH:(eb + 1) * SCH]
                Cx = Cex_sb[:, eb * SCH:(eb + 1) * SCH]
                wait("vector", "sM", a_v)
                bbv = mark("vector", lambda e, xc=xc, Bx=Bx: e.tensor_mul(b_sb[:], xc, Bx), vM)
                sc_v = mark("vector", lambda e, av=av, dt=dt, s=s: e.tensor_tensor_scan(
                    st_sb[:], av, b_sb[:], carry[:, dt * DST + s:dt * DST + s + 1],
                    ALU.mult, ALU.add), vM)
                scan_mk[(j, h, s, dt)] = sc_v
                aB.consumed(ai, "vM", sc_v)
                wait("vector", "vM", sc_v)
                put("vector", lambda e, dt=dt, s=s: e.tensor_copy(
                    carry[:, dt * DST + s:dt * DST + s + 1], st_sb[:, SCH - 1:SCH]))
                cmv = mark("vector", lambda e, Cx=Cx: e.tensor_mul(pr_sb[:], st_sb[:], Cx), vM)
                if dt == NDT - 1:
                    cm_last[(j, h, s)] = cmv
                    BexB.consumed(eb, "vM", cmv)
                    CexB.consumed(eb, "vM", cmv)
                ya = yacc_sb[:, dt * SCH:(dt + 1) * SCH]
                if s == 0:
                    put("vector", lambda e, ya=ya: e.tensor_copy(ya, pr_sb[:]))
                else:
                    put("vector", lambda e, ya=ya: e.tensor_add(ya, pr_sb[:], ya))
            yield
        if h == 0 and j >= 1:
            wait("gpsimd", "cc", ag_mark[("y", j - 1)])
        for dt in range(NDT):
            xc = xcR_sb[:, (gslot * NDT + dt) * SCH:(gslot * NDT + dt + 1) * SCH]
            ya = yacc_sb[:, dt * SCH:(dt + 1) * SCH]
            ysl = dt % 2
            yfinB.acquire("vector", ysl)
            yf = yfin_sb[:, ysl * SCH:(ysl + 1) * SCH]
            v_yf = mark("vector", lambda e, xc=xc, ya=ya, yf=yf, dt=dt:
                        e.scalar_tensor_tensor(yf, xc, D_sb[:, dt:dt + 1], ya,
                                               ALU.mult, ALU.add), vM)
            yfin_last[(j, h)] = v_yf
            wait("gpsimd", "vM", v_yf)
            g_ys = dmaG(bass.AP(ySend, dt * 128 * RT + h * SCH, [[RT, 128], [1, SCH]]), yf)
            yfinB.consumed(ysl, "dGP", g_ys)
        if h == 1:
            prev["bex_round_last"][j] = dSC.v
            wait("gpsimd", "dGP", dGP.v)
            if j >= 2:
                wait("gpsimd", "vM", yg_consumed[j - 2])
            ag_mark[("y", j)] = mark("gpsimd", lambda e, jb=jb: e.collective_compute(
                "AllGather", ALU.bypass, ins=[ySend.ap()], outs=[yAGo[jb].ap()],
                replica_groups=G4), cc)
        yield

    def emit_scan_half(j, h):
        for _ in emit_scan_half_gen(j, h):
            pass

    def emit_gating(j):
        jb = j % 2
        wait("sync", "cc", ag_mark[("y", j)])
        if prev["wout_last"] > 0:
            wait("sync", "tM", prev["wout_last"])
            wait("sync", "vM", prev["v_yg"])
        yv = yAGo[jb].ap().rearrange("(r n p t) -> p r n t", r=GRP, p=128, t=RT)
        y_ld = 0
        for r in range(GRP):
            def fy(e, r=r, yv=yv):
                return e.dma_start(
                    out=yt_sb[:, r * NDT * CH:(r + 1) * NDT * CH].rearrange(
                        "p (r n t) -> p r n t", r=1, n=NDT, t=CH),
                    in_=yv[:, r:r + 1, :, bass.ds(regs["qCH"], CH)])
            y_ld = dma_issue(trS, fy)
        wait("sync", "dGP", zmark[j])
        for kt in range(KI):
            zsl = kt % 2
            sztB.acquire("sync", zsl)
            ldz = dmaS(szt_sb[:, zsl * CH:(zsl + 1) * CH],
                       bass.AP(z_dram, (j * DI + kt * 128) * CH, [[CH, 128], [1, CH]]))
            wait("scalar", "dSY", ldz)
            s_z = mark("scalar", lambda e, zs=zsl: e.activation(
                szt_sb[:, zs * CH:(zs + 1) * CH], szt_sb[:, zs * CH:(zs + 1) * CH],
                AF.Silu), sM)
            wait("vector", "sM", s_z)
            if kt == 0:
                wait("vector", "dSY", y_ld)
            v_g = mark("vector", lambda e, kt=kt, zs=zsl: e.tensor_mul(
                yt_sb[:, kt * CH:(kt + 1) * CH], yt_sb[:, kt * CH:(kt + 1) * CH],
                szt_sb[:, zs * CH:(zs + 1) * CH]), vM)
            sztB.consumed(zsl, "vM", v_g)
        v_yg = vM.v
        yg_consumed[j] = v_yg
        prev["v_yg"] = v_yg
        for m in range(KD):
            sl = m % 2
            wbfB.acquire("sync", sl)
            lc3, lv3 = dma_slot("sync", wbfB, sl, wbf[:, sl * 2048:sl * 2048 + 2048],
                                bass.AP(wbf_out, m * 128,
                                        [[DM, 128], [128 * DM, KI], [1, 128]]))
            bk = 5 if m % 2 == 0 else 6
            te_take(bk)
            wait("tensor", lc3, lv3)
            wait("tensor", "vM", v_yg)
            t_mm = matgroup([
                (lambda e, k=k, b=bk, sl=sl: e.matmul(
                    ps[b][:, 0:CH], wbf[:, sl * 2048 + k * 128:sl * 2048 + (k + 1) * 128],
                    yt_sb[:, k * CH:(k + 1) * CH], start=(k == 0), stop=(k == KI - 1)))
                for k in range(KI)])
            wbfB.consumed(sl, "tM", t_mm)
            wait("scalar", "tM", t_mm)
            ssl = m % 2
            stgB.acquire("scalar", ssl)
            s_ev = mark("scalar", lambda e, b=bk, ss=ssl: e.copy(
                stg_sb[:, ss * CH:(ss + 1) * CH], ps[b][:, 0:CH]), sM)
            bank_ev[bk] = ("sM", s_ev)
            wait("gpsimd", "sM", s_ev)
            g_sp = dmaG(bass.AP(s_dram, (j * DM + m * 128) * CH, [[CH, 128], [1, CH]]),
                        stg_sb[:, ssl * CH:(ssl + 1) * CH])
            stgB.consumed(ssl, "dGP", g_sp)
            prev["wout_last"] = t_mm

    first_wload = True
    for j in range(RNDS):
        jb = j % 2
        co = j * CHH

        if j >= 1:
            emit_recv(j - 1)

        t_stats = 0
        for k in range(KD):
            sl = k % 2
            xf32B.acquire("sync", sl)
            lcx, lvx = dma_slot("sync", xf32B, sl, xf32[:, sl * CHH:(sl + 1) * CHH],
                                x_fm.ap()[k * 128:(k + 1) * 128, co:co + CHH])
            wait("scalar", lcx, lvx)
            if k == 0:
                wait("scalar", "tM", prev["t_stats"])
                wait("scalar", "vM", prev["v_xn"])
            put("scalar", lambda e, s=sl, k=k: e.copy(
                xb_sb[:, k * CHH:(k + 1) * CHH], xf32[:, s * CHH:(s + 1) * CHH]))
            s_k = mark("scalar", lambda e, s=sl: e.activation(
                xsqb[:, s * CHH:(s + 1) * CHH], xf32[:, s * CHH:(s + 1) * CHH],
                AF.Square), sM)
            xf32B.consumed(sl, "sM", s_k)
            wait("tensor", "sM", s_k)
            if k == 0:
                te_take(2), te_take(3), te_take(4), te_take(5)
            put("tensor", lambda e, k=k: e.matmul(
                ps[2][0:1, 0:CH], onesc_b[:, 0:1], xb_sb[:, k * CHH + HALO:(k + 1) * CHH],
                start=(k == 0), stop=(k == KD - 1)))
            put("tensor", lambda e, k=k, s=sl: e.matmul(
                ps[3][0:1, 0:CH], onesc_b[:, 0:1], xsqb[:, s * CHH + HALO:(s + 1) * CHH],
                start=(k == 0), stop=(k == KD - 1)))
            put("tensor", lambda e, k=k: e.matmul(
                ps[4][0:1, 0:HALO], onesc_b[:, 0:1], xb_sb[:, k * CHH:k * CHH + HALO],
                start=(k == 0), stop=(k == KD - 1)))
            t_stats = mark("tensor", lambda e, k=k, s=sl: e.matmul(
                ps[5][0:1, 0:HALO], onesc_b[:, 0:1], xsqb[:, s * CHH:s * CHH + HALO],
                start=(k == 0), stop=(k == KD - 1)), tM)
        prev["t_stats"] = t_stats
        wait("scalar", "tM", t_stats)
        put("scalar", lambda e: e.copy(srow[0:1, HALO:CHH], ps[2][0:1, 0:CH]))
        put("scalar", lambda e: e.copy(srow[0:1, CHH + HALO:2 * CHH], ps[3][0:1, 0:CH]))
        put("scalar", lambda e: e.copy(srow[0:1, 0:HALO], ps[4][0:1, 0:HALO]))
        put("scalar", lambda e: e.copy(srow[0:1, CHH:CHH + HALO], ps[5][0:1, 0:HALO]))
        put("scalar", lambda e: e.mul(srow[0:1, 2 * CHH:3 * CHH], srow[0:1, 0:CHH], 1.0 / DM))
        s_st = mark("scalar", lambda e: e.mul(srow[0:1, CHH:2 * CHH],
                                              srow[0:1, CHH:2 * CHH], 1.0 / DM), sM)
        bank_ev[2] = bank_ev[3] = bank_ev[4] = bank_ev[5] = ("sM", s_st)
        wait("vector", "sM", s_st)
        put("vector", lambda e: e.tensor_mul(srow[0:1, 5 * CHH:6 * CHH],
                                             srow[0:1, 2 * CHH:3 * CHH],
                                             srow[0:1, 2 * CHH:3 * CHH]))
        v_var = mark("vector", lambda e: e.tensor_sub(srow[0:1, CHH:2 * CHH],
                                                      srow[0:1, CHH:2 * CHH],
                                                      srow[0:1, 5 * CHH:6 * CHH]), vM)
        wait("scalar", "vM", v_var)
        put("scalar", lambda e: e.activation(srow[0:1, 3 * CHH:4 * CHH],
                                             srow[0:1, CHH:2 * CHH], AF.Ln,
                                             bias=srow[0:1, 4 * CHH:4 * CHH + 1]))
        s_r = mark("scalar", lambda e: e.activation(srow[0:1, 3 * CHH:4 * CHH],
                                                    srow[0:1, 3 * CHH:4 * CHH], AF.Exp,
                                                    scale=-0.5), sM)
        wait("sync", "sM", s_r)
        wait("sync", "dSY", prev["rows_dma"])
        dmaS(bass.AP(rowD, j * 2 * CHH, [[CHH, 1], [1, CHH]]), srow[0:1, 3 * CHH:4 * CHH])
        rr = dmaS(bass.AP(rowD, j * 2 * CHH + CHH, [[CHH, 1], [1, CHH]]),
                  srow[0:1, 2 * CHH:3 * CHH])
        if prev["v_xn"] > 0:
            wait("gpsimd", "vM", prev["v_xn"])
        wait("gpsimd", "dSY", rr)
        rb = dmaG(rbc_sb[:], bass.AP(rowD, j * 2 * CHH, [[0, 128], [1, CHH]]))
        mb = dmaG(mbc_sb[:], bass.AP(rowD, j * 2 * CHH + CHH, [[0, 128], [1, CHH]]))
        prev["rows_dma"] = rr
        wait("vector", "dGP", mb)
        if prev["win_last"] > 0:
            wait("vector", "tM", prev["win_last"])
        if prev["xn_spill"] > 0:
            wait("vector", "dGP", prev["xn_spill"])
        if j == 0:
            wait("vector", "vM", setup_v)
        for k in range(KD):
            put("vector", lambda e, k=k: e.tensor_sub(
                xn_sb[:, k * CHH:(k + 1) * CHH], xb_sb[:, k * CHH:(k + 1) * CHH], mbc_sb[:]))
            put("vector", lambda e, k=k: e.tensor_mul(
                xn_sb[:, k * CHH:(k + 1) * CHH], xn_sb[:, k * CHH:(k + 1) * CHH], rbc_sb[:]))
        for k in range(KD):
            v_hz = mark("vector", lambda e, k=k: e.tensor_reduce(
                qs_sb[:, 22:23], xn_sb[:, k * CHH + HALO:(k + 1) * CHH],
                axis=AX.X, op=ALU.max), vM)
            wait("vector", "vM", v_hz)
            put("vector", lambda e: e.tensor_max(xmax_sb[:, 0:1], xmax_sb[:, 0:1],
                                                 qs_sb[:, 22:23]))
        if debug and j == 0:
            put("vector", lambda e: e.tensor_copy(dbgx_sb[:, 0:2], xmax_sb[:, 0:2]))
            put("vector", lambda e: e.tensor_copy(dbgx_sb[:, 6:7], qs_sb[:, 22:23]))
        if debug and j == RNDS - 1:
            put("vector", lambda e: e.tensor_copy(dbgx_sb[:, 2:4], xmax_sb[:, 0:2]))
        v_xn = mark("vector", lambda e: e.tensor_copy(qs_sb[:, 22:23], qs_sb[:, 22:23]), vM)
        prev["v_xn"] = v_xn
        prev["v_xm"] = v_xn
        wait("gpsimd", "vM", v_xn)
        prev["xn_spill"] = dmaG(
            bass.AP(xn_dram, j * DM * CHH, [[CHH, 128], [128 * CHH, KD], [1, CHH]]), xn_sb[:])
        quiesce(tier=2)

        if first_wload:
            wait("sync", "vM", setup_v)
            first_wload = False
        win_last = 0
        for m in range(MI):
            wsl, wready, slot = w_stream(j, WinT.ap().tensor, wbf_in, 2 * DI, m, 128, KD)
            bk = m % 2
            te_take(bk)
            wait("tensor", wready[0], wready[1])
            wait("tensor", "vM", v_xn)
            t_mm = matgroup([
                (lambda e, k=k, w=wsl, b=bk: e.matmul(
                    ps[b][:, 0:CH], w[:, k * 128:(k + 1) * 128],
                    xn_sb[:, k * CHH + HALO:(k + 1) * CHH],
                    start=(k == 0), stop=(k == KD - 1)))
                for k in range(KD)])
            t_last = t_mm
            if m < KI:
                te_take(6)
                t_last = matgroup([
                    (lambda e, k=k, w=wsl, m=m: e.matmul(
                        ps[6][:, 3 * m:3 * m + HALO], w[:, k * 128:(k + 1) * 128],
                        xn_sb[:, k * CHH:k * CHH + HALO],
                        start=(k == 0), stop=(k == KD - 1)))
                    for k in range(KD)])
            wbfB.consumed(slot, "tM", t_last)
            win_last = t_last
            wait("scalar", "tM", t_last)
            if m < KI:
                xsl = m % 3
                xinB.acquire("scalar", xsl)
                put("scalar", lambda e, xs=xsl, b=bk: e.copy(
                    xin_sb[:, xs * CHH + HALO:(xs + 1) * CHH], ps[b][:, 0:CH]))
                s_ev = mark("scalar", lambda e, m=m, xs=xsl: e.copy(
                    xin_sb[:, xs * CHH:xs * CHH + HALO], ps[6][:, 3 * m:3 * m + HALO]), sM)
                bank_ev[6] = ("sM", s_ev)
                wait("vector", "sM", s_ev)
                if m == 0:
                    if prev["wx_last"] > 0:
                        wait("vector", "tM", prev["wx_last"])
                    if xcsend_mark.get(j - 1, 0) > 0:
                        wait("vector", "dGP", xcsend_mark[j - 1])
                xi = xin_sb[:, xsl * CHH:(xsl + 1) * CHH]
                xc = xcv_sb[:, m * CH:(m + 1) * CH]
                put("vector", lambda e, xi=xi, xc=xc, d=m: e.tensor_scalar(
                    xc, xi[:, 0:CH], wcv_sb[:, d * 4:d * 4 + 1], None, ALU.mult))
                for kk in range(1, 4):
                    put("vector", lambda e, xi=xi, xc=xc, d=m, kk=kk: e.scalar_tensor_tensor(
                        xc, xi[:, kk:kk + CH], wcv_sb[:, d * 4 + kk:d * 4 + kk + 1], xc,
                        ALU.mult, ALU.add))
                v_cv = mark("vector", lambda e, xc=xc: e.tensor_copy(xc[:, 0:1], xc[:, 0:1]),
                            vM)
                conv_mark[(j, m)] = v_cv
                xinB.consumed(xsl, "vM", v_cv)
                wait("scalar", "vM", v_cv)
                s_ev = mark("scalar", lambda e, xc=xc, d=m: e.activation(
                    xc, xc, AF.Silu, bias=bconv_sb[:, d:d + 1]), sM)
            else:
                zsl = m % 2
                stgB.acquire("scalar", zsl)
                s_ev = mark("scalar", lambda e, b=bk, zs=zsl: e.copy(
                    stg_sb[:, zs * CH:(zs + 1) * CH], ps[b][:, 0:CH]), sM)
                wait("gpsimd", "sM", s_ev)
                g_z = dmaG(bass.AP(z_dram, (j * DI + (m - KI) * 128) * CH,
                                   [[CH, 128], [1, CH]]),
                           stg_sb[:, zsl * CH:(zsl + 1) * CH])
                zmark[j] = g_z
                stgB.consumed(zsl, "dGP", g_z)
            bank_ev[bk] = ("sM", s_ev)
        prev["win_last"] = win_last
        xc_ready = sM.v

        wait("gpsimd", "sM", xc_ready)
        if j >= 1:
            wait("gpsimd", "cc", ag_mark[("x", j - 1)])
        for kk in range(KI):
            xcsend_mark[j] = dmaG(bass.AP(xbcSend, kk * 128 * CH, [[CH, 128], [1, CH]]),
                                  xcv_sb[:, kk * CH:(kk + 1) * CH])

        gen0 = emit_scan_half_gen(j - 1, 0) if j >= 1 else iter(())

        wx_last = 0
        for m in range(KI + 1):
            next(gen0, None)
            mw = 128 if m < KI else 2 * DST
            wsl, wready, slot = w_stream(j, WxT.ap().tensor, wbf_x, DXW, m, mw, KI,
                                         off=m * 128)
            bk = m % 2
            te_take(bk)
            wait("tensor", wready[0], wready[1])
            wait("tensor", "sM", xc_ready)
            t_mm = matgroup([
                (lambda e, k=k, w=wsl, b=bk, mw=mw: e.matmul(
                    ps[b][0:mw, 0:CH], w[:, k * mw:(k + 1) * mw],
                    xcv_sb[:, k * CH:(k + 1) * CH], start=(k == 0), stop=(k == KI - 1)))
                for k in range(KI)])
            wbfB.consumed(slot, "tM", t_mm)
            wx_last = t_mm
            wait("scalar", "tM", t_mm)
            if m < KI:
                if m == 0 and prev["wdt_last"] > 0:
                    wait("scalar", "tM", prev["wdt_last"])
                s_ev = mark("scalar", lambda e, m=m, b=bk: e.copy(
                    drw_sb[:, m * CH:(m + 1) * CH], ps[b][:, 0:CH]), sM)
            else:
                s_ev = mark("scalar", lambda e, b=bk: e.copy(
                    bcstg[:, 0:CH], ps[b][0:32, 0:CH]), sM)
                wait("gpsimd", "sM", s_ev)
                dmaG(bass.AP(xbcSend, DI * CH, [[CH, 32], [1, CH]]), bcstg[:, 0:CH])
            bank_ev[bk] = ("sM", s_ev)
        for _ in gen0:
            pass
        prev["wx_last"] = wx_last
        drw_ready = sM.v
        wait("gpsimd", "dGP", dGP.v)
        if j >= 2:
            wait("gpsimd", "dSY", prev["recv_last"].get(j - 2, 0))
            wait("gpsimd", "dSC", prev["bex_round_last"].get(j - 2, 0))
        ag_mark[("x", j)] = mark("gpsimd", lambda e, jb=jb: e.collective_compute(
            "AllGather", ALU.bypass, ins=[xbcSend.ap()], outs=[xbcAGo[jb].ap()],
            replica_groups=G4), cc)
        quiesce(tier=2)

        gen1 = emit_scan_half_gen(j - 1, 1) if j >= 1 else iter(())
        wdt_last = 0
        for m in range(KI):
            next(gen1, None)
            next(gen1, None)
            wsl, wready, slot = w_stream(j, WdtT.ap().tensor, wbf_dt, DI, m, 128, KI)
            bk = m % 2
            te_take(bk)
            wait("tensor", wready[0], wready[1])
            wait("tensor", "sM", drw_ready)
            t_mm = matgroup([
                (lambda e, k=k, w=wsl, b=bk: e.matmul(
                    ps[b][:, 0:CH], w[:, k * 128:(k + 1) * 128],
                    drw_sb[:, k * CH:(k + 1) * CH], start=(k == 0), stop=(k == KI - 1)))
                for k in range(KI)])
            wbfB.consumed(slot, "tM", t_mm)
            wdt_last = t_mm
            wait("scalar", "tM", t_mm)
            esl = m % 2
            stg2B.acquire("scalar", esl)
            put("scalar", lambda e, b=bk, es=esl, m=m: e.activation(
                stg2_sb[:, es * CH:(es + 1) * CH], ps[b][:, 0:CH], AF.Exp,
                bias=bdt_sb[:, m:m + 1]))
            s_ev = mark("scalar", lambda e, es=esl: e.activation(
                stg2_sb[:, es * CH:(es + 1) * CH], stg2_sb[:, es * CH:(es + 1) * CH],
                AF.Ln, bias=1.0), sM)
            bank_ev[bk] = ("sM", s_ev)
            wait("gpsimd", "sM", s_ev)
            if m == 0 and j >= 1:
                wait("gpsimd", "cc", ag_mark[("d", j - 1)])
            g_d = dmaG(bass.AP(dSend, m * 128 * CH, [[CH, 128], [1, CH]]),
                       stg2_sb[:, esl * CH:(esl + 1) * CH])
            stg2B.consumed(esl, "dGP", g_d)
        prev["wdt_last"] = wdt_last
        wait("gpsimd", "dGP", dGP.v)
        if j >= 2:
            wait("gpsimd", "dSY", prev["recv_last"].get(j - 2, 0))
        ag_mark[("d", j)] = mark("gpsimd", lambda e, jb=jb: e.collective_compute(
            "AllGather", ALU.bypass, ins=[dSend.ap()], outs=[dAGo[jb].ap()],
            replica_groups=G4), cc)

        if debug and j == 0:
            wait("gpsimd", "cc", ag_mark[("d", 0)])
            dmaG(dbg["dsend0"].ap(), dSend.ap())
            dmaG(dbg["xbc0"].ap(), xbcSend.ap())

        quiesce(tier=2)

        for _ in gen1:
            pass
        if j >= 2:
            emit_gating(j - 2)
        quiesce(tier=2)

    emit_recv(RNDS - 1)
    genp0 = emit_scan_half_gen(RNDS - 1, 0)
    genp1 = emit_scan_half_gen(RNDS - 1, 1)
    quiesce()

    wait("vector", "vM", prev["v_xm"])
    if debug:
        put("vector", lambda e: e.tensor_copy(dbgx_sb[:, 4:6], xmax_sb[:, 0:2]))
        put("vector", lambda e: e.tensor_copy(dbgx_sb[:, 8:16], qs_sb[:, 16:24]))
    v_fl = mark("vector", lambda e: e.tensor_copy(qs_sb[:, 23:24], xmax_sb[:, 0:1]), vM)
    wait("sync", "vM", v_fl)
    w_flip = dmaS(flipD.ap(), qs_sb[:, 23:24])
    wait("sync", "dSY", w_flip)
    fl2 = dmaS(srow[0:1, 0:128], flipD.ap()[None, :])
    wait("vector", "dSY", fl2)
    v_m1 = mark("vector", lambda e: e.tensor_reduce(
        r16[0:1, 0:1], srow[0:1, 0:128], axis=AX.X, op=ALU.max), vM)
    wait("sync", "vM", v_m1)
    ar1_in = dmaS(mx_in.ap()[None, :], r16[0:1, 0:16])
    wait("gpsimd", "dSY", ar1_in)
    cc_ar1 = mark("gpsimd", lambda e: e.collective_compute(
        "AllReduce", ALU.max, ins=[mx_in.ap()], outs=[mx1_out.ap()],
        replica_groups=G8), cc)
    wait("sync", "cc", cc_ar1)
    if debug:
        wait("gpsimd", "cc", cc_ar1)
        dmaG(bass.AP(dbg["ar1"].ap().tensor, 0, [[1, 128]]), flipD.ap())
        dmaG(bass.AP(dbg["ar1"].ap().tensor, 128, [[1, 16]]), mx_in.ap())
        dmaG(bass.AP(dbg["ar1"].ap().tensor, 144, [[1, 16]]), mx1_out.ap())
    m1_ld = dmaS(r16[0:1, 0:16], mx1_out.ap()[None, :])
    wait("vector", "dSY", m1_ld)
    put("vector", lambda e: e.tensor_scalar_max(r16[0:1, 0:1], r16[0:1, 0:1], 1e-8))
    put("vector", lambda e: e.tensor_scalar_mul(r16[0:1, 1:2], r16[0:1, 0:1], 1.0 / 255.0))
    v_hz = mark("vector", lambda e: e.reciprocal(rcp1_sb[:], r16[0:1, 0:1]), vM)
    wait("vector", "vM", v_hz)
    v_s1 = mark("vector", lambda e: e.tensor_scalar_mul(r16[0:1, 2:3], rcp1_sb[:],
                                                        255.0), vM)
    wait("sync", "vM", v_s1)
    w_scl = dmaS(bass.AP(sclD, 0, [[4, 1], [1, 4]]), r16[0:1, 0:4])
    wait("sync", "dSY", w_scl)
    s1b = dmaS(scl_sb[:, 0:1], bass.AP(sclD, 2, [[0, 128], [1, 1]]))
    s1c = dmaS(scl_sb[:, 2:3], bass.AP(sclD, 1, [[0, 128], [1, 1]]))
    wait("vector", "dSY", s1c)
    v_hz = mark("vector", lambda e: e.tensor_scalar(scale1_sb[:], sw1_sb[:],
                                                    scl_sb[:, 2:3], None, ALU.mult), vM)
    wait("vector", "vM", v_hz)
    v_hz = mark("vector", lambda e: e.reciprocal(bq1_sb[:], scale1_sb[:]), vM)
    wait("vector", "vM", v_hz)
    v_bq = mark("vector", lambda e: e.tensor_mul(bq1_sb[:], b1_sb[:], bq1_sb[:]), vM)
    wait("vector", "vM", v_bq)
    wait("scalar", "vM", v_bq)
    s_hz2 = mark("scalar", lambda e: e.copy(wqi_sb[:, 0:MF], bq1_sb[:]), sM)
    wait("scalar", "sM", s_hz2)
    s_bq = mark("scalar", lambda e: e.copy(bq1_sb[:], wqi_sb[:, 0:MF]), sM)
    wait("vector", "sM", s_bq)
    v_bq1 = mark("vector", lambda e: e.tensor_mul(bq1_sb[:], bq1_sb[:], scale1_sb[:]), vM)

    quiesce()

    def xq_tile(idx):
        base = xcv_sb if idx < KI else drw_sb
        i = idx % KI
        return base[:, i * CH:(i + 1) * CH]

    wait("scalar", "tM", prev["wx_last"])
    wait("scalar", "tM", prev["wdt_last"])
    wait("scalar", "dGP", xcsend_mark[RNDS - 1])
    wait("scalar", "dSY", s1b)
    wait("sync", "dGP", prev["xn_spill"])
    wait("sync", "cc", cc_wq)
    xq_last = 0
    for j in range(RNDS):
        for k in range(KD):
            xsl = k % 2
            stg2B.acquire("sync", xsl)
            ldxn = dmaS(stg2_sb[:, xsl * CH:(xsl + 1) * CH],
                        bass.AP(xn_dram, j * DM * CHH + k * 128 * CHH + HALO,
                                [[CHH, 128], [1, CH]]))
            wait("scalar", "dSY", ldxn)
            iqB.acquire("scalar", xsl)
            s_hz = mark("scalar", lambda e, xs=xsl: e.activation(
                iq_sb[:, xs * CH:(xs + 1) * CH], stg2_sb[:, xs * CH:(xs + 1) * CH],
                AF.Copy, scale=scl_sb[:, 0:1]), sM)
            wait("scalar", "sM", s_hz)
            s_i = mark("scalar", lambda e, j=j, k=k, xs=xsl: e.copy(
                xq_tile(j * KD + k), iq_sb[:, xs * CH:(xs + 1) * CH]), sM)
            stg2B.consumed(xsl, "sM", s_i)
            wait("vector", "sM", s_i)
            xq_last = mark("vector", lambda e, j=j, k=k: e.tensor_scalar_max(
                xq_tile(j * KD + k), xq_tile(j * KD + k), 0.0), vM)

    quiesce()

    wait("tensor", "cc", cc_wq)
    wait("tensor", "vM", xq_last)
    wait("scalar", "vM", v_bq1)
    hmax_v = 0
    for m in range(MF):
        if m < 17:
            next(genp0, None)
        else:
            next(genp1, None)
            next(genp1, None)
        sl = m % 2
        wbfB.acquire("sync", sl)
        lc1, lv1 = dma_slot("sync", wbfB, sl, wbf[:, sl * 2048:sl * 2048 + KD * 128],
                            bass.AP(wq_out, m * 128, [[4096, 128], [WQBLK, KD], [1, 128]]))
        wait("tensor", lc1, lv1)
        t_grp = 0
        for j in range(RNDS):
            bk = (m * RNDS + j) % 2
            te_take(bk)
            t_grp = matgroup([
                (lambda e, k=k, j=j, b=bk, s=sl: e.matmul(
                    ps[b][:, 0:CH], wbf[:, s * 2048 + k * 128:s * 2048 + (k + 1) * 128],
                    xq_tile(j * KD + k), start=(k == 0), stop=(k == KD - 1)))
                for k in range(KD)])
            wait("scalar", "tM", t_grp)
            hsl = (m * RNDS + j) % 2
            stgB.acquire("scalar", hsl)
            s_h = mark("scalar", lambda e, b=bk, hs=hsl, m=m: e.activation(
                stg_sb[:, hs * CH:(hs + 1) * CH], ps[b][:, 0:CH], AF.Relu,
                bias=bq1_sb[:, m:m + 1], scale=scale1_sb[:, m:m + 1]), sM)
            bank_ev[bk] = ("sM", s_h)
            wait("gpsimd", "sM", s_h)
            g_h = dmaG(bass.AP(h_dram, (m * 128) * TLOC + j * CH, [[TLOC, 128], [1, CH]]),
                       stg_sb[:, hsl * CH:(hsl + 1) * CH])
            stgB.consumed(hsl, "dGP", g_h)
            wait("vector", "sM", s_h)
            v_hz = mark("vector", lambda e, hs=hsl: e.tensor_reduce(
                qs_sb[:, 22:23], stg_sb[:, hs * CH:(hs + 1) * CH], axis=AX.X,
                op=ALU.max), vM)
            wait("vector", "vM", v_hz)
            hmax_v = mark("vector", lambda e: e.tensor_max(
                xmax_sb[:, 1:2], xmax_sb[:, 1:2], qs_sb[:, 22:23]), vM)
            stgB.consumed(hsl, "vM", hmax_v)
        wbfB.consumed(sl, "tM", t_grp)

    for _ in genp0:
        pass
    for _ in genp1:
        pass
    quiesce()
    h_done = dGP.v

    if RNDS >= 2:
        emit_gating(RNDS - 2)

    quiesce()

    wait("vector", "vM", hmax_v)
    put("vector", lambda e: e.tensor_scalar_max(xmax_sb[:, 1:2], xmax_sb[:, 1:2], 0.0))
    v_fl2 = mark("vector", lambda e: e.tensor_copy(qs_sb[:, 23:24], xmax_sb[:, 1:2]), vM)
    wait("sync", "vM", v_fl2)
    w_flip2 = dmaS(flipD.ap(), qs_sb[:, 23:24])
    wait("sync", "dSY", w_flip2)
    fl3 = dmaS(srow[0:1, 0:128], flipD.ap()[None, :])
    wait("vector", "dSY", fl3)
    v_m2 = mark("vector", lambda e: e.tensor_reduce(
        r16[0:1, 0:1], srow[0:1, 0:128], axis=AX.X, op=ALU.max), vM)
    wait("sync", "vM", v_m2)
    ar2_in = dmaS(mx_in.ap()[None, :], r16[0:1, 0:16])
    wait("gpsimd", "dSY", ar2_in)
    cc_ar2 = mark("gpsimd", lambda e: e.collective_compute(
        "AllReduce", ALU.max, ins=[mx_in.ap()], outs=[mx2_out.ap()],
        replica_groups=G8), cc)
    wait("sync", "cc", cc_ar2)
    m2_ld = dmaS(r16[0:1, 0:16], mx2_out.ap()[None, :])
    wait("vector", "dSY", m2_ld)
    put("vector", lambda e: e.tensor_scalar_max(r16[0:1, 0:1], r16[0:1, 0:1], 1e-8))
    put("vector", lambda e: e.tensor_scalar_mul(r16[0:1, 1:2], r16[0:1, 0:1], 1.0 / 255.0))
    v_hz = mark("vector", lambda e: e.reciprocal(rcp1_sb[:], r16[0:1, 0:1]), vM)
    wait("vector", "vM", v_hz)
    v_s2 = mark("vector", lambda e: e.tensor_scalar_mul(r16[0:1, 2:3], rcp1_sb[:],
                                                        255.0), vM)
    wait("sync", "vM", v_s2)
    w_scl2 = dmaS(bass.AP(sclD, 8, [[4, 1], [1, 4]]), r16[0:1, 0:4])
    wait("sync", "dSY", w_scl2)
    s2b = dmaS(scl_sb[:, 1:2], bass.AP(sclD, 10, [[0, 128], [1, 1]]))
    s2c = dmaS(scl_sb[:, 3:4], bass.AP(sclD, 9, [[0, 128], [1, 1]]))
    wait("vector", "dSY", s2c)
    v_hz = mark("vector", lambda e: e.tensor_scalar(scale2_sb[:], sw2_sb[:],
                                                    scl_sb[:, 3:4], None, ALU.mult), vM)
    wait("vector", "vM", v_hz)
    v_hz = mark("vector", lambda e: e.reciprocal(bq2_sb[:], scale2_sb[:]), vM)
    wait("vector", "vM", v_hz)
    v_bq2a = mark("vector", lambda e: e.tensor_mul(bq2_sb[:], b2_sb[:], bq2_sb[:]), vM)
    wait("vector", "vM", v_bq2a)
    wait("scalar", "vM", v_bq2a)
    s_hz2 = mark("scalar", lambda e: e.copy(wqi_sb[:, 0:KD], bq2_sb[:]), sM)
    wait("scalar", "sM", s_hz2)
    s_bq2 = mark("scalar", lambda e: e.copy(bq2_sb[:], wqi_sb[:, 0:KD]), sM)
    wait("vector", "sM", s_bq2)
    v_bq2 = mark("vector", lambda e: e.tensor_mul(bq2_sb[:], bq2_sb[:], scale2_sb[:]), vM)

    def hq_tile(idx):
        base = dR_sb if idx < KI else xcR_sb
        i = idx % KI
        return base[:, i * CH:(i + 1) * CH]

    quiesce()
    last_cm = cm_last[(RNDS - 1, 1, DST - 1)]
    last_yf = yfin_last[(RNDS - 1, 1)]
    wait("scalar", "vM", max(last_cm, last_yf))
    wait("scalar", "vM", v_bq2)
    wait("scalar", "dSY", s2b)
    wait("sync", "dGP", h_done)
    for j in range(RNDS):
        hq_last = 0
        for k in range(KH):
            hsl = k % 2
            stg2B.acquire("sync", hsl)
            ldh = dmaS(stg2_sb[:, hsl * CH:(hsl + 1) * CH],
                       bass.AP(h_dram, (k * 128) * TLOC + j * CH, [[TLOC, 128], [1, CH]]))
            wait("scalar", "dSY", ldh)
            iqB.acquire("scalar", hsl)
            s_hz = mark("scalar", lambda e, hs=hsl: e.activation(
                iq_sb[:, hs * CH:(hs + 1) * CH], stg2_sb[:, hs * CH:(hs + 1) * CH],
                AF.Copy, scale=scl_sb[:, 1:2]), sM)
            wait("scalar", "sM", s_hz)
            s_hq = mark("scalar", lambda e, k=k, hs=hsl: e.copy(
                hq_tile(k), iq_sb[:, hs * CH:(hs + 1) * CH]), sM)
            stg2B.consumed(hsl, "sM", s_hq)
            wait("vector", "sM", s_hq)
            hq_last = mark("vector", lambda e, k=k: e.tensor_scalar_max(
                hq_tile(k), hq_tile(k), 0.0), vM)
        if debug and j == 0:
            wait("gpsimd", "vM", hq_last)
            dmaG(bass.AP(dbg["hq0"].ap().tensor, 0, [[4 * CH, 128], [1, 4 * CH]]),
                 dR_sb[:, 0:4 * CH])
        wait("tensor", "vM", hq_last)
        for m in range(KD):
            sl = m % 2
            wbfB.acquire("sync", 0)
            wbfB.acquire("sync", 1)
            for r in range(8):
                lc2, lv2 = dma_slot("sync", wbfB, 0, wbf[:, r * 512:(r + 1) * 512],
                                    bass.AP(wq_out, r * WQBLK + 128 * 4096 + m * 128,
                                            [[DM, 128], [128 * DM, 4], [1, 128]]))
            te_take(m)
            wait("tensor", lc2, lv2)
            t_mm = matgroup([
                (lambda e, k=k, m=m: e.matmul(
                    ps[m][:, 0:CH], wbf[:, k * 128:(k + 1) * 128],
                    hq_tile(k), start=(k == 0), stop=(k == KH - 1)))
                for k in range(KH)])
            wbfB.consumed(0, "tM", t_mm)
            wbfB.consumed(1, "tM", t_mm)
            wait("scalar", "tM", t_mm)
            gsl = m % 2
            stgB.acquire("scalar", gsl)
            s_g = mark("scalar", lambda e, m=m, gs=gsl: e.activation(
                stg_sb[:, gs * CH:(gs + 1) * CH], ps[m][:, 0:CH], AF.Silu,
                bias=bq2_sb[:, m:m + 1], scale=scale2_sb[:, m:m + 1]), sM)
            bank_ev[m] = ("sM", s_g)
            wait("gpsimd", "sM", s_g)
            g_g = dmaG(bass.AP(g_dram, (m * 128) * TLOC + j * CH, [[TLOC, 128], [1, CH]]),
                       stg_sb[:, gsl * CH:(gsl + 1) * CH])
            stgB.consumed(gsl, "dGP", g_g)
    g_done = dGP.v

    quiesce()

    emit_gating(RNDS - 1)
    quiesce()
    s_done = dGP.v

    wait("sync", "dGP", max(g_done, s_done))
    out_last = 0
    for j in range(RNDS):
        for kt in range(KD):
            osl = kt % 2
            xroB.acquire("sync", 0)
            ldx = dmaS(xro_sb[:],
                       x_fm.ap()[kt * 128:(kt + 1) * 128, j * CHH + HALO:(j + 1) * CHH])
            stgB.acquire("sync", osl)
            lds = dmaS(stg_sb[:, osl * CH:(osl + 1) * CH],
                       bass.AP(s_dram, (j * DM + kt * 128) * CH, [[CH, 128], [1, CH]]))
            stg2B.acquire("sync", osl)
            ldg = dmaS(stg2_sb[:, osl * CH:(osl + 1) * CH],
                       bass.AP(g_dram, (kt * 128) * TLOC + j * CH, [[TLOC, 128], [1, CH]]))
            wait("vector", "dSY", ldg)
            v_p = mark("vector", lambda e, os=osl: e.tensor_mul(
                stg_sb[:, os * CH:(os + 1) * CH], stg_sb[:, os * CH:(os + 1) * CH],
                stg2_sb[:, os * CH:(os + 1) * CH]), vM)
            stg2B.consumed(osl, "vM", v_p)
            outB.acquire("vector", 0)
            v_o = mark("vector", lambda e, os=osl: e.tensor_add(
                out_sb[:], xro_sb[:],
                stg_sb[:, os * CH:(os + 1) * CH]), vM)
            stgB.consumed(osl, "vM", v_o)
            xroB.consumed(0, "vM", v_o)
            wait("sync", "vM", v_o)
            out_last = dmaS(out_fm.ap()[kt * 128:(kt + 1) * 128, j * CH:(j + 1) * CH],
                            out_sb[:])
            outB.consumed(0, "dSY", out_last)

    if debug:
        wait("gpsimd", "dGP", dGP.v)
        wait("gpsimd", "dSY", out_last)
        dmaG(dbg["y0"].ap(), ySend.ap())
        dmaG(dbg["h0"].ap(), h_dram.ap())
        dmaG(dbg["g0"].ap(), g_dram.ap())
        dmaG(dbg["s0"].ap(), s_dram.ap())
        dmaG(dbg["xn0"].ap(), xn_dram.ap())
        dmaG(bass.AP(dbg["sc"].ap().tensor, 0, [[1, 4]]), bass.AP(sclD, 0, [[1, 4]]))
        dmaG(bass.AP(dbg["sc"].ap().tensor, 8, [[1, 4]]), bass.AP(sclD, 8, [[1, 4]]))
        dmaG(dbg["wq"].ap(), wq_out.ap())
        dmaG(bass.AP(dbg["carry"].ap().tensor, 0, [[NDT * DST, 128], [1, NDT * DST]]),
             carry[:])
        dmaG(bass.AP(dbg["xm"].ap().tensor, 0, [[16, 128], [1, 16]]), dbgx_sb[:])
        dmaG(dbg["wqin"].ap(), wq_in.ap())
        dmaG(dbg["swq"].ap(), swq_out.ap())
        dmaG(bass.AP(dbg["xq"].ap().tensor, 0, [[KI * CH, 128], [1, KI * CH]]), xcv_sb[:])
        dmaG(bass.AP(dbg["xq"].ap().tensor, DI * CH, [[KI * CH, 128], [1, KI * CH]]), drw_sb[:])
        dmaG(bass.AP(dbg["q1"].ap().tensor, 0, [[2 * MF, 128], [1, MF]]), scale1_sb[:])
        dmaG(bass.AP(dbg["q1"].ap().tensor, MF, [[2 * MF, 128], [1, MF]]), bq1_sb[:])
        dmaG(bass.AP(dbg["q2"].ap().tensor, 0, [[2 * KD, 128], [1, KD]]), scale2_sb[:])
        dmaG(bass.AP(dbg["q2"].ap().tensor, KD, [[2 * KD, 128], [1, KD]]), bq2_sb[:])


    wait("sync", "dSY", out_last)
    wait("gpsimd", "dSY", out_last)
    wait("gpsimd", "dGP", dGP.v)

    if vec_limit is not None:
        if isinstance(vec_limit, dict):
            for k, n in vec_limit.items():
                plan[k] = plan[k][:n]
        else:
            plan["vector"] = plan["vector"][:vec_limit]

    with nc.Block() as block:
        @block.sync
        def _(e):
            for f in plan["sync"]:
                f(e)

        @block.tensor
        def _(e):
            for f in plan["tensor"]:
                f(e)

        @block.vector
        def _(e):
            for f in plan["vector"]:
                f(e)

        @block.scalar
        def _(e):
            for f in plan["scalar"]:
                f(e)

        @block.gpsimd
        def _(e):
            for f in plan["gpsimd"]:
                f(e)

    stack.close()
    return nc


def _layout_inputs(inputs, T_total, CH):
    NCHUNK = T_total // CH
    RNDS = NCHUNK // GRP
    CHH = CH + HALO
    x = np.asarray(inputs["L0_attn_x"], np.float32)
    WinT = np.ascontiguousarray(np.asarray(inputs["L0_attn_W_in"], np.float32).T)
    WxT = np.ascontiguousarray(np.asarray(inputs["L0_attn_W_x"], np.float32).T)
    WdtT = np.ascontiguousarray(np.asarray(inputs["L0_attn_W_dt"], np.float32).T)
    WoutT = np.ascontiguousarray(np.asarray(inputs["L0_attn_W_out"], np.float32).T)
    w1 = np.asarray(inputs["L0_ffn_w1"], np.float32)
    w2 = np.asarray(inputs["L0_ffn_w2"], np.float32)
    w1T = np.ascontiguousarray(w1.T)
    w2T = np.ascontiguousarray(w2.T)
    bconv = np.ascontiguousarray(
        np.asarray(inputs["L0_attn_b_conv"], np.float32).reshape(KI, 128).T)
    bdt = np.ascontiguousarray(
        np.asarray(inputs["L0_attn_b_dt"], np.float32).reshape(KI, 128).T)
    wcv = np.ascontiguousarray(
        np.asarray(inputs["L0_attn_w_conv"], np.float32).reshape(KI, 128, 4)
        .transpose(1, 0, 2).reshape(128, KI * 4))
    Alog = np.asarray(inputs["L0_attn_A_log"], np.float32)
    Dv = np.asarray(inputs["L0_attn_D"], np.float32)
    b1 = np.ascontiguousarray(np.asarray(inputs["L0_ffn_b1"], np.float32).reshape(MF, 128).T)
    b2 = np.ascontiguousarray(np.asarray(inputs["L0_ffn_b2"], np.float32).reshape(KD, 128).T)

    xp = np.pad(x, ((0, 0), (HALO, 0), (0, 0)))
    DQ = DI // GRP
    NDT = DQ // 128
    in_maps = []
    for c in range(8):
        b, q = divmod(c, GRP)
        cols = []
        for j in range(RNDS):
            g = q + GRP * j
            cols.append(xp[b, g * CH:g * CH + CHH, :].T)
        x_c = np.ascontiguousarray(np.concatenate(cols, axis=1), np.float32)
        dq0 = q * DQ
        Alog_my = np.ascontiguousarray(
            Alog[dq0:dq0 + DQ].reshape(NDT, 128, DST).transpose(1, 0, 2)
            .reshape(128, NDT * DST))
        D_my = np.ascontiguousarray(Dv[dq0:dq0 + DQ].reshape(NDT, 128).T)
        in_maps.append({
            "x_fm": x_c,
            "WinT": WinT, "WxT": WxT, "WdtT": WdtT, "WoutT": WoutT,
            "w1T_myk": np.ascontiguousarray(w1T[c * 128:(c + 1) * 128]),
            "w2T_myk": np.ascontiguousarray(w2T[c * 512:(c + 1) * 512]),
            "w1_rows": np.ascontiguousarray(w1[c * 512:(c + 1) * 512]),
            "w2_rows": np.ascontiguousarray(w2[c * 128:(c + 1) * 128]),
            "bconv_t": bconv, "bdt_t": bdt, "wcv_t": wcv,
            "Alog_my": Alog_my, "D_my": D_my,
            "b1_t": b1, "b2_t": b2,
            "rank_in": np.array([[q, q * CH, q * (DI // GRP // 128), 0]], np.int32),
        })
    return in_maps


def kernel(**inputs):
    T_total = int(np.asarray(inputs["L0_attn_x"]).shape[1])
    CH = 512 if T_total >= GRP * 512 else T_total // (GRP * 4)
    nc = build_program(T_total=T_total, CH=CH)
    in_maps = _layout_inputs(inputs, T_total, CH)
    res = run_bass_kernel_spmd(nc, in_maps, core_ids=list(range(8)))
    RNDS = T_total // CH // GRP
    out = np.empty((BATCH, T_total, DM), np.float32)
    for c in range(8):
        b, q = divmod(c, GRP)
        o = np.asarray(res.results[c]["out_fm"])
        for j in range(RNDS):
            g = q + GRP * j
            out[b, g * CH:(g + 1) * CH, :] = o[:, j * CH:(j + 1) * CH].T
    return out


# version 12
# speedup vs baseline: 1.0132x; 1.0012x over previous
import contextlib
import numpy as np
import concourse.bass as bass
import concourse.mybir as mybir
from concourse.bass_utils import run_bass_kernel_spmd

F32 = mybir.dt.float32
BF16 = mybir.dt.bfloat16
I32 = mybir.dt.int32
AF = mybir.ActivationFunctionType
ALU = mybir.AluOpType
AX = mybir.AxisListType

BATCH = 2
DM = 1024
DI = 2048
DST = 16
GRP = 4
KD = DM // 128
KI = DI // 128
MI = 2 * DI // 128
MF = 4096 // 128
KH = 4096 // 128
HALO = 3
DXW = DI + 2 * DST
DXP = DI + 128


class Ctr:
    def __init__(self, h):
        self.h = h
        self.v = 0


def build_program(T_total=8192, CH=512, debug=False, vec_limit=None, bar_tier=0):
    NCHUNK = T_total // CH
    RNDS = NCHUNK // GRP
    RT = GRP * CH
    SCH = RT // 2
    TLOC = RNDS * CH
    CHH = CH + HALO
    DQ = DI // GRP
    NDT = DQ // 128
    NSLOT = 3
    assert CH <= 512

    nc = bass.Bass("TRN2", target_bir_lowering=False,
                   detect_race_conditions=False, dynamic_dma_scratch_size=16384)

    def EIN(name, shape, dt=F32):
        return nc.dram_tensor(name, shape, dt, kind="ExternalInput")

    x_fm = EIN("x_fm", [DM, RNDS * CHH])
    WinT = EIN("WinT", [DM, 2 * DI])
    WxT = EIN("WxT", [DI, DXW])
    WdtT = EIN("WdtT", [DI, DI])
    WoutT = EIN("WoutT", [DI, DM])
    w1T_myk = EIN("w1T_myk", [128, 4096])
    w2T_myk = EIN("w2T_myk", [512, DM])
    w1_rows = EIN("w1_rows", [512, DM])
    w2_rows = EIN("w2_rows", [128, 4096])
    bconv_t = EIN("bconv_t", [128, KI])
    bdt_t = EIN("bdt_t", [128, KI])
    wcv_t = EIN("wcv_t", [128, KI * 4])
    Alog_my = EIN("Alog_my", [128, NDT * DST])
    D_my = EIN("D_my", [128, NDT])
    b1_t = EIN("b1_t", [128, MF])
    b2_t = EIN("b2_t", [128, KD])
    rank_in = EIN("rank_in", [1, 4], I32)

    out_fm = nc.dram_tensor("out_fm", [DM, TLOC], F32, kind="ExternalOutput")

    dbg = {}
    if debug:
        for nm, n, dt in [("dsend0", DI * CH, BF16), ("xbc0", DXP * CH, BF16),
                          ("y0", DQ * RT, BF16), ("sc", 64, F32),
                          ("h0", 4096 * TLOC, BF16), ("g0", DM * TLOC, BF16),
                          ("s0", RNDS * DM * CH, BF16), ("xn0", RNDS * DM * CHH, BF16),
                          ("wq", 8 * (128 * 4096 + 512 * DM), BF16), ("xq", 2 * DI * CH, BF16),
                          ("q1", 128 * 2 * MF, F32),
                          ("wqin", 128 * 4096 + 512 * DM, BF16),
                          ("swq", 8 * 1280, F32), ("ar1", 160, F32),
                          ("xm", 128 * 16, F32), ("hq0", 128 * 512, BF16),
                          ("q2", 128 * 2 * KD, F32),
                          ("carry", 128 * NDT * DST, F32)]:
            dbg[nm] = nc.dram_tensor("dbg_" + nm, [n], dt, kind="ExternalOutput")

    wbf_in = nc.dram_tensor("wbf_in", [DM * 2 * DI], BF16)
    wbf_x = nc.dram_tensor("wbf_x", [DI * DXW], BF16)
    wbf_dt = nc.dram_tensor("wbf_dt", [DI * DI], BF16)
    wbf_out = nc.dram_tensor("wbf_out", [DI * DM], BF16)
    z_dram = nc.dram_tensor("z_dram", [RNDS * DI * CH], BF16)
    xn_dram = nc.dram_tensor("xn_dram", [RNDS * DM * CHH], BF16)
    h_dram = nc.dram_tensor("h_dram", [4096 * TLOC], BF16)
    g_dram = nc.dram_tensor("g_dram", [DM * TLOC], BF16)
    s_dram = nc.dram_tensor("s_dram", [RNDS * DM * CH], BF16)
    rowD = nc.dram_tensor("rowD", [RNDS * 2 * CHH], F32)
    sclD = nc.dram_tensor("sclD", [64], F32)
    swq_in = nc.dram_tensor("swq_in", [1280], F32)
    swq_out = nc.dram_tensor("swq_out", [8 * 1280], F32)
    WQBLK = 128 * 4096 + 512 * DM
    wq_in = nc.dram_tensor("wq_in", [WQBLK], BF16)
    wq_out = nc.dram_tensor("wq_out", [8 * WQBLK], BF16)
    mx_in = nc.dram_tensor("mx_in", [16], F32)
    mx1_out = nc.dram_tensor("mx1_out", [16], F32)
    mx2_out = nc.dram_tensor("mx2_out", [16], F32)
    flipD = nc.dram_tensor("flipD", [128], F32)
    dSend = nc.dram_tensor("dSend", [DI * CH], BF16)
    xbcSend = nc.dram_tensor("xbcSend", [DXP * CH], BF16)
    ySend = nc.dram_tensor("ySend", [DQ * RT], BF16)
    dAGo = [nc.dram_tensor(f"dAGo{i}", [GRP * DI * CH], BF16) for i in range(2)]
    xbcAGo = [nc.dram_tensor(f"xbcAGo{i}", [GRP * DXP * CH], BF16) for i in range(2)]
    yAGo = [nc.dram_tensor(f"yAGo{i}", [GRP * DQ * RT], BF16) for i in range(2)]

    G4 = [[0, 1, 2, 3], [4, 5, 6, 7]]
    G8 = [[0, 1, 2, 3, 4, 5, 6, 7]]

    plan = {k: [] for k in ("sync", "tensor", "vector", "scalar", "gpsimd")}

    def put(eng, fn):
        plan[eng].append(fn)

    def mark(eng, fn, ctr, n=1):
        ctr.v += n
        plan[eng].append(lambda e, f=fn, c=ctr, m=n: f(e).then_inc(c.h, m))
        return ctr.v

    stack = contextlib.ExitStack()

    def sbuf(name, shape, dt):
        return stack.enter_context(nc.sbuf_tensor(name, shape, dt))

    def mksem(name):
        return Ctr(stack.enter_context(nc.semaphore(name)))

    dSY = Ctr(None)
    dGP = Ctr(None)
    dSC = Ctr(None)
    cc = mksem("cc")
    tM = mksem("tM")
    vM = mksem("vM")
    sM = mksem("sM")
    SEMS = {"dSY": dSY, "dGP": dGP, "dSC": dSC, "cc": cc, "tM": tM, "vM": vM, "sM": sM}

    RQ = 8

    class DmaTrack:
        def __init__(self, name, engname, ctr):
            self.eng = engname
            self.ctr = ctr
            self.S = [mksem(f"{name}S{r}").h for r in range(RQ)]
            self.F = mksem(f"{name}F").h
            self.n = 0
            self.slot_cum = []
            self.uses = [0] * RQ
            self.covered = 0
            self.fval = 0
            self.covers = []

    trS = DmaTrack("qS", "sync", dSY)
    trG = DmaTrack("qG", "gpsimd", dGP)
    trC = DmaTrack("qC", "scalar", dSC)
    TRACKS = {"dSY": trS, "dGP": trG, "dSC": trC}

    def ensure_relay(tr, idx):
        idx = min(idx, tr.n)
        if idx <= tr.covered:
            return
        targets = {}
        for j in range(tr.covered + 1, idx + 1):
            r, cnt = tr.slot_cum[j - 1]
            targets[r] = cnt
        for r, cnt in targets.items():
            plan[tr.eng].append(lambda e, s=tr.S[r], w=16 * cnt: e.wait_ge(s, w))
        tr.fval += 1
        plan[tr.eng].append(lambda e, s=tr.F: e.nop(nofuse=True).then_inc(s, 1))
        tr.covered = idx
        tr.covers.append((idx, tr.fval))

    def fval_for(tr, idx):
        ensure_relay(tr, idx)
        for hi, fv in tr.covers:
            if hi >= idx:
                return fv
        raise AssertionError("relay lookup failed")

    def dma_issue(tr, fn):
        if tr.n + 1 > RQ:
            ensure_relay(tr, tr.n + 1 - RQ)
        tr.n += 1
        r = (tr.n - 1) % RQ
        tr.uses[r] += 1
        tr.slot_cum.append((r, tr.uses[r]))
        plan[tr.eng].append(lambda e, f=fn, s=tr.S[r]: f(e).then_inc(s, 16))
        tr.ctr.v += 16
        return tr.ctr.v

    BAR = mksem("BAR")
    nbar = [0]
    ENGS = ("sync", "tensor", "vector", "scalar", "gpsimd")

    def quiesce(tier=1):
        if tier > bar_tier and tier >= 0:
            return
        for tr in (trS, trG, trC):
            ensure_relay(tr, tr.n)
        if cc.v > 0:
            plan["gpsimd"].append(lambda e, c=cc, w=cc.v: e.wait_ge(c.h, w))
        nbar[0] += 1
        tgt = 5 * nbar[0]
        for eng in ENGS:
            plan[eng].append(lambda e, s=BAR.h: e.nop(nofuse=True).then_inc(s, 1))
            plan[eng].append(lambda e, s=BAR.h, w=tgt: e.wait_ge(s, w))

    def wait(eng, sem, v=None):
        name = sem if isinstance(sem, str) else None
        ctr = SEMS[sem] if isinstance(sem, str) else sem
        vv = ctr.v if v is None else v
        if vv <= 0:
            return vv
        if name in TRACKS:
            tr = TRACKS[name]
            idx = (vv + 15) // 16
            fv = fval_for(tr, idx)
            plan[eng].append(lambda e, s=tr.F, w=fv: e.wait_ge(s, w))
        else:
            plan[eng].append(lambda e, c=ctr, w=vv: e.wait_ge(c.h, w))
        return vv

    onesc_b = sbuf("onesc_b", [128, 1], BF16)
    rank_sb = sbuf("rank_sb", [1, 4], I32)
    bconv_sb = sbuf("bconv_sb", [128, KI], F32)
    bdt_sb = sbuf("bdt_sb", [128, KI], F32)
    wcv_sb = sbuf("wcv_sb", [128, KI * 4], F32)
    A_sb = sbuf("A_sb", [128, NDT * DST], F32)
    D_sb = sbuf("D_sb", [128, NDT], F32)
    b1_sb = sbuf("b1_sb", [128, MF], F32)
    b2_sb = sbuf("b2_sb", [128, KD], F32)
    sw1_sb = sbuf("sw1_sb", [128, MF], F32)
    sw2_sb = sbuf("sw2_sb", [128, KD], F32)
    scale1_sb = sbuf("scale1_sb", [128, MF], F32)
    bq1_sb = sbuf("bq1_sb", [128, MF], F32)
    scale2_sb = sbuf("scale2_sb", [128, KD], F32)
    bq2_sb = sbuf("bq2_sb", [128, KD], F32)
    scl_sb = sbuf("scl_sb", [128, 4], F32)
    qs_sb = sbuf("qs_sb", [128, 24], F32)
    srow = sbuf("srow", [1, 6 * CHH], F32)
    carry = sbuf("carry", [128, NDT * DST], F32)
    xmax_sb = sbuf("xmax_sb", [128, 2], F32)
    r16 = sbuf("r16", [1, 16], F32)
    rcp5_sb = sbuf("rcp5_sb", [128, 5], F32)
    rcp1_sb = sbuf("rcp1_sb", [1, 1], F32)
    dbgx_sb = sbuf("dbgx_sb", [128, 16], F32)

    xf32 = sbuf("xf32", [128, 2 * CHH], F32)
    xsqb = sbuf("xsqb", [128, 2 * CHH], BF16)
    xb_sb = sbuf("xb_sb", [128, KD * CHH], BF16)
    xn_sb = sbuf("xn_sb", [128, KD * CHH], BF16)
    rbc_sb = sbuf("rbc_sb", [128, CHH], BF16)
    mbc_sb = sbuf("mbc_sb", [128, CHH], BF16)
    WF = max(2 * 2048, 2 * SCH)
    wf32 = sbuf("wf32", [128, WF], F32)
    wbf = sbuf("wbf", [128, 2 * 2048], BF16)
    xin_sb = sbuf("xin_sb", [128, 3 * CHH], BF16)
    xcv_sb = sbuf("xcv_sb", [128, KI * CH], BF16)
    drw_sb = sbuf("drw_sb", [128, KI * CH], BF16)
    stg_sb = sbuf("stg_sb", [128, 2 * CH], BF16)
    stg2_sb = sbuf("stg2_sb", [128, 2 * CH], BF16)
    bcstg = sbuf("bcstg", [32, CH], BF16)

    dR_sb = sbuf("dR_sb", [128, NSLOT * NDT * SCH], BF16)
    xcR_sb = sbuf("xcR_sb", [128, NSLOT * NDT * SCH], BF16)
    b_sb = sbuf("b_sb", [128, SCH], BF16)
    st_sb = sbuf("st_sb", [128, SCH], F32)
    pr_sb = sbuf("pr_sb", [128, SCH], BF16)
    Bex_sb = sbuf("Bex_sb", [128, 2 * SCH], BF16)
    Cex_sb = sbuf("Cex_sb", [128, 2 * SCH], BF16)
    yacc_sb = sbuf("yacc_sb", [128, NDT * SCH], BF16)
    yfin_sb = sbuf("yfin_sb", [128, 2 * SCH], BF16)

    yt_sb = sbuf("yt_sb", [128, KI * CH], BF16)
    szt_sb = sbuf("szt_sb", [128, 2 * CH], BF16)

    iq_sb = sbuf("iq_sb", [128, max(2 * CH, 1024)], I32)
    wqb_sb = sbuf("wqb_sb", [128, 1024], BF16)
    invbc_sb = wf32[:, 1024:2048]
    wqi_sb = iq_sb
    xro_sb = sbuf("xro_sb", [128, CH], F32)
    out_sb = sbuf("out_sb", [128, CH], F32)

    ps = [stack.enter_context(nc.psum_tensor(f"ps{i}", [128, 512], F32)) for i in range(8)]

    regs = {}

    def dmaS(out, in_):
        return dma_issue(trS, lambda e, o=out, i=in_: e.dma_start(out=o, in_=i))

    def dma_slot(eng, buf, slot, out, in_):
        ctr = buf.sems[slot]
        ctr.v += 16
        plan[eng].append(lambda e, o=out, i=in_, c=ctr: e.dma_start(out=o, in_=i)
                         .then_inc(c.h, 16))
        return ctr, ctr.v

    def dmaG(out, in_):
        return dma_issue(trG, lambda e, o=out, i=in_: e.dma_start(out=o, in_=i))

    def dmaC(out, in_):
        return dma_issue(trC, lambda e, o=out, i=in_: e.dma_start(out=o, in_=i))

    def dmaS_nc(out, in_):
        return dma_issue(trS, lambda e, o=out, i=in_: e.dma_start(
            out=o, in_=i, allow_slow_non_contiguous=True))

    def dmaG_nc(out, in_):
        return dma_issue(trG, lambda e, o=out, i=in_: e.dma_start(
            out=o, in_=i, allow_slow_non_contiguous=True))

    class Buf:
        def __init__(self, n, name=None):
            self.g = [[] for _ in range(n)]
            self.sems = [mksem(f"{name}{i}") for i in range(n)] if name else None

        def acquire(self, eng, slot):
            for s, v in self.g[slot]:
                wait(eng, s, v)
            self.g[slot] = []

        def consumed(self, slot, sem_name, v):
            self.g[slot].append((sem_name, v))

    xf32B = Buf(2, "bXF")
    wbfB = Buf(2, "bWB")
    wf32B = Buf(2, "bWF")
    stgB = Buf(2)
    stg2B = Buf(2)
    xinB = Buf(3)
    aB = Buf(4)
    BexB = Buf(2)
    CexB = Buf(2)
    yfinB = Buf(2)
    iqB = Buf(2)
    xroB = Buf(1)
    outB = Buf(1)
    sztB = Buf(2)

    bank_ev = [("sM", 0)] * 8

    def te_take(bk):
        s, v = bank_ev[bk]
        if v > 0:
            wait("tensor", s, v)

    def matgroup(emitters):
        for f in emitters[:-1]:
            put("tensor", f)
        return mark("tensor", emitters[-1], tM)

    dmaS(rank_sb[:], rank_in.ap())
    dmaS(bconv_sb[:], bconv_t.ap())
    dmaS(bdt_sb[:], bdt_t.ap())
    dmaS(wcv_sb[:], wcv_t.ap())
    dmaS(A_sb[:], Alog_my.ap())
    dmaS(D_sb[:], D_my.ap())
    dmaS(b1_sb[:], b1_t.ap())
    ld_setup = dmaS(b2_sb[:], b2_t.ap())

    wait("sync", "dSY", 16)

    def _ld_rank(e):
        regs["q"] = e.value_load(rank_sb[0:1, 0:1])
        regs["qCH"] = e.value_load(rank_sb[0:1, 1:2])
        regs["qn4"] = e.value_load(rank_sb[0:1, 2:3])
    put("sync", _ld_rank)

    put("vector", lambda e: e.memset(onesc_b[:], 1.0))
    put("vector", lambda e: e.memset(carry[:], 0.0))
    put("vector", lambda e: e.memset(r16[:], 0.0))
    put("vector", lambda e: e.memset(srow[0:1, 4 * CHH:4 * CHH + 1], 1e-5))
    mark("vector", lambda e: e.memset(xmax_sb[:], -1e30), vM)
    m_z = mark("vector", lambda e: e.memset(yacc_sb[0:96, 0:CH], 0.0), vM)
    wait("gpsimd", "vM", m_z)
    g_pad = dmaG(bass.AP(xbcSend, DXW * CH, [[CH, 96], [1, CH]]), yacc_sb[0:96, 0:CH])
    wait("vector", "dGP", g_pad)

    wait("scalar", "dSY", ld_setup)
    put("scalar", lambda e: e.activation(A_sb[:], A_sb[:], AF.Exp))
    mark("scalar", lambda e: e.mul(A_sb[:], A_sb[:], -1.0), sM)

    for i4 in range(4):
        sl = i4 % 2
        ld = dmaS(wf32[:, sl * 2048: sl * 2048 + DM], w1_rows.ap()[i4 * 128:(i4 + 1) * 128, :])
        wait("vector", "dSY", ld)
        put("vector", lambda e, i=i4, s=sl: e.tensor_reduce(
            qs_sb[:, i:i + 1], wf32[:, s * 2048: s * 2048 + DM],
            axis=AX.X, op=ALU.max, apply_absolute_value=True))
    ld = dmaS(wf32[:, 0:2048], w2_rows.ap()[:, 0:2048])
    ld = dmaS(wf32[:, 2048:4096], w2_rows.ap()[:, 2048:4096])
    wait("vector", "dSY", ld)
    put("vector", lambda e: e.tensor_reduce(qs_sb[:, 4:5], wf32[:, 0:2048],
                                            axis=AX.X, op=ALU.max, apply_absolute_value=True))
    v_hz = mark("vector", lambda e: e.tensor_reduce(qs_sb[:, 5:6], wf32[:, 2048:4096],
                                                    axis=AX.X, op=ALU.max,
                                                    apply_absolute_value=True), vM)
    wait("vector", "vM", v_hz)
    v_hz = mark("vector", lambda e: e.tensor_max(qs_sb[:, 4:5], qs_sb[:, 4:5],
                                                 qs_sb[:, 5:6]), vM)
    wait("vector", "vM", v_hz)
    put("vector", lambda e: e.tensor_scalar_mul(qs_sb[:, 8:13], qs_sb[:, 0:5], 1.0 / 127.0))
    v_hz = mark("vector", lambda e: e.reciprocal(rcp5_sb[:], qs_sb[:, 0:5]), vM)
    wait("vector", "vM", v_hz)
    v_sw = mark("vector", lambda e: e.tensor_scalar_mul(qs_sb[:, 16:21], rcp5_sb[:],
                                                        127.0), vM)
    wait("gpsimd", "vM", v_sw)
    dmaG_nc(bass.AP(swq_in, 0, [[1, 128], [128, 4]]), qs_sb[:, 8:12])
    dmaG_nc(bass.AP(swq_in, 512, [[1, 128], [128, 4]]), qs_sb[:, 16:20])
    dmaG_nc(bass.AP(swq_in, 1024, [[1, 128], [128, 1]]), qs_sb[:, 12:13])
    g_pack = dmaG_nc(bass.AP(swq_in, 1152, [[1, 128], [128, 1]]), qs_sb[:, 20:21])
    wait("gpsimd", "dGP", g_pack)
    cc_sw = mark("gpsimd", lambda e: e.collective_compute(
        "AllGather", ALU.bypass, ins=[swq_in.ap()], outs=[swq_out.ap()],
        replica_groups=G8), cc)

    wait("sync", "cc", cc_sw)
    for r in range(8):
        dmaS_nc(sw1_sb[:, 4 * r:4 * r + 4], bass.AP(swq_out, r * 1280, [[1, 128], [128, 4]]))
    dmaS_nc(sw2_sb[:, 0:KD], bass.AP(swq_out, 1024, [[1, 128], [1280, KD]]))

    v_q_prev = 0
    g_prev = 0
    for p4 in range(4):
        if v_q_prev > 0:
            wait("sync", "vM", v_q_prev)
        bcv = dmaS(wf32[:, 1024:2048], bass.AP(swq_out, 2 * p4 * 1280 + 512,
                                               [[0, 128], [1280, 2], [1, 512]]))
        off = (p4 % 2) * 2048
        ldw = dmaS(wf32[:, off:off + 1024], w1T_myk.ap()[:, p4 * 1024:(p4 + 1) * 1024])
        wait("vector", "dSY", max(bcv, ldw))
        v_q = mark("vector", lambda e, o=off: e.tensor_tensor(
            wf32[:, o:o + 1024], wf32[:, o:o + 1024], wf32[:, 1024:2048], ALU.mult), vM)
        v_q_prev = v_q
        wait("scalar", "vM", v_q)
        put("scalar", lambda e, o=off: e.copy(wqi_sb[:, 0:1024], wf32[:, o:o + 1024]))
        if g_prev > 0:
            wait("scalar", "dGP", g_prev)
        s_q = mark("scalar", lambda e: e.copy(wqb_sb[:], wqi_sb[:]), sM)
        wait("gpsimd", "sM", s_q)
        g_prev = dmaG(bass.AP(wq_in, p4 * 1024, [[4096, 128], [1, 1024]]), wqb_sb[:])
    wait("sync", "vM", v_q_prev)
    bcv2 = dmaS(wf32[:, 1024:2048], bass.AP(swq_out, 1152, [[0, 128], [1280, 8], [1, 128]]))
    for i4 in range(4):
        if i4 >= 2:
            wait("sync", "vM", v_q_prev)
        off = (i4 % 2) * 2048
        ldw = dmaS(wf32[:, off:off + 1024], w2T_myk.ap()[i4 * 128:(i4 + 1) * 128, :])
        wait("vector", "dSY", max(bcv2, ldw))
        v_q = mark("vector", lambda e, o=off: e.tensor_tensor(
            wf32[:, o:o + 1024], wf32[:, o:o + 1024], wf32[:, 1024:2048], ALU.mult), vM)
        v_q_prev = v_q
        wait("scalar", "vM", v_q)
        put("scalar", lambda e, o=off: e.copy(wqi_sb[:, 0:1024], wf32[:, o:o + 1024]))
        wait("scalar", "dGP", g_prev)
        s_q = mark("scalar", lambda e: e.copy(wqb_sb[:], wqi_sb[:]), sM)
        wait("gpsimd", "sM", s_q)
        g_prev = dmaG(bass.AP(wq_in, 128 * 4096 + i4 * 131072, [[1024, 128], [1, 1024]]),
                      wqb_sb[:])
    wf32B.consumed(0, "vM", v_q_prev)
    wf32B.consumed(1, "vM", v_q_prev)
    wait("gpsimd", "dGP", dGP.v)
    cc_wq = mark("gpsimd", lambda e: e.collective_compute(
        "AllGather", ALU.bypass, ins=[wq_in.ap()], outs=[wq_out.ap()],
        replica_groups=G8), cc)

    for m in range(KD):
        sl = m % 2
        wf32B.acquire("sync", sl)
        ldw = dmaS(wf32[:, sl * 2048:sl * 2048 + 2048],
                   bass.AP(WoutT.ap().tensor, m * 128, [[DM, 128], [128 * DM, KI], [1, 128]]))
        wbfB.acquire("scalar", sl)
        wait("scalar", "dSY", ldw)
        s_c = mark("scalar", lambda e, s=sl: e.copy(
            wbf[:, s * 2048:s * 2048 + 2048], wf32[:, s * 2048:s * 2048 + 2048]), sM)
        wf32B.consumed(sl, "sM", s_c)
        wait("gpsimd", "sM", s_c)
        g_sp = dmaG(bass.AP(wbf_out, m * 128, [[DM, 128], [128 * DM, KI], [1, 128]]),
                    wbf[:, sl * 2048:sl * 2048 + 2048])
        wbfB.consumed(sl, "dGP", g_sp)
    setup_v = vM.v

    recv_half = {}
    cm_last = {}
    scan_seq = []
    scan_mk = {}
    a_seq = []
    bex_marks = {}
    bex_seq = []
    yg_consumed = {}
    ag_mark = {}
    ysend_mark = {}
    xcsend_mark = {}
    prev = {"t_stats": 0, "v_xn": 0, "win_last": 0, "wx_last": 0, "wdt_last": 0,
            "xn_spill": 0, "conv_last": 0, "wout_last": 0, "v_yg": 0,
            "bex_round_last": {}, "recv_last": {}, "rows_dma": 0, "v_xm": 0}
    conv_mark = {}
    zmark = {}

    def w_stream(j, f32_t, bf_t, row_stride, m, mw, kk, off=None):
        sl = m % 2
        n = kk * mw
        if off is None:
            off = m * mw
        dst = wbf[:, sl * 2048: sl * 2048 + n]
        if j == 0:
            wf32B.acquire("sync", sl)
            lc, lv = dma_slot("sync", wf32B, sl, wf32[:, sl * 2048:sl * 2048 + n],
                              bass.AP(f32_t, off,
                                      [[row_stride, 128], [128 * row_stride, kk],
                                       [1, mw]]))
            wbfB.acquire("scalar", sl)
            wait("scalar", lc, lv)
            s_c = mark("scalar", lambda e, s=sl, n=n: e.copy(
                wbf[:, s * 2048:s * 2048 + n], wf32[:, s * 2048:s * 2048 + n]), sM)
            wf32B.consumed(sl, "sM", s_c)
            wait("gpsimd", "sM", s_c)
            g_sp = dmaG(bass.AP(bf_t, off,
                                [[row_stride, 128], [128 * row_stride, kk], [1, mw]]), dst)
            wbfB.consumed(sl, "dGP", g_sp)
            return dst, ("sM", s_c), sl
        wbfB.acquire("sync", sl)
        lc, lv = dma_slot("sync", wbfB, sl, dst,
                          bass.AP(bf_t, off,
                                  [[row_stride, 128], [128 * row_stride, kk], [1, mw]]))
        return dst, (lc, lv), sl

    def emit_recv(j):
        jb = j % 2
        wait("sync", "cc", ag_mark[("d", j)])
        wait("sync", "cc", ag_mark[("x", j)])
        dview = dAGo[jb].ap().rearrange("(r n p t) -> p n r t", r=GRP, p=128, t=CH)
        xview = xbcAGo[jb].ap().rearrange("(r n p t) -> p n r t", r=GRP, p=128, t=CH)
        for h in range(2):
            gslot = (2 * j + h) % NSLOT
            if 2 * j + h >= NSLOT:
                pidx = 2 * j + h - NSLOT
                pj, ph = pidx // 2, pidx % 2
                wait("sync", "vM", cm_last[(pj, ph, DST - 1)])
                wait("sync", "vM", yfin_last.get((pj, ph), 0))
            for rr2 in range(2):
                def fd(e, h=h, gs=gslot, dv=dview, rr2=rr2):
                    o = dR_sb[:, gs * NDT * SCH:(gs + 1) * NDT * SCH].rearrange(
                        "p (n x) -> p n x", n=NDT, x=SCH)[:, :, rr2 * CH:(rr2 + 1) * CH]
                    return e.dma_start(
                        out=o, in_=dv[:, bass.ds(regs["qn4"], NDT),
                                      2 * h + rr2:2 * h + rr2 + 1, :])
                dma_issue(trS, fd)
                def fx(e, h=h, gs=gslot, xv=xview, rr2=rr2):
                    o = xcR_sb[:, gs * NDT * SCH:(gs + 1) * NDT * SCH].rearrange(
                        "p (n x) -> p n x", n=NDT, x=SCH)[:, :, rr2 * CH:(rr2 + 1) * CH]
                    return e.dma_start(
                        out=o, in_=xv[:, bass.ds(regs["qn4"], NDT),
                                      2 * h + rr2:2 * h + rr2 + 1, :])
                recv_half[(j, h)] = dma_issue(trS, fx)
        prev["recv_last"][j] = recv_half[(j, 1)]

    yfin_last = {}

    def emit_scan_half_gen(j, h):
        jb = j % 2
        gslot = (2 * j + h) % NSLOT
        wait("scalar", "dSY", recv_half[(j, h)])
        wait("vector", "dSY", recv_half[(j, h)])
        for s in range(DST):
            eb = len(bex_seq) % 2
            if len(bex_seq) >= 2:
                pk = bex_seq[-2]
                BexB.acquire("scalar", eb)
                CexB.acquire("scalar", eb)
            dmaC(Bex_sb[:, eb * SCH:(eb + 1) * SCH],
                 bass.AP(xbcAGo[jb], (2 * h) * DXP * CH + (DI + s) * CH,
                         [[0, 128], [DXP * CH, 2], [1, CH]]))
            bx = dmaC(Cex_sb[:, eb * SCH:(eb + 1) * SCH],
                      bass.AP(xbcAGo[jb], (2 * h) * DXP * CH + (DI + DST + s) * CH,
                              [[0, 128], [DXP * CH, 2], [1, CH]]))
            bex_marks[(j, h, s)] = bx
            bex_seq.append((j, h, s))
            wait("vector", "dSC", bx)
            for dt in range(NDT):
                ai = len(a_seq) % 4
                aB.acquire("scalar", ai)
                src = dR_sb[:, (gslot * NDT + dt) * SCH:(gslot * NDT + dt + 1) * SCH]
                av = wf32[:, ai * SCH:(ai + 1) * SCH]
                a_v = mark("scalar", lambda e, src=src, av=av, dt=dt, s=s: e.activation(
                    av, src, AF.Exp, scale=A_sb[:, dt * DST + s:dt * DST + s + 1]), sM)
                a_seq.append((j, h, s, dt))
                xc = xcR_sb[:, (gslot * NDT + dt) * SCH:(gslot * NDT + dt + 1) * SCH]
                Bx = Bex_sb[:, eb * SCH:(eb + 1) * SCH]
                Cx = Cex_sb[:, eb * SCH:(eb + 1) * SCH]
                wait("vector", "sM", a_v)
                bbv = mark("vector", lambda e, xc=xc, Bx=Bx: e.tensor_mul(b_sb[:], xc, Bx), vM)
                sc_v = mark("vector", lambda e, av=av, dt=dt, s=s: e.tensor_tensor_scan(
                    st_sb[:], av, b_sb[:], carry[:, dt * DST + s:dt * DST + s + 1],
                    ALU.mult, ALU.add), vM)
                scan_mk[(j, h, s, dt)] = sc_v
                aB.consumed(ai, "vM", sc_v)
                wait("vector", "vM", sc_v)
                put("vector", lambda e, dt=dt, s=s: e.tensor_copy(
                    carry[:, dt * DST + s:dt * DST + s + 1], st_sb[:, SCH - 1:SCH]))
                cmv = mark("vector", lambda e, Cx=Cx: e.tensor_mul(pr_sb[:], st_sb[:], Cx), vM)
                if dt == NDT - 1:
                    cm_last[(j, h, s)] = cmv
                    BexB.consumed(eb, "vM", cmv)
                    CexB.consumed(eb, "vM", cmv)
                ya = yacc_sb[:, dt * SCH:(dt + 1) * SCH]
                if s == 0:
                    put("vector", lambda e, ya=ya: e.tensor_copy(ya, pr_sb[:]))
                else:
                    put("vector", lambda e, ya=ya: e.tensor_add(ya, pr_sb[:], ya))
            yield
        if h == 0 and j >= 1:
            wait("gpsimd", "cc", ag_mark[("y", j - 1)])
        for dt in range(NDT):
            xc = xcR_sb[:, (gslot * NDT + dt) * SCH:(gslot * NDT + dt + 1) * SCH]
            ya = yacc_sb[:, dt * SCH:(dt + 1) * SCH]
            ysl = dt % 2
            yfinB.acquire("vector", ysl)
            yf = yfin_sb[:, ysl * SCH:(ysl + 1) * SCH]
            v_yf = mark("vector", lambda e, xc=xc, ya=ya, yf=yf, dt=dt:
                        e.scalar_tensor_tensor(yf, xc, D_sb[:, dt:dt + 1], ya,
                                               ALU.mult, ALU.add), vM)
            yfin_last[(j, h)] = v_yf
            wait("gpsimd", "vM", v_yf)
            g_ys = dmaG(bass.AP(ySend, dt * 128 * RT + h * SCH, [[RT, 128], [1, SCH]]), yf)
            yfinB.consumed(ysl, "dGP", g_ys)
        if h == 1:
            prev["bex_round_last"][j] = dSC.v
            wait("gpsimd", "dGP", dGP.v)
            if j >= 2:
                wait("gpsimd", "vM", yg_consumed[j - 2])
            ag_mark[("y", j)] = mark("gpsimd", lambda e, jb=jb: e.collective_compute(
                "AllGather", ALU.bypass, ins=[ySend.ap()], outs=[yAGo[jb].ap()],
                replica_groups=G4), cc)
        yield

    def emit_scan_half(j, h):
        for _ in emit_scan_half_gen(j, h):
            pass

    def emit_gating(j):
        jb = j % 2
        wait("sync", "cc", ag_mark[("y", j)])
        if prev["wout_last"] > 0:
            wait("sync", "tM", prev["wout_last"])
            wait("sync", "vM", prev["v_yg"])
        yv = yAGo[jb].ap().rearrange("(r n p t) -> p r n t", r=GRP, p=128, t=RT)
        y_ld = 0
        for r in range(GRP):
            def fy(e, r=r, yv=yv):
                return e.dma_start(
                    out=yt_sb[:, r * NDT * CH:(r + 1) * NDT * CH].rearrange(
                        "p (r n t) -> p r n t", r=1, n=NDT, t=CH),
                    in_=yv[:, r:r + 1, :, bass.ds(regs["qCH"], CH)])
            y_ld = dma_issue(trS, fy)
        wait("sync", "dGP", zmark[j])
        for kt in range(KI):
            zsl = kt % 2
            sztB.acquire("sync", zsl)
            ldz = dmaS(szt_sb[:, zsl * CH:(zsl + 1) * CH],
                       bass.AP(z_dram, (j * DI + kt * 128) * CH, [[CH, 128], [1, CH]]))
            wait("scalar", "dSY", ldz)
            s_z = mark("scalar", lambda e, zs=zsl: e.activation(
                szt_sb[:, zs * CH:(zs + 1) * CH], szt_sb[:, zs * CH:(zs + 1) * CH],
                AF.Silu), sM)
            wait("vector", "sM", s_z)
            if kt == 0:
                wait("vector", "dSY", y_ld)
            v_g = mark("vector", lambda e, kt=kt, zs=zsl: e.tensor_mul(
                yt_sb[:, kt * CH:(kt + 1) * CH], yt_sb[:, kt * CH:(kt + 1) * CH],
                szt_sb[:, zs * CH:(zs + 1) * CH]), vM)
            sztB.consumed(zsl, "vM", v_g)
        v_yg = vM.v
        yg_consumed[j] = v_yg
        prev["v_yg"] = v_yg
        for m in range(KD):
            sl = m % 2
            wbfB.acquire("sync", sl)
            lc3, lv3 = dma_slot("sync", wbfB, sl, wbf[:, sl * 2048:sl * 2048 + 2048],
                                bass.AP(wbf_out, m * 128,
                                        [[DM, 128], [128 * DM, KI], [1, 128]]))
            bk = 5 if m % 2 == 0 else 6
            te_take(bk)
            wait("tensor", lc3, lv3)
            wait("tensor", "vM", v_yg)
            t_mm = matgroup([
                (lambda e, k=k, b=bk, sl=sl: e.matmul(
                    ps[b][:, 0:CH], wbf[:, sl * 2048 + k * 128:sl * 2048 + (k + 1) * 128],
                    yt_sb[:, k * CH:(k + 1) * CH], start=(k == 0), stop=(k == KI - 1)))
                for k in range(KI)])
            wbfB.consumed(sl, "tM", t_mm)
            wait("scalar", "tM", t_mm)
            ssl = m % 2
            stgB.acquire("scalar", ssl)
            s_ev = mark("scalar", lambda e, b=bk, ss=ssl: e.copy(
                stg_sb[:, ss * CH:(ss + 1) * CH], ps[b][:, 0:CH]), sM)
            bank_ev[bk] = ("sM", s_ev)
            wait("gpsimd", "sM", s_ev)
            g_sp = dmaG(bass.AP(s_dram, (j * DM + m * 128) * CH, [[CH, 128], [1, CH]]),
                        stg_sb[:, ssl * CH:(ssl + 1) * CH])
            stgB.consumed(ssl, "dGP", g_sp)
            prev["wout_last"] = t_mm

    first_wload = True
    for j in range(RNDS):
        jb = j % 2
        co = j * CHH

        if j >= 1:
            emit_recv(j - 1)

        t_stats = 0
        for k in range(KD):
            sl = k % 2
            xf32B.acquire("sync", sl)
            lcx, lvx = dma_slot("sync", xf32B, sl, xf32[:, sl * CHH:(sl + 1) * CHH],
                                x_fm.ap()[k * 128:(k + 1) * 128, co:co + CHH])
            wait("scalar", lcx, lvx)
            if k == 0:
                wait("scalar", "tM", prev["t_stats"])
                wait("scalar", "vM", prev["v_xn"])
            put("scalar", lambda e, s=sl, k=k: e.copy(
                xb_sb[:, k * CHH:(k + 1) * CHH], xf32[:, s * CHH:(s + 1) * CHH]))
            s_k = mark("scalar", lambda e, s=sl: e.activation(
                xsqb[:, s * CHH:(s + 1) * CHH], xf32[:, s * CHH:(s + 1) * CHH],
                AF.Square), sM)
            xf32B.consumed(sl, "sM", s_k)
            wait("tensor", "sM", s_k)
            if k == 0:
                te_take(2), te_take(3), te_take(4), te_take(5)
            put("tensor", lambda e, k=k: e.matmul(
                ps[2][0:1, 0:CH], onesc_b[:, 0:1], xb_sb[:, k * CHH + HALO:(k + 1) * CHH],
                start=(k == 0), stop=(k == KD - 1)))
            put("tensor", lambda e, k=k, s=sl: e.matmul(
                ps[3][0:1, 0:CH], onesc_b[:, 0:1], xsqb[:, s * CHH + HALO:(s + 1) * CHH],
                start=(k == 0), stop=(k == KD - 1)))
            put("tensor", lambda e, k=k: e.matmul(
                ps[4][0:1, 0:HALO], onesc_b[:, 0:1], xb_sb[:, k * CHH:k * CHH + HALO],
                start=(k == 0), stop=(k == KD - 1)))
            t_stats = mark("tensor", lambda e, k=k, s=sl: e.matmul(
                ps[5][0:1, 0:HALO], onesc_b[:, 0:1], xsqb[:, s * CHH:s * CHH + HALO],
                start=(k == 0), stop=(k == KD - 1)), tM)
        prev["t_stats"] = t_stats
        wait("scalar", "tM", t_stats)
        put("scalar", lambda e: e.copy(srow[0:1, HALO:CHH], ps[2][0:1, 0:CH]))
        put("scalar", lambda e: e.copy(srow[0:1, CHH + HALO:2 * CHH], ps[3][0:1, 0:CH]))
        put("scalar", lambda e: e.copy(srow[0:1, 0:HALO], ps[4][0:1, 0:HALO]))
        put("scalar", lambda e: e.copy(srow[0:1, CHH:CHH + HALO], ps[5][0:1, 0:HALO]))
        put("scalar", lambda e: e.mul(srow[0:1, 2 * CHH:3 * CHH], srow[0:1, 0:CHH], 1.0 / DM))
        s_st = mark("scalar", lambda e: e.mul(srow[0:1, CHH:2 * CHH],
                                              srow[0:1, CHH:2 * CHH], 1.0 / DM), sM)
        bank_ev[2] = bank_ev[3] = bank_ev[4] = bank_ev[5] = ("sM", s_st)
        wait("vector", "sM", s_st)
        put("vector", lambda e: e.tensor_mul(srow[0:1, 5 * CHH:6 * CHH],
                                             srow[0:1, 2 * CHH:3 * CHH],
                                             srow[0:1, 2 * CHH:3 * CHH]))
        v_var = mark("vector", lambda e: e.tensor_sub(srow[0:1, CHH:2 * CHH],
                                                      srow[0:1, CHH:2 * CHH],
                                                      srow[0:1, 5 * CHH:6 * CHH]), vM)
        wait("scalar", "vM", v_var)
        put("scalar", lambda e: e.activation(srow[0:1, 3 * CHH:4 * CHH],
                                             srow[0:1, CHH:2 * CHH], AF.Ln,
                                             bias=srow[0:1, 4 * CHH:4 * CHH + 1]))
        s_r = mark("scalar", lambda e: e.activation(srow[0:1, 3 * CHH:4 * CHH],
                                                    srow[0:1, 3 * CHH:4 * CHH], AF.Exp,
                                                    scale=-0.5), sM)
        wait("sync", "sM", s_r)
        wait("sync", "dSY", prev["rows_dma"])
        dmaS(bass.AP(rowD, j * 2 * CHH, [[CHH, 1], [1, CHH]]), srow[0:1, 3 * CHH:4 * CHH])
        rr = dmaS(bass.AP(rowD, j * 2 * CHH + CHH, [[CHH, 1], [1, CHH]]),
                  srow[0:1, 2 * CHH:3 * CHH])
        if prev["v_xn"] > 0:
            wait("gpsimd", "vM", prev["v_xn"])
        wait("gpsimd", "dSY", rr)
        rb = dmaG(rbc_sb[:], bass.AP(rowD, j * 2 * CHH, [[0, 128], [1, CHH]]))
        mb = dmaG(mbc_sb[:], bass.AP(rowD, j * 2 * CHH + CHH, [[0, 128], [1, CHH]]))
        prev["rows_dma"] = rr
        wait("vector", "dGP", mb)
        if prev["win_last"] > 0:
            wait("vector", "tM", prev["win_last"])
        if prev["xn_spill"] > 0:
            wait("vector", "dGP", prev["xn_spill"])
        if j == 0:
            wait("vector", "vM", setup_v)
        for k in range(KD):
            put("vector", lambda e, k=k: e.tensor_sub(
                xn_sb[:, k * CHH:(k + 1) * CHH], xb_sb[:, k * CHH:(k + 1) * CHH], mbc_sb[:]))
            put("vector", lambda e, k=k: e.tensor_mul(
                xn_sb[:, k * CHH:(k + 1) * CHH], xn_sb[:, k * CHH:(k + 1) * CHH], rbc_sb[:]))
        for k in range(KD):
            v_hz = mark("vector", lambda e, k=k: e.tensor_reduce(
                qs_sb[:, 22:23], xn_sb[:, k * CHH + HALO:(k + 1) * CHH],
                axis=AX.X, op=ALU.max), vM)
            wait("vector", "vM", v_hz)
            put("vector", lambda e: e.tensor_max(xmax_sb[:, 0:1], xmax_sb[:, 0:1],
                                                 qs_sb[:, 22:23]))
        if debug and j == 0:
            put("vector", lambda e: e.tensor_copy(dbgx_sb[:, 0:2], xmax_sb[:, 0:2]))
            put("vector", lambda e: e.tensor_copy(dbgx_sb[:, 6:7], qs_sb[:, 22:23]))
        if debug and j == RNDS - 1:
            put("vector", lambda e: e.tensor_copy(dbgx_sb[:, 2:4], xmax_sb[:, 0:2]))
        v_xn = mark("vector", lambda e: e.tensor_copy(qs_sb[:, 22:23], qs_sb[:, 22:23]), vM)
        prev["v_xn"] = v_xn
        prev["v_xm"] = v_xn
        wait("gpsimd", "vM", v_xn)
        prev["xn_spill"] = dmaG(
            bass.AP(xn_dram, j * DM * CHH, [[CHH, 128], [128 * CHH, KD], [1, CHH]]), xn_sb[:])
        quiesce(tier=2)

        if first_wload:
            first_wload = False
        win_last = 0
        for m in range(MI):
            wsl, wready, slot = w_stream(j, WinT.ap().tensor, wbf_in, 2 * DI, m, 128, KD)
            bk = m % 2
            te_take(bk)
            wait("tensor", wready[0], wready[1])
            wait("tensor", "vM", v_xn)
            t_mm = matgroup([
                (lambda e, k=k, w=wsl, b=bk: e.matmul(
                    ps[b][:, 0:CH], w[:, k * 128:(k + 1) * 128],
                    xn_sb[:, k * CHH + HALO:(k + 1) * CHH],
                    start=(k == 0), stop=(k == KD - 1)))
                for k in range(KD)])
            t_last = t_mm
            if m < KI:
                te_take(6)
                t_last = matgroup([
                    (lambda e, k=k, w=wsl, m=m: e.matmul(
                        ps[6][:, 3 * m:3 * m + HALO], w[:, k * 128:(k + 1) * 128],
                        xn_sb[:, k * CHH:k * CHH + HALO],
                        start=(k == 0), stop=(k == KD - 1)))
                    for k in range(KD)])
            wbfB.consumed(slot, "tM", t_last)
            win_last = t_last
            wait("scalar", "tM", t_last)
            if m < KI:
                xsl = m % 3
                xinB.acquire("scalar", xsl)
                put("scalar", lambda e, xs=xsl, b=bk: e.copy(
                    xin_sb[:, xs * CHH + HALO:(xs + 1) * CHH], ps[b][:, 0:CH]))
                s_ev = mark("scalar", lambda e, m=m, xs=xsl: e.copy(
                    xin_sb[:, xs * CHH:xs * CHH + HALO], ps[6][:, 3 * m:3 * m + HALO]), sM)
                bank_ev[6] = ("sM", s_ev)
                wait("vector", "sM", s_ev)
                if m == 0:
                    if prev["wx_last"] > 0:
                        wait("vector", "tM", prev["wx_last"])
                    if xcsend_mark.get(j - 1, 0) > 0:
                        wait("vector", "dGP", xcsend_mark[j - 1])
                xi = xin_sb[:, xsl * CHH:(xsl + 1) * CHH]
                xc = xcv_sb[:, m * CH:(m + 1) * CH]
                put("vector", lambda e, xi=xi, xc=xc, d=m: e.tensor_scalar(
                    xc, xi[:, 0:CH], wcv_sb[:, d * 4:d * 4 + 1], None, ALU.mult))
                for kk in range(1, 4):
                    put("vector", lambda e, xi=xi, xc=xc, d=m, kk=kk: e.scalar_tensor_tensor(
                        xc, xi[:, kk:kk + CH], wcv_sb[:, d * 4 + kk:d * 4 + kk + 1], xc,
                        ALU.mult, ALU.add))
                v_cv = mark("vector", lambda e, xc=xc: e.tensor_copy(xc[:, 0:1], xc[:, 0:1]),
                            vM)
                conv_mark[(j, m)] = v_cv
                xinB.consumed(xsl, "vM", v_cv)
                wait("scalar", "vM", v_cv)
                s_ev = mark("scalar", lambda e, xc=xc, d=m: e.activation(
                    xc, xc, AF.Silu, bias=bconv_sb[:, d:d + 1]), sM)
            else:
                zsl = m % 2
                stgB.acquire("scalar", zsl)
                s_ev = mark("scalar", lambda e, b=bk, zs=zsl: e.copy(
                    stg_sb[:, zs * CH:(zs + 1) * CH], ps[b][:, 0:CH]), sM)
                wait("gpsimd", "sM", s_ev)
                g_z = dmaG(bass.AP(z_dram, (j * DI + (m - KI) * 128) * CH,
                                   [[CH, 128], [1, CH]]),
                           stg_sb[:, zsl * CH:(zsl + 1) * CH])
                zmark[j] = g_z
                stgB.consumed(zsl, "dGP", g_z)
            bank_ev[bk] = ("sM", s_ev)
        prev["win_last"] = win_last
        xc_ready = sM.v

        wait("gpsimd", "sM", xc_ready)
        if j >= 1:
            wait("gpsimd", "cc", ag_mark[("x", j - 1)])
        for kk in range(KI):
            xcsend_mark[j] = dmaG(bass.AP(xbcSend, kk * 128 * CH, [[CH, 128], [1, CH]]),
                                  xcv_sb[:, kk * CH:(kk + 1) * CH])

        gen0 = emit_scan_half_gen(j - 1, 0) if j >= 1 else iter(())

        wx_last = 0
        for m in range(KI + 1):
            next(gen0, None)
            mw = 128 if m < KI else 2 * DST
            wsl, wready, slot = w_stream(j, WxT.ap().tensor, wbf_x, DXW, m, mw, KI,
                                         off=m * 128)
            bk = m % 2
            te_take(bk)
            wait("tensor", wready[0], wready[1])
            wait("tensor", "sM", xc_ready)
            t_mm = matgroup([
                (lambda e, k=k, w=wsl, b=bk, mw=mw: e.matmul(
                    ps[b][0:mw, 0:CH], w[:, k * mw:(k + 1) * mw],
                    xcv_sb[:, k * CH:(k + 1) * CH], start=(k == 0), stop=(k == KI - 1)))
                for k in range(KI)])
            wbfB.consumed(slot, "tM", t_mm)
            wx_last = t_mm
            wait("scalar", "tM", t_mm)
            if m < KI:
                if m == 0 and prev["wdt_last"] > 0:
                    wait("scalar", "tM", prev["wdt_last"])
                s_ev = mark("scalar", lambda e, m=m, b=bk: e.copy(
                    drw_sb[:, m * CH:(m + 1) * CH], ps[b][:, 0:CH]), sM)
            else:
                s_ev = mark("scalar", lambda e, b=bk: e.copy(
                    bcstg[:, 0:CH], ps[b][0:32, 0:CH]), sM)
                wait("gpsimd", "sM", s_ev)
                dmaG(bass.AP(xbcSend, DI * CH, [[CH, 32], [1, CH]]), bcstg[:, 0:CH])
            bank_ev[bk] = ("sM", s_ev)
        for _ in gen0:
            pass
        prev["wx_last"] = wx_last
        drw_ready = sM.v
        wait("gpsimd", "dGP", dGP.v)
        if j >= 2:
            wait("gpsimd", "dSY", prev["recv_last"].get(j - 2, 0))
            wait("gpsimd", "dSC", prev["bex_round_last"].get(j - 2, 0))
        ag_mark[("x", j)] = mark("gpsimd", lambda e, jb=jb: e.collective_compute(
            "AllGather", ALU.bypass, ins=[xbcSend.ap()], outs=[xbcAGo[jb].ap()],
            replica_groups=G4), cc)
        quiesce(tier=2)

        gen1 = emit_scan_half_gen(j - 1, 1) if j >= 1 else iter(())
        wdt_last = 0
        for m in range(KI):
            next(gen1, None)
            next(gen1, None)
            wsl, wready, slot = w_stream(j, WdtT.ap().tensor, wbf_dt, DI, m, 128, KI)
            bk = m % 2
            te_take(bk)
            wait("tensor", wready[0], wready[1])
            wait("tensor", "sM", drw_ready)
            t_mm = matgroup([
                (lambda e, k=k, w=wsl, b=bk: e.matmul(
                    ps[b][:, 0:CH], w[:, k * 128:(k + 1) * 128],
                    drw_sb[:, k * CH:(k + 1) * CH], start=(k == 0), stop=(k == KI - 1)))
                for k in range(KI)])
            wbfB.consumed(slot, "tM", t_mm)
            wdt_last = t_mm
            wait("scalar", "tM", t_mm)
            esl = m % 2
            stg2B.acquire("scalar", esl)
            put("scalar", lambda e, b=bk, es=esl, m=m: e.activation(
                stg2_sb[:, es * CH:(es + 1) * CH], ps[b][:, 0:CH], AF.Exp,
                bias=bdt_sb[:, m:m + 1]))
            s_ev = mark("scalar", lambda e, es=esl: e.activation(
                stg2_sb[:, es * CH:(es + 1) * CH], stg2_sb[:, es * CH:(es + 1) * CH],
                AF.Ln, bias=1.0), sM)
            bank_ev[bk] = ("sM", s_ev)
            wait("gpsimd", "sM", s_ev)
            if m == 0 and j >= 1:
                wait("gpsimd", "cc", ag_mark[("d", j - 1)])
            g_d = dmaG(bass.AP(dSend, m * 128 * CH, [[CH, 128], [1, CH]]),
                       stg2_sb[:, esl * CH:(esl + 1) * CH])
            stg2B.consumed(esl, "dGP", g_d)
        prev["wdt_last"] = wdt_last
        wait("gpsimd", "dGP", dGP.v)
        if j >= 2:
            wait("gpsimd", "dSY", prev["recv_last"].get(j - 2, 0))
        ag_mark[("d", j)] = mark("gpsimd", lambda e, jb=jb: e.collective_compute(
            "AllGather", ALU.bypass, ins=[dSend.ap()], outs=[dAGo[jb].ap()],
            replica_groups=G4), cc)

        if debug and j == 0:
            wait("gpsimd", "cc", ag_mark[("d", 0)])
            dmaG(dbg["dsend0"].ap(), dSend.ap())
            dmaG(dbg["xbc0"].ap(), xbcSend.ap())

        quiesce(tier=2)

        for _ in gen1:
            pass
        if j >= 2:
            emit_gating(j - 2)
        quiesce(tier=2)

    emit_recv(RNDS - 1)
    genp0 = emit_scan_half_gen(RNDS - 1, 0)
    genp1 = emit_scan_half_gen(RNDS - 1, 1)
    quiesce()

    wait("vector", "vM", prev["v_xm"])
    if debug:
        put("vector", lambda e: e.tensor_copy(dbgx_sb[:, 4:6], xmax_sb[:, 0:2]))
        put("vector", lambda e: e.tensor_copy(dbgx_sb[:, 8:16], qs_sb[:, 16:24]))
    v_fl = mark("vector", lambda e: e.tensor_copy(qs_sb[:, 23:24], xmax_sb[:, 0:1]), vM)
    wait("sync", "vM", v_fl)
    w_flip = dmaS(flipD.ap(), qs_sb[:, 23:24])
    wait("sync", "dSY", w_flip)
    fl2 = dmaS(srow[0:1, 0:128], flipD.ap()[None, :])
    wait("vector", "dSY", fl2)
    v_m1 = mark("vector", lambda e: e.tensor_reduce(
        r16[0:1, 0:1], srow[0:1, 0:128], axis=AX.X, op=ALU.max), vM)
    wait("sync", "vM", v_m1)
    ar1_in = dmaS(mx_in.ap()[None, :], r16[0:1, 0:16])
    wait("gpsimd", "dSY", ar1_in)
    cc_ar1 = mark("gpsimd", lambda e: e.collective_compute(
        "AllReduce", ALU.max, ins=[mx_in.ap()], outs=[mx1_out.ap()],
        replica_groups=G8), cc)
    wait("sync", "cc", cc_ar1)
    if debug:
        wait("gpsimd", "cc", cc_ar1)
        dmaG(bass.AP(dbg["ar1"].ap().tensor, 0, [[1, 128]]), flipD.ap())
        dmaG(bass.AP(dbg["ar1"].ap().tensor, 128, [[1, 16]]), mx_in.ap())
        dmaG(bass.AP(dbg["ar1"].ap().tensor, 144, [[1, 16]]), mx1_out.ap())
    m1_ld = dmaS(r16[0:1, 0:16], mx1_out.ap()[None, :])
    wait("vector", "dSY", m1_ld)
    put("vector", lambda e: e.tensor_scalar_max(r16[0:1, 0:1], r16[0:1, 0:1], 1e-8))
    put("vector", lambda e: e.tensor_scalar_mul(r16[0:1, 1:2], r16[0:1, 0:1], 1.0 / 255.0))
    v_hz = mark("vector", lambda e: e.reciprocal(rcp1_sb[:], r16[0:1, 0:1]), vM)
    wait("vector", "vM", v_hz)
    v_s1 = mark("vector", lambda e: e.tensor_scalar_mul(r16[0:1, 2:3], rcp1_sb[:],
                                                        255.0), vM)
    wait("sync", "vM", v_s1)
    w_scl = dmaS(bass.AP(sclD, 0, [[4, 1], [1, 4]]), r16[0:1, 0:4])
    wait("sync", "dSY", w_scl)
    s1b = dmaS(scl_sb[:, 0:1], bass.AP(sclD, 2, [[0, 128], [1, 1]]))
    s1c = dmaS(scl_sb[:, 2:3], bass.AP(sclD, 1, [[0, 128], [1, 1]]))
    wait("vector", "dSY", s1c)
    v_hz = mark("vector", lambda e: e.tensor_scalar(scale1_sb[:], sw1_sb[:],
                                                    scl_sb[:, 2:3], None, ALU.mult), vM)
    wait("vector", "vM", v_hz)
    v_hz = mark("vector", lambda e: e.reciprocal(bq1_sb[:], scale1_sb[:]), vM)
    wait("vector", "vM", v_hz)
    v_bq = mark("vector", lambda e: e.tensor_mul(bq1_sb[:], b1_sb[:], bq1_sb[:]), vM)
    wait("vector", "vM", v_bq)
    wait("scalar", "vM", v_bq)
    s_hz2 = mark("scalar", lambda e: e.copy(wqi_sb[:, 0:MF], bq1_sb[:]), sM)
    wait("scalar", "sM", s_hz2)
    s_bq = mark("scalar", lambda e: e.copy(bq1_sb[:], wqi_sb[:, 0:MF]), sM)
    wait("vector", "sM", s_bq)
    v_bq1 = mark("vector", lambda e: e.tensor_mul(bq1_sb[:], bq1_sb[:], scale1_sb[:]), vM)

    quiesce()

    def xq_tile(idx):
        base = xcv_sb if idx < KI else drw_sb
        i = idx % KI
        return base[:, i * CH:(i + 1) * CH]

    wait("scalar", "tM", prev["wx_last"])
    wait("scalar", "tM", prev["wdt_last"])
    wait("scalar", "dGP", xcsend_mark[RNDS - 1])
    wait("scalar", "dSY", s1b)
    wait("sync", "dGP", prev["xn_spill"])
    wait("sync", "cc", cc_wq)
    xq_last = 0
    for j in range(RNDS):
        for k in range(KD):
            xsl = k % 2
            stg2B.acquire("sync", xsl)
            ldxn = dmaS(stg2_sb[:, xsl * CH:(xsl + 1) * CH],
                        bass.AP(xn_dram, j * DM * CHH + k * 128 * CHH + HALO,
                                [[CHH, 128], [1, CH]]))
            wait("scalar", "dSY", ldxn)
            iqB.acquire("scalar", xsl)
            s_hz = mark("scalar", lambda e, xs=xsl: e.activation(
                iq_sb[:, xs * CH:(xs + 1) * CH], stg2_sb[:, xs * CH:(xs + 1) * CH],
                AF.Copy, scale=scl_sb[:, 0:1]), sM)
            wait("scalar", "sM", s_hz)
            s_i = mark("scalar", lambda e, j=j, k=k, xs=xsl: e.copy(
                xq_tile(j * KD + k), iq_sb[:, xs * CH:(xs + 1) * CH]), sM)
            stg2B.consumed(xsl, "sM", s_i)
            wait("vector", "sM", s_i)
            xq_last = mark("vector", lambda e, j=j, k=k: e.tensor_scalar_max(
                xq_tile(j * KD + k), xq_tile(j * KD + k), 0.0), vM)

    quiesce()

    wait("tensor", "cc", cc_wq)
    wait("tensor", "vM", xq_last)
    wait("scalar", "vM", v_bq1)
    hmax_v = 0
    for m in range(MF):
        if m < 17:
            next(genp0, None)
        else:
            next(genp1, None)
            next(genp1, None)
        sl = m % 2
        wbfB.acquire("sync", sl)
        lc1, lv1 = dma_slot("sync", wbfB, sl, wbf[:, sl * 2048:sl * 2048 + KD * 128],
                            bass.AP(wq_out, m * 128, [[4096, 128], [WQBLK, KD], [1, 128]]))
        wait("tensor", lc1, lv1)
        t_grp = 0
        for j in range(RNDS):
            bk = (m * RNDS + j) % 2
            te_take(bk)
            t_grp = matgroup([
                (lambda e, k=k, j=j, b=bk, s=sl: e.matmul(
                    ps[b][:, 0:CH], wbf[:, s * 2048 + k * 128:s * 2048 + (k + 1) * 128],
                    xq_tile(j * KD + k), start=(k == 0), stop=(k == KD - 1)))
                for k in range(KD)])
            wait("scalar", "tM", t_grp)
            hsl = (m * RNDS + j) % 2
            stgB.acquire("scalar", hsl)
            s_h = mark("scalar", lambda e, b=bk, hs=hsl, m=m: e.activation(
                stg_sb[:, hs * CH:(hs + 1) * CH], ps[b][:, 0:CH], AF.Relu,
                bias=bq1_sb[:, m:m + 1], scale=scale1_sb[:, m:m + 1]), sM)
            bank_ev[bk] = ("sM", s_h)
            wait("gpsimd", "sM", s_h)
            g_h = dmaG(bass.AP(h_dram, (m * 128) * TLOC + j * CH, [[TLOC, 128], [1, CH]]),
                       stg_sb[:, hsl * CH:(hsl + 1) * CH])
            stgB.consumed(hsl, "dGP", g_h)
            wait("vector", "sM", s_h)
            v_hz = mark("vector", lambda e, hs=hsl: e.tensor_reduce(
                qs_sb[:, 22:23], stg_sb[:, hs * CH:(hs + 1) * CH], axis=AX.X,
                op=ALU.max), vM)
            wait("vector", "vM", v_hz)
            hmax_v = mark("vector", lambda e: e.tensor_max(
                xmax_sb[:, 1:2], xmax_sb[:, 1:2], qs_sb[:, 22:23]), vM)
            stgB.consumed(hsl, "vM", hmax_v)
        wbfB.consumed(sl, "tM", t_grp)

    for _ in genp0:
        pass
    for _ in genp1:
        pass
    quiesce()
    h_done = dGP.v

    if RNDS >= 2:
        emit_gating(RNDS - 2)

    quiesce()

    wait("vector", "vM", hmax_v)
    put("vector", lambda e: e.tensor_scalar_max(xmax_sb[:, 1:2], xmax_sb[:, 1:2], 0.0))
    v_fl2 = mark("vector", lambda e: e.tensor_copy(qs_sb[:, 23:24], xmax_sb[:, 1:2]), vM)
    wait("sync", "vM", v_fl2)
    w_flip2 = dmaS(flipD.ap(), qs_sb[:, 23:24])
    wait("sync", "dSY", w_flip2)
    fl3 = dmaS(srow[0:1, 0:128], flipD.ap()[None, :])
    wait("vector", "dSY", fl3)
    v_m2 = mark("vector", lambda e: e.tensor_reduce(
        r16[0:1, 0:1], srow[0:1, 0:128], axis=AX.X, op=ALU.max), vM)
    wait("sync", "vM", v_m2)
    ar2_in = dmaS(mx_in.ap()[None, :], r16[0:1, 0:16])
    wait("gpsimd", "dSY", ar2_in)
    cc_ar2 = mark("gpsimd", lambda e: e.collective_compute(
        "AllReduce", ALU.max, ins=[mx_in.ap()], outs=[mx2_out.ap()],
        replica_groups=G8), cc)
    wait("sync", "cc", cc_ar2)
    m2_ld = dmaS(r16[0:1, 0:16], mx2_out.ap()[None, :])
    wait("vector", "dSY", m2_ld)
    put("vector", lambda e: e.tensor_scalar_max(r16[0:1, 0:1], r16[0:1, 0:1], 1e-8))
    put("vector", lambda e: e.tensor_scalar_mul(r16[0:1, 1:2], r16[0:1, 0:1], 1.0 / 255.0))
    v_hz = mark("vector", lambda e: e.reciprocal(rcp1_sb[:], r16[0:1, 0:1]), vM)
    wait("vector", "vM", v_hz)
    v_s2 = mark("vector", lambda e: e.tensor_scalar_mul(r16[0:1, 2:3], rcp1_sb[:],
                                                        255.0), vM)
    wait("sync", "vM", v_s2)
    w_scl2 = dmaS(bass.AP(sclD, 8, [[4, 1], [1, 4]]), r16[0:1, 0:4])
    wait("sync", "dSY", w_scl2)
    s2b = dmaS(scl_sb[:, 1:2], bass.AP(sclD, 10, [[0, 128], [1, 1]]))
    s2c = dmaS(scl_sb[:, 3:4], bass.AP(sclD, 9, [[0, 128], [1, 1]]))
    wait("vector", "dSY", s2c)
    v_hz = mark("vector", lambda e: e.tensor_scalar(scale2_sb[:], sw2_sb[:],
                                                    scl_sb[:, 3:4], None, ALU.mult), vM)
    wait("vector", "vM", v_hz)
    v_hz = mark("vector", lambda e: e.reciprocal(bq2_sb[:], scale2_sb[:]), vM)
    wait("vector", "vM", v_hz)
    v_bq2a = mark("vector", lambda e: e.tensor_mul(bq2_sb[:], b2_sb[:], bq2_sb[:]), vM)
    wait("vector", "vM", v_bq2a)
    wait("scalar", "vM", v_bq2a)
    s_hz2 = mark("scalar", lambda e: e.copy(wqi_sb[:, 0:KD], bq2_sb[:]), sM)
    wait("scalar", "sM", s_hz2)
    s_bq2 = mark("scalar", lambda e: e.copy(bq2_sb[:], wqi_sb[:, 0:KD]), sM)
    wait("vector", "sM", s_bq2)
    v_bq2 = mark("vector", lambda e: e.tensor_mul(bq2_sb[:], bq2_sb[:], scale2_sb[:]), vM)

    def hq_tile(idx):
        base = dR_sb if idx < KI else xcR_sb
        i = idx % KI
        return base[:, i * CH:(i + 1) * CH]

    quiesce()
    last_cm = cm_last[(RNDS - 1, 1, DST - 1)]
    last_yf = yfin_last[(RNDS - 1, 1)]
    wait("scalar", "vM", max(last_cm, last_yf))
    wait("scalar", "vM", v_bq2)
    wait("scalar", "dSY", s2b)
    wait("sync", "dGP", h_done)
    for j in range(RNDS):
        hq_last = 0
        for k in range(KH):
            hsl = k % 2
            stg2B.acquire("sync", hsl)
            ldh = dmaS(stg2_sb[:, hsl * CH:(hsl + 1) * CH],
                       bass.AP(h_dram, (k * 128) * TLOC + j * CH, [[TLOC, 128], [1, CH]]))
            wait("scalar", "dSY", ldh)
            iqB.acquire("scalar", hsl)
            s_hz = mark("scalar", lambda e, hs=hsl: e.activation(
                iq_sb[:, hs * CH:(hs + 1) * CH], stg2_sb[:, hs * CH:(hs + 1) * CH],
                AF.Copy, scale=scl_sb[:, 1:2]), sM)
            wait("scalar", "sM", s_hz)
            s_hq = mark("scalar", lambda e, k=k, hs=hsl: e.copy(
                hq_tile(k), iq_sb[:, hs * CH:(hs + 1) * CH]), sM)
            stg2B.consumed(hsl, "sM", s_hq)
            wait("vector", "sM", s_hq)
            hq_last = mark("vector", lambda e, k=k: e.tensor_scalar_max(
                hq_tile(k), hq_tile(k), 0.0), vM)
        if debug and j == 0:
            wait("gpsimd", "vM", hq_last)
            dmaG(bass.AP(dbg["hq0"].ap().tensor, 0, [[4 * CH, 128], [1, 4 * CH]]),
                 dR_sb[:, 0:4 * CH])
        wait("tensor", "vM", hq_last)
        for m in range(KD):
            sl = m % 2
            wbfB.acquire("sync", 0)
            wbfB.acquire("sync", 1)
            for r in range(8):
                lc2, lv2 = dma_slot("sync", wbfB, 0, wbf[:, r * 512:(r + 1) * 512],
                                    bass.AP(wq_out, r * WQBLK + 128 * 4096 + m * 128,
                                            [[DM, 128], [128 * DM, 4], [1, 128]]))
            te_take(m)
            wait("tensor", lc2, lv2)
            t_mm = matgroup([
                (lambda e, k=k, m=m: e.matmul(
                    ps[m][:, 0:CH], wbf[:, k * 128:(k + 1) * 128],
                    hq_tile(k), start=(k == 0), stop=(k == KH - 1)))
                for k in range(KH)])
            wbfB.consumed(0, "tM", t_mm)
            wbfB.consumed(1, "tM", t_mm)
            wait("scalar", "tM", t_mm)
            gsl = m % 2
            stgB.acquire("scalar", gsl)
            s_g = mark("scalar", lambda e, m=m, gs=gsl: e.activation(
                stg_sb[:, gs * CH:(gs + 1) * CH], ps[m][:, 0:CH], AF.Silu,
                bias=bq2_sb[:, m:m + 1], scale=scale2_sb[:, m:m + 1]), sM)
            bank_ev[m] = ("sM", s_g)
            wait("gpsimd", "sM", s_g)
            g_g = dmaG(bass.AP(g_dram, (m * 128) * TLOC + j * CH, [[TLOC, 128], [1, CH]]),
                       stg_sb[:, gsl * CH:(gsl + 1) * CH])
            stgB.consumed(gsl, "dGP", g_g)
    g_done = dGP.v

    quiesce()

    emit_gating(RNDS - 1)
    quiesce()
    s_done = dGP.v

    wait("sync", "dGP", max(g_done, s_done))
    out_last = 0
    for j in range(RNDS):
        for kt in range(KD):
            osl = kt % 2
            xroB.acquire("sync", 0)
            ldx = dmaS(xro_sb[:],
                       x_fm.ap()[kt * 128:(kt + 1) * 128, j * CHH + HALO:(j + 1) * CHH])
            stgB.acquire("sync", osl)
            lds = dmaS(stg_sb[:, osl * CH:(osl + 1) * CH],
                       bass.AP(s_dram, (j * DM + kt * 128) * CH, [[CH, 128], [1, CH]]))
            stg2B.acquire("sync", osl)
            ldg = dmaS(stg2_sb[:, osl * CH:(osl + 1) * CH],
                       bass.AP(g_dram, (kt * 128) * TLOC + j * CH, [[TLOC, 128], [1, CH]]))
            wait("vector", "dSY", ldg)
            v_p = mark("vector", lambda e, os=osl: e.tensor_mul(
                stg_sb[:, os * CH:(os + 1) * CH], stg_sb[:, os * CH:(os + 1) * CH],
                stg2_sb[:, os * CH:(os + 1) * CH]), vM)
            stg2B.consumed(osl, "vM", v_p)
            outB.acquire("vector", 0)
            v_o = mark("vector", lambda e, os=osl: e.tensor_add(
                out_sb[:], xro_sb[:],
                stg_sb[:, os * CH:(os + 1) * CH]), vM)
            stgB.consumed(osl, "vM", v_o)
            xroB.consumed(0, "vM", v_o)
            wait("sync", "vM", v_o)
            out_last = dmaS(out_fm.ap()[kt * 128:(kt + 1) * 128, j * CH:(j + 1) * CH],
                            out_sb[:])
            outB.consumed(0, "dSY", out_last)

    if debug:
        wait("gpsimd", "dGP", dGP.v)
        wait("gpsimd", "dSY", out_last)
        dmaG(dbg["y0"].ap(), ySend.ap())
        dmaG(dbg["h0"].ap(), h_dram.ap())
        dmaG(dbg["g0"].ap(), g_dram.ap())
        dmaG(dbg["s0"].ap(), s_dram.ap())
        dmaG(dbg["xn0"].ap(), xn_dram.ap())
        dmaG(bass.AP(dbg["sc"].ap().tensor, 0, [[1, 4]]), bass.AP(sclD, 0, [[1, 4]]))
        dmaG(bass.AP(dbg["sc"].ap().tensor, 8, [[1, 4]]), bass.AP(sclD, 8, [[1, 4]]))
        dmaG(dbg["wq"].ap(), wq_out.ap())
        dmaG(bass.AP(dbg["carry"].ap().tensor, 0, [[NDT * DST, 128], [1, NDT * DST]]),
             carry[:])
        dmaG(bass.AP(dbg["xm"].ap().tensor, 0, [[16, 128], [1, 16]]), dbgx_sb[:])
        dmaG(dbg["wqin"].ap(), wq_in.ap())
        dmaG(dbg["swq"].ap(), swq_out.ap())
        dmaG(bass.AP(dbg["xq"].ap().tensor, 0, [[KI * CH, 128], [1, KI * CH]]), xcv_sb[:])
        dmaG(bass.AP(dbg["xq"].ap().tensor, DI * CH, [[KI * CH, 128], [1, KI * CH]]), drw_sb[:])
        dmaG(bass.AP(dbg["q1"].ap().tensor, 0, [[2 * MF, 128], [1, MF]]), scale1_sb[:])
        dmaG(bass.AP(dbg["q1"].ap().tensor, MF, [[2 * MF, 128], [1, MF]]), bq1_sb[:])
        dmaG(bass.AP(dbg["q2"].ap().tensor, 0, [[2 * KD, 128], [1, KD]]), scale2_sb[:])
        dmaG(bass.AP(dbg["q2"].ap().tensor, KD, [[2 * KD, 128], [1, KD]]), bq2_sb[:])


    wait("sync", "dSY", out_last)
    wait("gpsimd", "dSY", out_last)
    wait("gpsimd", "dGP", dGP.v)

    if vec_limit is not None:
        if isinstance(vec_limit, dict):
            for k, n in vec_limit.items():
                plan[k] = plan[k][:n]
        else:
            plan["vector"] = plan["vector"][:vec_limit]

    with nc.Block() as block:
        @block.sync
        def _(e):
            for f in plan["sync"]:
                f(e)

        @block.tensor
        def _(e):
            for f in plan["tensor"]:
                f(e)

        @block.vector
        def _(e):
            for f in plan["vector"]:
                f(e)

        @block.scalar
        def _(e):
            for f in plan["scalar"]:
                f(e)

        @block.gpsimd
        def _(e):
            for f in plan["gpsimd"]:
                f(e)

    stack.close()
    return nc


def _layout_inputs(inputs, T_total, CH):
    NCHUNK = T_total // CH
    RNDS = NCHUNK // GRP
    CHH = CH + HALO
    x = np.asarray(inputs["L0_attn_x"], np.float32)
    WinT = np.ascontiguousarray(np.asarray(inputs["L0_attn_W_in"], np.float32).T)
    WxT = np.ascontiguousarray(np.asarray(inputs["L0_attn_W_x"], np.float32).T)
    WdtT = np.ascontiguousarray(np.asarray(inputs["L0_attn_W_dt"], np.float32).T)
    WoutT = np.ascontiguousarray(np.asarray(inputs["L0_attn_W_out"], np.float32).T)
    w1 = np.asarray(inputs["L0_ffn_w1"], np.float32)
    w2 = np.asarray(inputs["L0_ffn_w2"], np.float32)
    w1T = np.ascontiguousarray(w1.T)
    w2T = np.ascontiguousarray(w2.T)
    bconv = np.ascontiguousarray(
        np.asarray(inputs["L0_attn_b_conv"], np.float32).reshape(KI, 128).T)
    bdt = np.ascontiguousarray(
        np.asarray(inputs["L0_attn_b_dt"], np.float32).reshape(KI, 128).T)
    wcv = np.ascontiguousarray(
        np.asarray(inputs["L0_attn_w_conv"], np.float32).reshape(KI, 128, 4)
        .transpose(1, 0, 2).reshape(128, KI * 4))
    Alog = np.asarray(inputs["L0_attn_A_log"], np.float32)
    Dv = np.asarray(inputs["L0_attn_D"], np.float32)
    b1 = np.ascontiguousarray(np.asarray(inputs["L0_ffn_b1"], np.float32).reshape(MF, 128).T)
    b2 = np.ascontiguousarray(np.asarray(inputs["L0_ffn_b2"], np.float32).reshape(KD, 128).T)

    xp = np.pad(x, ((0, 0), (HALO, 0), (0, 0)))
    DQ = DI // GRP
    NDT = DQ // 128
    in_maps = []
    for c in range(8):
        b, q = divmod(c, GRP)
        cols = []
        for j in range(RNDS):
            g = q + GRP * j
            cols.append(xp[b, g * CH:g * CH + CHH, :].T)
        x_c = np.ascontiguousarray(np.concatenate(cols, axis=1), np.float32)
        dq0 = q * DQ
        Alog_my = np.ascontiguousarray(
            Alog[dq0:dq0 + DQ].reshape(NDT, 128, DST).transpose(1, 0, 2)
            .reshape(128, NDT * DST))
        D_my = np.ascontiguousarray(Dv[dq0:dq0 + DQ].reshape(NDT, 128).T)
        in_maps.append({
            "x_fm": x_c,
            "WinT": WinT, "WxT": WxT, "WdtT": WdtT, "WoutT": WoutT,
            "w1T_myk": np.ascontiguousarray(w1T[c * 128:(c + 1) * 128]),
            "w2T_myk": np.ascontiguousarray(w2T[c * 512:(c + 1) * 512]),
            "w1_rows": np.ascontiguousarray(w1[c * 512:(c + 1) * 512]),
            "w2_rows": np.ascontiguousarray(w2[c * 128:(c + 1) * 128]),
            "bconv_t": bconv, "bdt_t": bdt, "wcv_t": wcv,
            "Alog_my": Alog_my, "D_my": D_my,
            "b1_t": b1, "b2_t": b2,
            "rank_in": np.array([[q, q * CH, q * (DI // GRP // 128), 0]], np.int32),
        })
    return in_maps


def kernel(**inputs):
    T_total = int(np.asarray(inputs["L0_attn_x"]).shape[1])
    CH = 512 if T_total >= GRP * 512 else T_total // (GRP * 4)
    nc = build_program(T_total=T_total, CH=CH)
    in_maps = _layout_inputs(inputs, T_total, CH)
    res = run_bass_kernel_spmd(nc, in_maps, core_ids=list(range(8)))
    RNDS = T_total // CH // GRP
    out = np.empty((BATCH, T_total, DM), np.float32)
    for c in range(8):
        b, q = divmod(c, GRP)
        o = np.asarray(res.results[c]["out_fm"])
        for j in range(RNDS):
            g = q + GRP * j
            out[b, g * CH:(g + 1) * CH, :] = o[:, j * CH:(j + 1) * CH].T
    return out


# version 13 (final)
# speedup vs baseline: 1.0283x; 1.0149x over previous
import contextlib
import numpy as np
import concourse.bass as bass
import concourse.mybir as mybir
from concourse.bass_utils import run_bass_kernel_spmd

F32 = mybir.dt.float32
BF16 = mybir.dt.bfloat16
I32 = mybir.dt.int32
AF = mybir.ActivationFunctionType
ALU = mybir.AluOpType
AX = mybir.AxisListType

BATCH = 2
DM = 1024
DI = 2048
DST = 16
GRP = 4
KD = DM // 128
KI = DI // 128
MI = 2 * DI // 128
MF = 4096 // 128
KH = 4096 // 128
HALO = 3
DXW = DI + 2 * DST
DXP = DI + 128


class Ctr:
    def __init__(self, h):
        self.h = h
        self.v = 0


def build_program(T_total=8192, CH=512, debug=False, vec_limit=None, bar_tier=0):
    NCHUNK = T_total // CH
    RNDS = NCHUNK // GRP
    RT = GRP * CH
    SCH = RT // 2
    TLOC = RNDS * CH
    CHH = CH + HALO
    DQ = DI // GRP
    NDT = DQ // 128
    NSLOT = 3
    assert CH <= 512

    nc = bass.Bass("TRN2", target_bir_lowering=False,
                   detect_race_conditions=False, dynamic_dma_scratch_size=16384)

    def EIN(name, shape, dt=F32):
        return nc.dram_tensor(name, shape, dt, kind="ExternalInput")

    x_fm = EIN("x_fm", [DM, RNDS * CHH])
    WinT = EIN("WinT", [DM, 2 * DI])
    WxT = EIN("WxT", [DI, DXW])
    WdtT = EIN("WdtT", [DI, DI])
    WoutT = EIN("WoutT", [DI, DM])
    w1T_myk = EIN("w1T_myk", [128, 4096])
    w2T_myk = EIN("w2T_myk", [512, DM])
    w1_rows = EIN("w1_rows", [512, DM])
    w2_rows = EIN("w2_rows", [128, 4096])
    bconv_t = EIN("bconv_t", [128, KI])
    bdt_t = EIN("bdt_t", [128, KI])
    wcv_t = EIN("wcv_t", [128, KI * 4])
    Alog_my = EIN("Alog_my", [128, NDT * DST])
    D_my = EIN("D_my", [128, NDT])
    b1_t = EIN("b1_t", [128, MF])
    b2_t = EIN("b2_t", [128, KD])
    rank_in = EIN("rank_in", [1, 4], I32)

    out_fm = nc.dram_tensor("out_fm", [DM, TLOC], F32, kind="ExternalOutput")

    dbg = {}
    if debug:
        for nm, n, dt in [("dsend0", DI * CH, BF16), ("xbc0", DXP * CH, BF16),
                          ("y0", DQ * RT, BF16), ("sc", 64, F32),
                          ("h0", 4096 * TLOC, BF16), ("g0", DM * TLOC, BF16),
                          ("s0", RNDS * DM * CH, BF16), ("xn0", RNDS * DM * CHH, BF16),
                          ("wq", 8 * (128 * 4096 + 512 * DM), BF16), ("xq", 2 * DI * CH, BF16),
                          ("q1", 128 * 2 * MF, F32),
                          ("wqin", 128 * 4096 + 512 * DM, BF16),
                          ("swq", 8 * 1280, F32), ("ar1", 160, F32),
                          ("xm", 128 * 16, F32), ("hq0", 128 * 512, BF16),
                          ("q2", 128 * 2 * KD, F32),
                          ("carry", 128 * NDT * DST, F32)]:
            dbg[nm] = nc.dram_tensor("dbg_" + nm, [n], dt, kind="ExternalOutput")

    wbf_in = nc.dram_tensor("wbf_in", [DM * 2 * DI], BF16)
    wbf_x = nc.dram_tensor("wbf_x", [DI * DXW], BF16)
    wbf_dt = nc.dram_tensor("wbf_dt", [DI * DI], BF16)
    wbf_out = nc.dram_tensor("wbf_out", [DI * DM], BF16)
    z_dram = nc.dram_tensor("z_dram", [RNDS * DI * CH], BF16)
    xn_dram = nc.dram_tensor("xn_dram", [RNDS * DM * CHH], BF16)
    h_dram = nc.dram_tensor("h_dram", [4096 * TLOC], BF16)
    g_dram = nc.dram_tensor("g_dram", [DM * TLOC], BF16)
    s_dram = nc.dram_tensor("s_dram", [RNDS * DM * CH], BF16)
    rowD = nc.dram_tensor("rowD", [RNDS * 2 * CHH], F32)
    sclD = nc.dram_tensor("sclD", [64], F32)
    swq_in = nc.dram_tensor("swq_in", [1280], F32)
    swq_out = nc.dram_tensor("swq_out", [8 * 1280], F32)
    WQBLK = 128 * 4096 + 512 * DM
    wq_in = nc.dram_tensor("wq_in", [WQBLK], BF16)
    wq_out = nc.dram_tensor("wq_out", [8 * WQBLK], BF16)
    mx_in = nc.dram_tensor("mx_in", [16], F32)
    mx1_out = nc.dram_tensor("mx1_out", [16], F32)
    mx2_out = nc.dram_tensor("mx2_out", [16], F32)
    flipD = nc.dram_tensor("flipD", [128], F32)
    dSend = nc.dram_tensor("dSend", [DI * CH], BF16)
    xbcSend = nc.dram_tensor("xbcSend", [DXP * CH], BF16)
    ySend = nc.dram_tensor("ySend", [DQ * RT], BF16)
    dAGo = [nc.dram_tensor(f"dAGo{i}", [GRP * DI * CH], BF16) for i in range(2)]
    xbcAGo = [nc.dram_tensor(f"xbcAGo{i}", [GRP * DXP * CH], BF16) for i in range(2)]
    yAGo = [nc.dram_tensor(f"yAGo{i}", [GRP * DQ * RT], BF16) for i in range(2)]

    G4 = [[0, 1, 2, 3], [4, 5, 6, 7]]
    G8 = [[0, 1, 2, 3, 4, 5, 6, 7]]

    plan = {k: [] for k in ("sync", "tensor", "vector", "scalar", "gpsimd")}

    def put(eng, fn):
        plan[eng].append(fn)

    def mark(eng, fn, ctr, n=1):
        ctr.v += n
        plan[eng].append(lambda e, f=fn, c=ctr, m=n: f(e).then_inc(c.h, m))
        return ctr.v

    stack = contextlib.ExitStack()

    def sbuf(name, shape, dt):
        return stack.enter_context(nc.sbuf_tensor(name, shape, dt))

    def mksem(name):
        return Ctr(stack.enter_context(nc.semaphore(name)))

    dSY = Ctr(None)
    dGP = Ctr(None)
    dSC = Ctr(None)
    cc = mksem("cc")
    tM = mksem("tM")
    vM = mksem("vM")
    sM = mksem("sM")
    SEMS = {"dSY": dSY, "dGP": dGP, "dSC": dSC, "cc": cc, "tM": tM, "vM": vM, "sM": sM}

    RQ = 8

    class DmaTrack:
        def __init__(self, name, engname, ctr):
            self.eng = engname
            self.ctr = ctr
            self.S = [mksem(f"{name}S{r}").h for r in range(RQ)]
            self.F = mksem(f"{name}F").h
            self.n = 0
            self.slot_cum = []
            self.uses = [0] * RQ
            self.covered = 0
            self.fval = 0
            self.covers = []

    trS = DmaTrack("qS", "sync", dSY)
    trG = DmaTrack("qG", "gpsimd", dGP)
    trC = DmaTrack("qC", "scalar", dSC)
    TRACKS = {"dSY": trS, "dGP": trG, "dSC": trC}

    def ensure_relay(tr, idx):
        idx = min(idx, tr.n)
        if idx <= tr.covered:
            return
        targets = {}
        for j in range(tr.covered + 1, idx + 1):
            r, cnt = tr.slot_cum[j - 1]
            targets[r] = cnt
        for r, cnt in targets.items():
            plan[tr.eng].append(lambda e, s=tr.S[r], w=16 * cnt: e.wait_ge(s, w))
        tr.fval += 1
        plan[tr.eng].append(lambda e, s=tr.F: e.nop(nofuse=True).then_inc(s, 1))
        tr.covered = idx
        tr.covers.append((idx, tr.fval))

    def fval_for(tr, idx):
        ensure_relay(tr, idx)
        for hi, fv in tr.covers:
            if hi >= idx:
                return fv
        raise AssertionError("relay lookup failed")

    def dma_issue(tr, fn):
        if tr.n + 1 > RQ:
            ensure_relay(tr, tr.n + 1 - RQ)
        tr.n += 1
        r = (tr.n - 1) % RQ
        tr.uses[r] += 1
        tr.slot_cum.append((r, tr.uses[r]))
        plan[tr.eng].append(lambda e, f=fn, s=tr.S[r]: f(e).then_inc(s, 16))
        tr.ctr.v += 16
        return tr.ctr.v

    BAR = mksem("BAR")
    nbar = [0]
    ENGS = ("sync", "tensor", "vector", "scalar", "gpsimd")

    def quiesce(tier=1):
        if tier > bar_tier and tier >= 0:
            return
        for tr in (trS, trG, trC):
            ensure_relay(tr, tr.n)
        if cc.v > 0:
            plan["gpsimd"].append(lambda e, c=cc, w=cc.v: e.wait_ge(c.h, w))
        nbar[0] += 1
        tgt = 5 * nbar[0]
        for eng in ENGS:
            plan[eng].append(lambda e, s=BAR.h: e.nop(nofuse=True).then_inc(s, 1))
            plan[eng].append(lambda e, s=BAR.h, w=tgt: e.wait_ge(s, w))

    def wait(eng, sem, v=None):
        name = sem if isinstance(sem, str) else None
        ctr = SEMS[sem] if isinstance(sem, str) else sem
        vv = ctr.v if v is None else v
        if vv <= 0:
            return vv
        if name in TRACKS:
            tr = TRACKS[name]
            idx = (vv + 15) // 16
            fv = fval_for(tr, idx)
            plan[eng].append(lambda e, s=tr.F, w=fv: e.wait_ge(s, w))
        else:
            plan[eng].append(lambda e, c=ctr, w=vv: e.wait_ge(c.h, w))
        return vv

    onesc_b = sbuf("onesc_b", [128, 1], BF16)
    rank_sb = sbuf("rank_sb", [1, 4], I32)
    bconv_sb = sbuf("bconv_sb", [128, KI], F32)
    bdt_sb = sbuf("bdt_sb", [128, KI], F32)
    wcv_sb = sbuf("wcv_sb", [128, KI * 4], F32)
    A_sb = sbuf("A_sb", [128, NDT * DST], F32)
    D_sb = sbuf("D_sb", [128, NDT], F32)
    b1_sb = sbuf("b1_sb", [128, MF], F32)
    b2_sb = sbuf("b2_sb", [128, KD], F32)
    sw1_sb = sbuf("sw1_sb", [128, MF], F32)
    sw2_sb = sbuf("sw2_sb", [128, KD], F32)
    scale1_sb = sbuf("scale1_sb", [128, MF], F32)
    bq1_sb = sbuf("bq1_sb", [128, MF], F32)
    scale2_sb = sbuf("scale2_sb", [128, KD], F32)
    bq2_sb = sbuf("bq2_sb", [128, KD], F32)
    scl_sb = sbuf("scl_sb", [128, 4], F32)
    qs_sb = sbuf("qs_sb", [128, 24], F32)
    srow = sbuf("srow", [1, 6 * CHH], F32)
    carry = sbuf("carry", [128, NDT * DST], F32)
    xmax_sb = sbuf("xmax_sb", [128, 2], F32)
    r16 = sbuf("r16", [1, 16], F32)
    rcp5_sb = sbuf("rcp5_sb", [128, 5], F32)
    rcp1_sb = sbuf("rcp1_sb", [1, 1], F32)
    dbgx_sb = sbuf("dbgx_sb", [128, 16], F32)

    xf32 = sbuf("xf32", [128, 2 * CHH], F32)
    xsqb = sbuf("xsqb", [128, 2 * CHH], BF16)
    xb_sb = sbuf("xb_sb", [128, KD * CHH], BF16)
    xn_sb = sbuf("xn_sb", [128, KD * CHH], BF16)
    rbc_sb = sbuf("rbc_sb", [128, CHH], BF16)
    mbc_sb = sbuf("mbc_sb", [128, CHH], BF16)
    WF = max(2 * 2048, 2 * SCH)
    wf32 = sbuf("wf32", [128, WF], F32)
    wbf = sbuf("wbf", [128, 2 * 2048], BF16)
    xin_sb = sbuf("xin_sb", [128, 3 * CHH], BF16)
    xcv_sb = sbuf("xcv_sb", [128, KI * CH], BF16)
    drw_sb = sbuf("drw_sb", [128, KI * CH], BF16)
    stg_sb = sbuf("stg_sb", [128, 2 * CH], BF16)
    stg2_sb = sbuf("stg2_sb", [128, 2 * CH], BF16)
    bcstg = sbuf("bcstg", [32, CH], BF16)

    dR_sb = sbuf("dR_sb", [128, NSLOT * NDT * SCH], BF16)
    xcR_sb = sbuf("xcR_sb", [128, NSLOT * NDT * SCH], BF16)
    b_sb = sbuf("b_sb", [128, SCH], BF16)
    st_sb = sbuf("st_sb", [128, SCH], BF16)
    c128_sb = sbuf("c128_sb", [128, 128], F32)
    pr_sb = sbuf("pr_sb", [128, SCH], BF16)
    Bex_sb = sbuf("Bex_sb", [128, 2 * SCH], BF16)
    Cex_sb = sbuf("Cex_sb", [128, 2 * SCH], BF16)
    yacc_sb = sbuf("yacc_sb", [128, NDT * SCH], BF16)
    yfin_sb = sbuf("yfin_sb", [128, 2 * SCH], BF16)

    yt_sb = sbuf("yt_sb", [128, KI * CH], BF16)
    szt_sb = sbuf("szt_sb", [128, 2 * CH], BF16)

    iq_sb = sbuf("iq_sb", [128, max(2 * CH, 1024)], I32)
    wqb_sb = sbuf("wqb_sb", [128, 1024], BF16)
    invbc_sb = wf32[:, 1024:2048]
    wqi_sb = iq_sb
    xro_sb = sbuf("xro_sb", [128, CH], F32)
    out_sb = sbuf("out_sb", [128, CH], F32)

    ps = [stack.enter_context(nc.psum_tensor(f"ps{i}", [128, 512], F32)) for i in range(8)]

    regs = {}

    def dmaS(out, in_):
        return dma_issue(trS, lambda e, o=out, i=in_: e.dma_start(out=o, in_=i))

    def dma_slot(eng, buf, slot, out, in_):
        ctr = buf.sems[slot]
        ctr.v += 16
        plan[eng].append(lambda e, o=out, i=in_, c=ctr: e.dma_start(out=o, in_=i)
                         .then_inc(c.h, 16))
        return ctr, ctr.v

    def dmaG(out, in_):
        return dma_issue(trG, lambda e, o=out, i=in_: e.dma_start(out=o, in_=i))

    def dmaC(out, in_):
        return dma_issue(trC, lambda e, o=out, i=in_: e.dma_start(out=o, in_=i))

    def dmaS_nc(out, in_):
        return dma_issue(trS, lambda e, o=out, i=in_: e.dma_start(
            out=o, in_=i, allow_slow_non_contiguous=True))

    def dmaG_nc(out, in_):
        return dma_issue(trG, lambda e, o=out, i=in_: e.dma_start(
            out=o, in_=i, allow_slow_non_contiguous=True))

    class Buf:
        def __init__(self, n, name=None):
            self.g = [[] for _ in range(n)]
            self.sems = [mksem(f"{name}{i}") for i in range(n)] if name else None

        def acquire(self, eng, slot):
            for s, v in self.g[slot]:
                wait(eng, s, v)
            self.g[slot] = []

        def consumed(self, slot, sem_name, v):
            self.g[slot].append((sem_name, v))

    xf32B = Buf(2, "bXF")
    wbfB = Buf(2, "bWB")
    wf32B = Buf(2, "bWF")
    stgB = Buf(2)
    stg2B = Buf(2)
    xinB = Buf(3)
    aB = Buf(2)
    BexB = Buf(2)
    CexB = Buf(2)
    yfinB = Buf(2)
    iqB = Buf(2)
    xroB = Buf(1)
    outB = Buf(1)
    sztB = Buf(2)

    bank_ev = [("sM", 0)] * 8

    def te_take(bk):
        s, v = bank_ev[bk]
        if v > 0:
            wait("tensor", s, v)

    def matgroup(emitters):
        for f in emitters[:-1]:
            put("tensor", f)
        return mark("tensor", emitters[-1], tM)

    dmaS(rank_sb[:], rank_in.ap())
    dmaS(bconv_sb[:], bconv_t.ap())
    dmaS(bdt_sb[:], bdt_t.ap())
    dmaS(wcv_sb[:], wcv_t.ap())
    dmaS(A_sb[:], Alog_my.ap())
    dmaS(D_sb[:], D_my.ap())
    dmaS(b1_sb[:], b1_t.ap())
    ld_setup = dmaS(b2_sb[:], b2_t.ap())

    wait("sync", "dSY", 16)

    def _ld_rank(e):
        regs["q"] = e.value_load(rank_sb[0:1, 0:1])
        regs["qCH"] = e.value_load(rank_sb[0:1, 1:2])
        regs["qn4"] = e.value_load(rank_sb[0:1, 2:3])
    put("sync", _ld_rank)

    put("vector", lambda e: e.memset(onesc_b[:], 1.0))
    put("vector", lambda e: e.memset(carry[:], 0.0))
    put("vector", lambda e: e.memset(r16[:], 0.0))
    put("vector", lambda e: e.memset(srow[0:1, 4 * CHH:4 * CHH + 1], 1e-5))
    mark("vector", lambda e: e.memset(xmax_sb[:], -1e30), vM)
    m_z = mark("vector", lambda e: e.memset(yacc_sb[0:96, 0:CH], 0.0), vM)
    wait("gpsimd", "vM", m_z)
    g_pad = dmaG(bass.AP(xbcSend, DXW * CH, [[CH, 96], [1, CH]]), yacc_sb[0:96, 0:CH])
    wait("vector", "dGP", g_pad)

    wait("scalar", "dSY", ld_setup)
    put("scalar", lambda e: e.activation(A_sb[:], A_sb[:], AF.Exp))
    mark("scalar", lambda e: e.mul(A_sb[:], A_sb[:], -1.0), sM)

    for i4 in range(4):
        sl = i4 % 2
        ld = dmaS(wf32[:, sl * 2048: sl * 2048 + DM], w1_rows.ap()[i4 * 128:(i4 + 1) * 128, :])
        wait("vector", "dSY", ld)
        put("vector", lambda e, i=i4, s=sl: e.tensor_reduce(
            qs_sb[:, i:i + 1], wf32[:, s * 2048: s * 2048 + DM],
            axis=AX.X, op=ALU.max, apply_absolute_value=True))
    ld = dmaS(wf32[:, 0:2048], w2_rows.ap()[:, 0:2048])
    ld = dmaS(wf32[:, 2048:4096], w2_rows.ap()[:, 2048:4096])
    wait("vector", "dSY", ld)
    put("vector", lambda e: e.tensor_reduce(qs_sb[:, 4:5], wf32[:, 0:2048],
                                            axis=AX.X, op=ALU.max, apply_absolute_value=True))
    v_hz = mark("vector", lambda e: e.tensor_reduce(qs_sb[:, 5:6], wf32[:, 2048:4096],
                                                    axis=AX.X, op=ALU.max,
                                                    apply_absolute_value=True), vM)
    wait("vector", "vM", v_hz)
    v_hz = mark("vector", lambda e: e.tensor_max(qs_sb[:, 4:5], qs_sb[:, 4:5],
                                                 qs_sb[:, 5:6]), vM)
    wait("vector", "vM", v_hz)
    put("vector", lambda e: e.tensor_scalar_mul(qs_sb[:, 8:13], qs_sb[:, 0:5], 1.0 / 127.0))
    v_hz = mark("vector", lambda e: e.reciprocal(rcp5_sb[:], qs_sb[:, 0:5]), vM)
    wait("vector", "vM", v_hz)
    v_sw = mark("vector", lambda e: e.tensor_scalar_mul(qs_sb[:, 16:21], rcp5_sb[:],
                                                        127.0), vM)
    wait("gpsimd", "vM", v_sw)
    dmaG_nc(bass.AP(swq_in, 0, [[1, 128], [128, 4]]), qs_sb[:, 8:12])
    dmaG_nc(bass.AP(swq_in, 512, [[1, 128], [128, 4]]), qs_sb[:, 16:20])
    dmaG_nc(bass.AP(swq_in, 1024, [[1, 128], [128, 1]]), qs_sb[:, 12:13])
    g_pack = dmaG_nc(bass.AP(swq_in, 1152, [[1, 128], [128, 1]]), qs_sb[:, 20:21])
    wait("gpsimd", "dGP", g_pack)
    cc_sw = mark("gpsimd", lambda e: e.collective_compute(
        "AllGather", ALU.bypass, ins=[swq_in.ap()], outs=[swq_out.ap()],
        replica_groups=G8), cc)

    wait("sync", "cc", cc_sw)
    for r in range(8):
        dmaS_nc(sw1_sb[:, 4 * r:4 * r + 4], bass.AP(swq_out, r * 1280, [[1, 128], [128, 4]]))
    dmaS_nc(sw2_sb[:, 0:KD], bass.AP(swq_out, 1024, [[1, 128], [1280, KD]]))

    v_q_prev = 0
    g_prev = 0
    for p4 in range(4):
        if v_q_prev > 0:
            wait("sync", "vM", v_q_prev)
        bcv = dmaS(wf32[:, 1024:2048], bass.AP(swq_out, 2 * p4 * 1280 + 512,
                                               [[0, 128], [1280, 2], [1, 512]]))
        off = (p4 % 2) * 2048
        ldw = dmaS(wf32[:, off:off + 1024], w1T_myk.ap()[:, p4 * 1024:(p4 + 1) * 1024])
        wait("vector", "dSY", max(bcv, ldw))
        v_q = mark("vector", lambda e, o=off: e.tensor_tensor(
            wf32[:, o:o + 1024], wf32[:, o:o + 1024], wf32[:, 1024:2048], ALU.mult), vM)
        v_q_prev = v_q
        wait("scalar", "vM", v_q)
        put("scalar", lambda e, o=off: e.copy(wqi_sb[:, 0:1024], wf32[:, o:o + 1024]))
        if g_prev > 0:
            wait("scalar", "dGP", g_prev)
        s_q = mark("scalar", lambda e: e.copy(wqb_sb[:], wqi_sb[:]), sM)
        wait("gpsimd", "sM", s_q)
        g_prev = dmaG(bass.AP(wq_in, p4 * 1024, [[4096, 128], [1, 1024]]), wqb_sb[:])
    wait("sync", "vM", v_q_prev)
    bcv2 = dmaS(wf32[:, 1024:2048], bass.AP(swq_out, 1152, [[0, 128], [1280, 8], [1, 128]]))
    for i4 in range(4):
        if i4 >= 2:
            wait("sync", "vM", v_q_prev)
        off = (i4 % 2) * 2048
        ldw = dmaS(wf32[:, off:off + 1024], w2T_myk.ap()[i4 * 128:(i4 + 1) * 128, :])
        wait("vector", "dSY", max(bcv2, ldw))
        v_q = mark("vector", lambda e, o=off: e.tensor_tensor(
            wf32[:, o:o + 1024], wf32[:, o:o + 1024], wf32[:, 1024:2048], ALU.mult), vM)
        v_q_prev = v_q
        wait("scalar", "vM", v_q)
        put("scalar", lambda e, o=off: e.copy(wqi_sb[:, 0:1024], wf32[:, o:o + 1024]))
        wait("scalar", "dGP", g_prev)
        s_q = mark("scalar", lambda e: e.copy(wqb_sb[:], wqi_sb[:]), sM)
        wait("gpsimd", "sM", s_q)
        g_prev = dmaG(bass.AP(wq_in, 128 * 4096 + i4 * 131072, [[1024, 128], [1, 1024]]),
                      wqb_sb[:])
    wf32B.consumed(0, "vM", v_q_prev)
    wf32B.consumed(1, "vM", v_q_prev)
    wait("gpsimd", "dGP", dGP.v)
    cc_wq = mark("gpsimd", lambda e: e.collective_compute(
        "AllGather", ALU.bypass, ins=[wq_in.ap()], outs=[wq_out.ap()],
        replica_groups=G8), cc)

    for m in range(KD):
        sl = m % 2
        wf32B.acquire("sync", sl)
        ldw = dmaS(wf32[:, sl * 2048:sl * 2048 + 2048],
                   bass.AP(WoutT.ap().tensor, m * 128, [[DM, 128], [128 * DM, KI], [1, 128]]))
        wbfB.acquire("scalar", sl)
        wait("scalar", "dSY", ldw)
        s_c = mark("scalar", lambda e, s=sl: e.copy(
            wbf[:, s * 2048:s * 2048 + 2048], wf32[:, s * 2048:s * 2048 + 2048]), sM)
        wf32B.consumed(sl, "sM", s_c)
        wait("gpsimd", "sM", s_c)
        g_sp = dmaG(bass.AP(wbf_out, m * 128, [[DM, 128], [128 * DM, KI], [1, 128]]),
                    wbf[:, sl * 2048:sl * 2048 + 2048])
        wbfB.consumed(sl, "dGP", g_sp)
    quiesce(tier=-1)
    setup_v = vM.v

    recv_half = {}
    cm_last = {}
    scan_seq = []
    scan_mk = {}
    a_seq = []
    bex_marks = {}
    bex_seq = []
    yg_consumed = {}
    ag_mark = {}
    ysend_mark = {}
    xcsend_mark = {}
    prev = {"t_stats": 0, "v_xn": 0, "win_last": 0, "wx_last": 0, "wdt_last": 0,
            "xn_spill": 0, "conv_last": 0, "wout_last": 0, "v_yg": 0,
            "bex_round_last": {}, "recv_last": {}, "rows_dma": 0, "v_xm": 0}
    conv_mark = {}
    zmark = {}

    def w_stream(j, f32_t, bf_t, row_stride, m, mw, kk, off=None):
        sl = m % 2
        n = kk * mw
        if off is None:
            off = m * mw
        dst = wbf[:, sl * 2048: sl * 2048 + n]
        if j == 0:
            wf32B.acquire("sync", sl)
            lc, lv = dma_slot("sync", wf32B, sl, wf32[:, sl * 2048:sl * 2048 + n],
                              bass.AP(f32_t, off,
                                      [[row_stride, 128], [128 * row_stride, kk],
                                       [1, mw]]))
            wbfB.acquire("scalar", sl)
            wait("scalar", lc, lv)
            s_c = mark("scalar", lambda e, s=sl, n=n: e.copy(
                wbf[:, s * 2048:s * 2048 + n], wf32[:, s * 2048:s * 2048 + n]), sM)
            wf32B.consumed(sl, "sM", s_c)
            wait("gpsimd", "sM", s_c)
            g_sp = dmaG(bass.AP(bf_t, off,
                                [[row_stride, 128], [128 * row_stride, kk], [1, mw]]), dst)
            wbfB.consumed(sl, "dGP", g_sp)
            return dst, ("sM", s_c), sl
        wbfB.acquire("sync", sl)
        lc, lv = dma_slot("sync", wbfB, sl, dst,
                          bass.AP(bf_t, off,
                                  [[row_stride, 128], [128 * row_stride, kk], [1, mw]]))
        return dst, (lc, lv), sl

    def emit_recv(j):
        jb = j % 2
        wait("sync", "cc", ag_mark[("d", j)])
        wait("sync", "cc", ag_mark[("x", j)])
        dview = dAGo[jb].ap().rearrange("(r n p t) -> p n r t", r=GRP, p=128, t=CH)
        xview = xbcAGo[jb].ap().rearrange("(r n p t) -> p n r t", r=GRP, p=128, t=CH)
        for h in range(2):
            gslot = (2 * j + h) % NSLOT
            if 2 * j + h >= NSLOT:
                pidx = 2 * j + h - NSLOT
                pj, ph = pidx // 2, pidx % 2
                wait("sync", "vM", cm_last[(pj, ph, DST - 1)])
                wait("sync", "vM", yfin_last.get((pj, ph), 0))
            for rr2 in range(2):
                def fd(e, h=h, gs=gslot, dv=dview, rr2=rr2):
                    o = dR_sb[:, gs * NDT * SCH:(gs + 1) * NDT * SCH].rearrange(
                        "p (n x) -> p n x", n=NDT, x=SCH)[:, :, rr2 * CH:(rr2 + 1) * CH]
                    return e.dma_start(
                        out=o, in_=dv[:, bass.ds(regs["qn4"], NDT),
                                      2 * h + rr2:2 * h + rr2 + 1, :])
                dma_issue(trS, fd)
                def fx(e, h=h, gs=gslot, xv=xview, rr2=rr2):
                    o = xcR_sb[:, gs * NDT * SCH:(gs + 1) * NDT * SCH].rearrange(
                        "p (n x) -> p n x", n=NDT, x=SCH)[:, :, rr2 * CH:(rr2 + 1) * CH]
                    return e.dma_start(
                        out=o, in_=xv[:, bass.ds(regs["qn4"], NDT),
                                      2 * h + rr2:2 * h + rr2 + 1, :])
                recv_half[(j, h)] = dma_issue(trS, fx)
        prev["recv_last"][j] = recv_half[(j, 1)]

    yfin_last = {}

    def emit_scan_half_gen(j, h):
        jb = j % 2
        gslot = (2 * j + h) % NSLOT
        wait("scalar", "dSY", recv_half[(j, h)])
        wait("vector", "dSY", recv_half[(j, h)])
        for s in range(DST):
            eb = len(bex_seq) % 2
            if len(bex_seq) >= 2:
                pk = bex_seq[-2]
                BexB.acquire("scalar", eb)
                CexB.acquire("scalar", eb)
            dmaC(Bex_sb[:, eb * SCH:(eb + 1) * SCH],
                 bass.AP(xbcAGo[jb], (2 * h) * DXP * CH + (DI + s) * CH,
                         [[0, 128], [DXP * CH, 2], [1, CH]]))
            bx = dmaC(Cex_sb[:, eb * SCH:(eb + 1) * SCH],
                      bass.AP(xbcAGo[jb], (2 * h) * DXP * CH + (DI + DST + s) * CH,
                              [[0, 128], [DXP * CH, 2], [1, CH]]))
            bex_marks[(j, h, s)] = bx
            bex_seq.append((j, h, s))
            wait("vector", "dSC", bx)
            for dt in range(NDT):
                ai = len(a_seq) % 2
                aB.acquire("scalar", ai)
                src = dR_sb[:, (gslot * NDT + dt) * SCH:(gslot * NDT + dt + 1) * SCH]
                av = wf32[:, ai * SCH:(ai + 1) * SCH]
                a_v = mark("scalar", lambda e, src=src, av=av, dt=dt, s=s: e.activation(
                    av, src, AF.Exp, scale=A_sb[:, dt * DST + s:dt * DST + s + 1]), sM)
                a_seq.append((j, h, s, dt))
                xc = xcR_sb[:, (gslot * NDT + dt) * SCH:(gslot * NDT + dt + 1) * SCH]
                Bx = Bex_sb[:, eb * SCH:(eb + 1) * SCH]
                Cx = Cex_sb[:, eb * SCH:(eb + 1) * SCH]
                wait("vector", "sM", a_v)
                bbv = mark("vector", lambda e, xc=xc, Bx=Bx: e.tensor_mul(b_sb[:], xc, Bx), vM)
                sc_v = mark("vector", lambda e, av=av, dt=dt, s=s: e.tensor_tensor_scan(
                    st_sb[:], av, b_sb[:], carry[:, dt * DST + s:dt * DST + s + 1],
                    ALU.mult, ALU.add), vM)
                scan_mk[(j, h, s, dt)] = sc_v
                aB.consumed(ai, "vM", sc_v)
                wait("vector", "vM", sc_v)
                v_cw = mark("vector", lambda e: e.tensor_copy(
                    c128_sb[:], st_sb[:, SCH - 128:SCH]), vM)
                wait("vector", "vM", v_cw)
                put("vector", lambda e, dt=dt, s=s: e.tensor_copy(
                    carry[:, dt * DST + s:dt * DST + s + 1], c128_sb[:, 127:128]))
                cmv = mark("vector", lambda e, Cx=Cx: e.tensor_mul(pr_sb[:], st_sb[:], Cx), vM)
                if dt == NDT - 1:
                    cm_last[(j, h, s)] = cmv
                    BexB.consumed(eb, "vM", cmv)
                    CexB.consumed(eb, "vM", cmv)
                ya = yacc_sb[:, dt * SCH:(dt + 1) * SCH]
                if s == 0:
                    put("vector", lambda e, ya=ya: e.tensor_copy(ya, pr_sb[:]))
                else:
                    put("vector", lambda e, ya=ya: e.tensor_add(ya, pr_sb[:], ya))
            yield
        if h == 0 and j >= 1:
            wait("gpsimd", "cc", ag_mark[("y", j - 1)])
        for dt in range(NDT):
            xc = xcR_sb[:, (gslot * NDT + dt) * SCH:(gslot * NDT + dt + 1) * SCH]
            ya = yacc_sb[:, dt * SCH:(dt + 1) * SCH]
            ysl = dt % 2
            yfinB.acquire("vector", ysl)
            yf = yfin_sb[:, ysl * SCH:(ysl + 1) * SCH]
            v_yf = mark("vector", lambda e, xc=xc, ya=ya, yf=yf, dt=dt:
                        e.scalar_tensor_tensor(yf, xc, D_sb[:, dt:dt + 1], ya,
                                               ALU.mult, ALU.add), vM)
            yfin_last[(j, h)] = v_yf
            wait("gpsimd", "vM", v_yf)
            g_ys = dmaG(bass.AP(ySend, dt * 128 * RT + h * SCH, [[RT, 128], [1, SCH]]), yf)
            yfinB.consumed(ysl, "dGP", g_ys)
        if h == 1:
            prev["bex_round_last"][j] = dSC.v
            wait("gpsimd", "dGP", dGP.v)
            if j >= 2:
                wait("gpsimd", "vM", yg_consumed[j - 2])
            ag_mark[("y", j)] = mark("gpsimd", lambda e, jb=jb: e.collective_compute(
                "AllGather", ALU.bypass, ins=[ySend.ap()], outs=[yAGo[jb].ap()],
                replica_groups=G4), cc)
        yield

    def emit_scan_half(j, h):
        for _ in emit_scan_half_gen(j, h):
            pass

    def emit_gating(j):
        jb = j % 2
        wait("sync", "cc", ag_mark[("y", j)])
        if prev["wout_last"] > 0:
            wait("sync", "tM", prev["wout_last"])
            wait("sync", "vM", prev["v_yg"])
        yv = yAGo[jb].ap().rearrange("(r n p t) -> p r n t", r=GRP, p=128, t=RT)
        y_ld = 0
        for r in range(GRP):
            def fy(e, r=r, yv=yv):
                return e.dma_start(
                    out=yt_sb[:, r * NDT * CH:(r + 1) * NDT * CH].rearrange(
                        "p (r n t) -> p r n t", r=1, n=NDT, t=CH),
                    in_=yv[:, r:r + 1, :, bass.ds(regs["qCH"], CH)])
            y_ld = dma_issue(trS, fy)
        wait("sync", "dGP", zmark[j])
        for kt in range(KI):
            zsl = kt % 2
            sztB.acquire("sync", zsl)
            ldz = dmaS(szt_sb[:, zsl * CH:(zsl + 1) * CH],
                       bass.AP(z_dram, (j * DI + kt * 128) * CH, [[CH, 128], [1, CH]]))
            wait("scalar", "dSY", ldz)
            s_z = mark("scalar", lambda e, zs=zsl: e.activation(
                szt_sb[:, zs * CH:(zs + 1) * CH], szt_sb[:, zs * CH:(zs + 1) * CH],
                AF.Silu), sM)
            wait("vector", "sM", s_z)
            if kt == 0:
                wait("vector", "dSY", y_ld)
            v_g = mark("vector", lambda e, kt=kt, zs=zsl: e.tensor_mul(
                yt_sb[:, kt * CH:(kt + 1) * CH], yt_sb[:, kt * CH:(kt + 1) * CH],
                szt_sb[:, zs * CH:(zs + 1) * CH]), vM)
            sztB.consumed(zsl, "vM", v_g)
        v_yg = vM.v
        yg_consumed[j] = v_yg
        prev["v_yg"] = v_yg
        for m in range(KD):
            sl = m % 2
            wbfB.acquire("sync", sl)
            lc3, lv3 = dma_slot("sync", wbfB, sl, wbf[:, sl * 2048:sl * 2048 + 2048],
                                bass.AP(wbf_out, m * 128,
                                        [[DM, 128], [128 * DM, KI], [1, 128]]))
            bk = 5 if m % 2 == 0 else 6
            te_take(bk)
            wait("tensor", lc3, lv3)
            wait("tensor", "vM", v_yg)
            t_mm = matgroup([
                (lambda e, k=k, b=bk, sl=sl: e.matmul(
                    ps[b][:, 0:CH], wbf[:, sl * 2048 + k * 128:sl * 2048 + (k + 1) * 128],
                    yt_sb[:, k * CH:(k + 1) * CH], start=(k == 0), stop=(k == KI - 1)))
                for k in range(KI)])
            wbfB.consumed(sl, "tM", t_mm)
            wait("scalar", "tM", t_mm)
            ssl = m % 2
            stgB.acquire("scalar", ssl)
            s_ev = mark("scalar", lambda e, b=bk, ss=ssl: e.copy(
                stg_sb[:, ss * CH:(ss + 1) * CH], ps[b][:, 0:CH]), sM)
            bank_ev[bk] = ("sM", s_ev)
            wait("gpsimd", "sM", s_ev)
            g_sp = dmaG(bass.AP(s_dram, (j * DM + m * 128) * CH, [[CH, 128], [1, CH]]),
                        stg_sb[:, ssl * CH:(ssl + 1) * CH])
            stgB.consumed(ssl, "dGP", g_sp)
            prev["wout_last"] = t_mm

    first_wload = True
    for j in range(RNDS):
        jb = j % 2
        co = j * CHH

        if j >= 1:
            emit_recv(j - 1)

        t_stats = 0
        for k in range(KD):
            sl = k % 2
            xf32B.acquire("sync", sl)
            lcx, lvx = dma_slot("sync", xf32B, sl, xf32[:, sl * CHH:(sl + 1) * CHH],
                                x_fm.ap()[k * 128:(k + 1) * 128, co:co + CHH])
            wait("scalar", lcx, lvx)
            if k == 0:
                wait("scalar", "tM", prev["t_stats"])
                wait("scalar", "vM", prev["v_xn"])
            put("scalar", lambda e, s=sl, k=k: e.copy(
                xb_sb[:, k * CHH:(k + 1) * CHH], xf32[:, s * CHH:(s + 1) * CHH]))
            s_k = mark("scalar", lambda e, s=sl: e.activation(
                xsqb[:, s * CHH:(s + 1) * CHH], xf32[:, s * CHH:(s + 1) * CHH],
                AF.Square), sM)
            xf32B.consumed(sl, "sM", s_k)
            wait("tensor", "sM", s_k)
            if k == 0:
                te_take(2), te_take(3), te_take(4), te_take(5)
            put("tensor", lambda e, k=k: e.matmul(
                ps[2][0:1, 0:CH], onesc_b[:, 0:1], xb_sb[:, k * CHH + HALO:(k + 1) * CHH],
                start=(k == 0), stop=(k == KD - 1)))
            put("tensor", lambda e, k=k, s=sl: e.matmul(
                ps[3][0:1, 0:CH], onesc_b[:, 0:1], xsqb[:, s * CHH + HALO:(s + 1) * CHH],
                start=(k == 0), stop=(k == KD - 1)))
            put("tensor", lambda e, k=k: e.matmul(
                ps[4][0:1, 0:HALO], onesc_b[:, 0:1], xb_sb[:, k * CHH:k * CHH + HALO],
                start=(k == 0), stop=(k == KD - 1)))
            t_stats = mark("tensor", lambda e, k=k, s=sl: e.matmul(
                ps[5][0:1, 0:HALO], onesc_b[:, 0:1], xsqb[:, s * CHH:s * CHH + HALO],
                start=(k == 0), stop=(k == KD - 1)), tM)
        prev["t_stats"] = t_stats
        wait("scalar", "tM", t_stats)
        put("scalar", lambda e: e.copy(srow[0:1, HALO:CHH], ps[2][0:1, 0:CH]))
        put("scalar", lambda e: e.copy(srow[0:1, CHH + HALO:2 * CHH], ps[3][0:1, 0:CH]))
        put("scalar", lambda e: e.copy(srow[0:1, 0:HALO], ps[4][0:1, 0:HALO]))
        put("scalar", lambda e: e.copy(srow[0:1, CHH:CHH + HALO], ps[5][0:1, 0:HALO]))
        put("scalar", lambda e: e.mul(srow[0:1, 2 * CHH:3 * CHH], srow[0:1, 0:CHH], 1.0 / DM))
        s_st = mark("scalar", lambda e: e.mul(srow[0:1, CHH:2 * CHH],
                                              srow[0:1, CHH:2 * CHH], 1.0 / DM), sM)
        bank_ev[2] = bank_ev[3] = bank_ev[4] = bank_ev[5] = ("sM", s_st)
        wait("vector", "sM", s_st)
        put("vector", lambda e: e.tensor_mul(srow[0:1, 5 * CHH:6 * CHH],
                                             srow[0:1, 2 * CHH:3 * CHH],
                                             srow[0:1, 2 * CHH:3 * CHH]))
        v_var = mark("vector", lambda e: e.tensor_sub(srow[0:1, CHH:2 * CHH],
                                                      srow[0:1, CHH:2 * CHH],
                                                      srow[0:1, 5 * CHH:6 * CHH]), vM)
        wait("scalar", "vM", v_var)
        put("scalar", lambda e: e.activation(srow[0:1, 3 * CHH:4 * CHH],
                                             srow[0:1, CHH:2 * CHH], AF.Ln,
                                             bias=srow[0:1, 4 * CHH:4 * CHH + 1]))
        s_r = mark("scalar", lambda e: e.activation(srow[0:1, 3 * CHH:4 * CHH],
                                                    srow[0:1, 3 * CHH:4 * CHH], AF.Exp,
                                                    scale=-0.5), sM)
        wait("sync", "sM", s_r)
        wait("sync", "dSY", prev["rows_dma"])
        dmaS(bass.AP(rowD, j * 2 * CHH, [[CHH, 1], [1, CHH]]), srow[0:1, 3 * CHH:4 * CHH])
        rr = dmaS(bass.AP(rowD, j * 2 * CHH + CHH, [[CHH, 1], [1, CHH]]),
                  srow[0:1, 2 * CHH:3 * CHH])
        if prev["v_xn"] > 0:
            wait("gpsimd", "vM", prev["v_xn"])
        wait("gpsimd", "dSY", rr)
        rb = dmaG(rbc_sb[:], bass.AP(rowD, j * 2 * CHH, [[0, 128], [1, CHH]]))
        mb = dmaG(mbc_sb[:], bass.AP(rowD, j * 2 * CHH + CHH, [[0, 128], [1, CHH]]))
        prev["rows_dma"] = rr
        wait("vector", "dGP", mb)
        if prev["win_last"] > 0:
            wait("vector", "tM", prev["win_last"])
        if prev["xn_spill"] > 0:
            wait("vector", "dGP", prev["xn_spill"])
        if j == 0:
            wait("vector", "vM", setup_v)
        for k in range(KD):
            put("vector", lambda e, k=k: e.tensor_sub(
                xn_sb[:, k * CHH:(k + 1) * CHH], xb_sb[:, k * CHH:(k + 1) * CHH], mbc_sb[:]))
            put("vector", lambda e, k=k: e.tensor_mul(
                xn_sb[:, k * CHH:(k + 1) * CHH], xn_sb[:, k * CHH:(k + 1) * CHH], rbc_sb[:]))
        for k in range(KD):
            v_hz = mark("vector", lambda e, k=k: e.tensor_reduce(
                qs_sb[:, 22:23], xn_sb[:, k * CHH + HALO:(k + 1) * CHH],
                axis=AX.X, op=ALU.max), vM)
            wait("vector", "vM", v_hz)
            put("vector", lambda e: e.tensor_max(xmax_sb[:, 0:1], xmax_sb[:, 0:1],
                                                 qs_sb[:, 22:23]))
        if debug and j == 0:
            put("vector", lambda e: e.tensor_copy(dbgx_sb[:, 0:2], xmax_sb[:, 0:2]))
            put("vector", lambda e: e.tensor_copy(dbgx_sb[:, 6:7], qs_sb[:, 22:23]))
        if debug and j == RNDS - 1:
            put("vector", lambda e: e.tensor_copy(dbgx_sb[:, 2:4], xmax_sb[:, 0:2]))
        v_xn = mark("vector", lambda e: e.tensor_copy(qs_sb[:, 22:23], qs_sb[:, 22:23]), vM)
        prev["v_xn"] = v_xn
        prev["v_xm"] = v_xn
        wait("gpsimd", "vM", v_xn)
        prev["xn_spill"] = dmaG(
            bass.AP(xn_dram, j * DM * CHH, [[CHH, 128], [128 * CHH, KD], [1, CHH]]), xn_sb[:])
        quiesce(tier=2)

        if first_wload:
            wait("sync", "vM", setup_v)
            first_wload = False
        win_last = 0
        for m in range(MI):
            wsl, wready, slot = w_stream(j, WinT.ap().tensor, wbf_in, 2 * DI, m, 128, KD)
            bk = m % 2
            te_take(bk)
            wait("tensor", wready[0], wready[1])
            wait("tensor", "vM", v_xn)
            t_mm = matgroup([
                (lambda e, k=k, w=wsl, b=bk: e.matmul(
                    ps[b][:, 0:CH], w[:, k * 128:(k + 1) * 128],
                    xn_sb[:, k * CHH + HALO:(k + 1) * CHH],
                    start=(k == 0), stop=(k == KD - 1)))
                for k in range(KD)])
            t_last = t_mm
            if m < KI:
                te_take(6)
                t_last = matgroup([
                    (lambda e, k=k, w=wsl, m=m: e.matmul(
                        ps[6][:, 3 * m:3 * m + HALO], w[:, k * 128:(k + 1) * 128],
                        xn_sb[:, k * CHH:k * CHH + HALO],
                        start=(k == 0), stop=(k == KD - 1)))
                    for k in range(KD)])
            wbfB.consumed(slot, "tM", t_last)
            win_last = t_last
            wait("scalar", "tM", t_last)
            if m < KI:
                xsl = m % 3
                xinB.acquire("scalar", xsl)
                put("scalar", lambda e, xs=xsl, b=bk: e.copy(
                    xin_sb[:, xs * CHH + HALO:(xs + 1) * CHH], ps[b][:, 0:CH]))
                s_ev = mark("scalar", lambda e, m=m, xs=xsl: e.copy(
                    xin_sb[:, xs * CHH:xs * CHH + HALO], ps[6][:, 3 * m:3 * m + HALO]), sM)
                bank_ev[6] = ("sM", s_ev)
                wait("vector", "sM", s_ev)
                if m == 0:
                    if prev["wx_last"] > 0:
                        wait("vector", "tM", prev["wx_last"])
                    if xcsend_mark.get(j - 1, 0) > 0:
                        wait("vector", "dGP", xcsend_mark[j - 1])
                xi = xin_sb[:, xsl * CHH:(xsl + 1) * CHH]
                xc = xcv_sb[:, m * CH:(m + 1) * CH]
                put("vector", lambda e, xi=xi, xc=xc, d=m: e.tensor_scalar(
                    xc, xi[:, 0:CH], wcv_sb[:, d * 4:d * 4 + 1], None, ALU.mult))
                for kk in range(1, 4):
                    put("vector", lambda e, xi=xi, xc=xc, d=m, kk=kk: e.scalar_tensor_tensor(
                        xc, xi[:, kk:kk + CH], wcv_sb[:, d * 4 + kk:d * 4 + kk + 1], xc,
                        ALU.mult, ALU.add))
                v_cv = mark("vector", lambda e, xc=xc: e.tensor_copy(xc[:, 0:1], xc[:, 0:1]),
                            vM)
                conv_mark[(j, m)] = v_cv
                xinB.consumed(xsl, "vM", v_cv)
                wait("scalar", "vM", v_cv)
                s_ev = mark("scalar", lambda e, xc=xc, d=m: e.activation(
                    xc, xc, AF.Silu, bias=bconv_sb[:, d:d + 1]), sM)
            else:
                zsl = m % 2
                stgB.acquire("scalar", zsl)
                s_ev = mark("scalar", lambda e, b=bk, zs=zsl: e.copy(
                    stg_sb[:, zs * CH:(zs + 1) * CH], ps[b][:, 0:CH]), sM)
                wait("gpsimd", "sM", s_ev)
                g_z = dmaG(bass.AP(z_dram, (j * DI + (m - KI) * 128) * CH,
                                   [[CH, 128], [1, CH]]),
                           stg_sb[:, zsl * CH:(zsl + 1) * CH])
                zmark[j] = g_z
                stgB.consumed(zsl, "dGP", g_z)
            bank_ev[bk] = ("sM", s_ev)
        prev["win_last"] = win_last
        xc_ready = sM.v

        wait("gpsimd", "sM", xc_ready)
        if j >= 1:
            wait("gpsimd", "cc", ag_mark[("x", j - 1)])
        for kk in range(KI):
            xcsend_mark[j] = dmaG(bass.AP(xbcSend, kk * 128 * CH, [[CH, 128], [1, CH]]),
                                  xcv_sb[:, kk * CH:(kk + 1) * CH])

        gen0 = emit_scan_half_gen(j - 1, 0) if j >= 1 else iter(())

        wx_last = 0
        for m in range(KI + 1):
            next(gen0, None)
            mw = 128 if m < KI else 2 * DST
            wsl, wready, slot = w_stream(j, WxT.ap().tensor, wbf_x, DXW, m, mw, KI,
                                         off=m * 128)
            bk = m % 2
            te_take(bk)
            wait("tensor", wready[0], wready[1])
            wait("tensor", "sM", xc_ready)
            t_mm = matgroup([
                (lambda e, k=k, w=wsl, b=bk, mw=mw: e.matmul(
                    ps[b][0:mw, 0:CH], w[:, k * mw:(k + 1) * mw],
                    xcv_sb[:, k * CH:(k + 1) * CH], start=(k == 0), stop=(k == KI - 1)))
                for k in range(KI)])
            wbfB.consumed(slot, "tM", t_mm)
            wx_last = t_mm
            wait("scalar", "tM", t_mm)
            if m < KI:
                if m == 0 and prev["wdt_last"] > 0:
                    wait("scalar", "tM", prev["wdt_last"])
                s_ev = mark("scalar", lambda e, m=m, b=bk: e.copy(
                    drw_sb[:, m * CH:(m + 1) * CH], ps[b][:, 0:CH]), sM)
            else:
                s_ev = mark("scalar", lambda e, b=bk: e.copy(
                    bcstg[:, 0:CH], ps[b][0:32, 0:CH]), sM)
                wait("gpsimd", "sM", s_ev)
                dmaG(bass.AP(xbcSend, DI * CH, [[CH, 32], [1, CH]]), bcstg[:, 0:CH])
            bank_ev[bk] = ("sM", s_ev)
        for _ in gen0:
            pass
        prev["wx_last"] = wx_last
        drw_ready = sM.v
        wait("gpsimd", "dGP", dGP.v)
        if j >= 2:
            wait("gpsimd", "dSY", prev["recv_last"].get(j - 2, 0))
            wait("gpsimd", "dSC", prev["bex_round_last"].get(j - 2, 0))
        ag_mark[("x", j)] = mark("gpsimd", lambda e, jb=jb: e.collective_compute(
            "AllGather", ALU.bypass, ins=[xbcSend.ap()], outs=[xbcAGo[jb].ap()],
            replica_groups=G4), cc)
        quiesce(tier=2)

        gen1 = emit_scan_half_gen(j - 1, 1) if j >= 1 else iter(())
        wdt_last = 0
        for m in range(KI):
            next(gen1, None)
            next(gen1, None)
            wsl, wready, slot = w_stream(j, WdtT.ap().tensor, wbf_dt, DI, m, 128, KI)
            bk = m % 2
            te_take(bk)
            wait("tensor", wready[0], wready[1])
            wait("tensor", "sM", drw_ready)
            t_mm = matgroup([
                (lambda e, k=k, w=wsl, b=bk: e.matmul(
                    ps[b][:, 0:CH], w[:, k * 128:(k + 1) * 128],
                    drw_sb[:, k * CH:(k + 1) * CH], start=(k == 0), stop=(k == KI - 1)))
                for k in range(KI)])
            wbfB.consumed(slot, "tM", t_mm)
            wdt_last = t_mm
            wait("scalar", "tM", t_mm)
            esl = m % 2
            stg2B.acquire("scalar", esl)
            put("scalar", lambda e, b=bk, es=esl, m=m: e.activation(
                stg2_sb[:, es * CH:(es + 1) * CH], ps[b][:, 0:CH], AF.Exp,
                bias=bdt_sb[:, m:m + 1]))
            s_ev = mark("scalar", lambda e, es=esl: e.activation(
                stg2_sb[:, es * CH:(es + 1) * CH], stg2_sb[:, es * CH:(es + 1) * CH],
                AF.Ln, bias=1.0), sM)
            bank_ev[bk] = ("sM", s_ev)
            wait("gpsimd", "sM", s_ev)
            if m == 0 and j >= 1:
                wait("gpsimd", "cc", ag_mark[("d", j - 1)])
            g_d = dmaG(bass.AP(dSend, m * 128 * CH, [[CH, 128], [1, CH]]),
                       stg2_sb[:, esl * CH:(esl + 1) * CH])
            stg2B.consumed(esl, "dGP", g_d)
        prev["wdt_last"] = wdt_last
        wait("gpsimd", "dGP", dGP.v)
        if j >= 2:
            wait("gpsimd", "dSY", prev["recv_last"].get(j - 2, 0))
        ag_mark[("d", j)] = mark("gpsimd", lambda e, jb=jb: e.collective_compute(
            "AllGather", ALU.bypass, ins=[dSend.ap()], outs=[dAGo[jb].ap()],
            replica_groups=G4), cc)

        if debug and j == 0:
            wait("gpsimd", "cc", ag_mark[("d", 0)])
            dmaG(dbg["dsend0"].ap(), dSend.ap())
            dmaG(dbg["xbc0"].ap(), xbcSend.ap())

        quiesce(tier=2)

        for _ in gen1:
            pass
        if j >= 2:
            emit_gating(j - 2)
        quiesce(tier=2)

    emit_recv(RNDS - 1)
    emit_scan_half(RNDS - 1, 0)
    quiesce()

    wait("vector", "vM", prev["v_xm"])
    if debug:
        put("vector", lambda e: e.tensor_copy(dbgx_sb[:, 4:6], xmax_sb[:, 0:2]))
        put("vector", lambda e: e.tensor_copy(dbgx_sb[:, 8:16], qs_sb[:, 16:24]))
    v_fl = mark("vector", lambda e: e.tensor_copy(qs_sb[:, 23:24], xmax_sb[:, 0:1]), vM)
    wait("sync", "vM", v_fl)
    w_flip = dmaS(flipD.ap(), qs_sb[:, 23:24])
    wait("sync", "dSY", w_flip)
    fl2 = dmaS(srow[0:1, 0:128], flipD.ap()[None, :])
    wait("vector", "dSY", fl2)
    v_m1 = mark("vector", lambda e: e.tensor_reduce(
        r16[0:1, 0:1], srow[0:1, 0:128], axis=AX.X, op=ALU.max), vM)
    wait("sync", "vM", v_m1)
    ar1_in = dmaS(mx_in.ap()[None, :], r16[0:1, 0:16])
    wait("gpsimd", "dSY", ar1_in)
    cc_ar1 = mark("gpsimd", lambda e: e.collective_compute(
        "AllReduce", ALU.max, ins=[mx_in.ap()], outs=[mx1_out.ap()],
        replica_groups=G8), cc)
    wait("sync", "cc", cc_ar1)
    if debug:
        wait("gpsimd", "cc", cc_ar1)
        dmaG(bass.AP(dbg["ar1"].ap().tensor, 0, [[1, 128]]), flipD.ap())
        dmaG(bass.AP(dbg["ar1"].ap().tensor, 128, [[1, 16]]), mx_in.ap())
        dmaG(bass.AP(dbg["ar1"].ap().tensor, 144, [[1, 16]]), mx1_out.ap())
    m1_ld = dmaS(r16[0:1, 0:16], mx1_out.ap()[None, :])
    wait("vector", "dSY", m1_ld)
    put("vector", lambda e: e.tensor_scalar_max(r16[0:1, 0:1], r16[0:1, 0:1], 1e-8))
    put("vector", lambda e: e.tensor_scalar_mul(r16[0:1, 1:2], r16[0:1, 0:1], 1.0 / 255.0))
    v_hz = mark("vector", lambda e: e.reciprocal(rcp1_sb[:], r16[0:1, 0:1]), vM)
    wait("vector", "vM", v_hz)
    v_s1 = mark("vector", lambda e: e.tensor_scalar_mul(r16[0:1, 2:3], rcp1_sb[:],
                                                        255.0), vM)
    wait("sync", "vM", v_s1)
    w_scl = dmaS(bass.AP(sclD, 0, [[4, 1], [1, 4]]), r16[0:1, 0:4])
    wait("sync", "dSY", w_scl)
    s1b = dmaS(scl_sb[:, 0:1], bass.AP(sclD, 2, [[0, 128], [1, 1]]))
    s1c = dmaS(scl_sb[:, 2:3], bass.AP(sclD, 1, [[0, 128], [1, 1]]))
    wait("vector", "dSY", s1c)
    v_hz = mark("vector", lambda e: e.tensor_scalar(scale1_sb[:], sw1_sb[:],
                                                    scl_sb[:, 2:3], None, ALU.mult), vM)
    wait("vector", "vM", v_hz)
    v_hz = mark("vector", lambda e: e.reciprocal(bq1_sb[:], scale1_sb[:]), vM)
    wait("vector", "vM", v_hz)
    v_bq = mark("vector", lambda e: e.tensor_mul(bq1_sb[:], b1_sb[:], bq1_sb[:]), vM)
    wait("vector", "vM", v_bq)
    wait("scalar", "vM", v_bq)
    s_hz2 = mark("scalar", lambda e: e.copy(wqi_sb[:, 0:MF], bq1_sb[:]), sM)
    wait("scalar", "sM", s_hz2)
    s_bq = mark("scalar", lambda e: e.copy(bq1_sb[:], wqi_sb[:, 0:MF]), sM)
    wait("vector", "sM", s_bq)
    v_bq1 = mark("vector", lambda e: e.tensor_mul(bq1_sb[:], bq1_sb[:], scale1_sb[:]), vM)

    quiesce()

    def xq_tile(idx):
        base = xcv_sb if idx < KI else drw_sb
        i = idx % KI
        return base[:, i * CH:(i + 1) * CH]

    wait("scalar", "tM", prev["wx_last"])
    wait("scalar", "tM", prev["wdt_last"])
    wait("scalar", "dGP", xcsend_mark[RNDS - 1])
    wait("scalar", "dSY", s1b)
    wait("sync", "dGP", prev["xn_spill"])
    wait("sync", "cc", cc_wq)
    xq_last = 0
    for j in range(RNDS):
        for k in range(KD):
            xsl = k % 2
            stg2B.acquire("sync", xsl)
            ldxn = dmaS(stg2_sb[:, xsl * CH:(xsl + 1) * CH],
                        bass.AP(xn_dram, j * DM * CHH + k * 128 * CHH + HALO,
                                [[CHH, 128], [1, CH]]))
            wait("scalar", "dSY", ldxn)
            iqB.acquire("scalar", xsl)
            s_hz = mark("scalar", lambda e, xs=xsl: e.activation(
                iq_sb[:, xs * CH:(xs + 1) * CH], stg2_sb[:, xs * CH:(xs + 1) * CH],
                AF.Copy, scale=scl_sb[:, 0:1]), sM)
            wait("scalar", "sM", s_hz)
            s_i = mark("scalar", lambda e, j=j, k=k, xs=xsl: e.copy(
                xq_tile(j * KD + k), iq_sb[:, xs * CH:(xs + 1) * CH]), sM)
            stg2B.consumed(xsl, "sM", s_i)
            wait("vector", "sM", s_i)
            xq_last = mark("vector", lambda e, j=j, k=k: e.tensor_scalar_max(
                xq_tile(j * KD + k), xq_tile(j * KD + k), 0.0), vM)

    quiesce()

    wait("tensor", "cc", cc_wq)
    wait("tensor", "vM", xq_last)
    wait("scalar", "vM", v_bq1)
    hmax_v = 0
    for m in range(MF):
        sl = m % 2
        wbfB.acquire("sync", sl)
        lc1, lv1 = dma_slot("sync", wbfB, sl, wbf[:, sl * 2048:sl * 2048 + KD * 128],
                            bass.AP(wq_out, m * 128, [[4096, 128], [WQBLK, KD], [1, 128]]))
        wait("tensor", lc1, lv1)
        t_grp = 0
        for j in range(RNDS):
            bk = (m * RNDS + j) % 2
            te_take(bk)
            t_grp = matgroup([
                (lambda e, k=k, j=j, b=bk, s=sl: e.matmul(
                    ps[b][:, 0:CH], wbf[:, s * 2048 + k * 128:s * 2048 + (k + 1) * 128],
                    xq_tile(j * KD + k), start=(k == 0), stop=(k == KD - 1)))
                for k in range(KD)])
            wait("scalar", "tM", t_grp)
            hsl = (m * RNDS + j) % 2
            stgB.acquire("scalar", hsl)
            s_h = mark("scalar", lambda e, b=bk, hs=hsl, m=m: e.activation(
                stg_sb[:, hs * CH:(hs + 1) * CH], ps[b][:, 0:CH], AF.Relu,
                bias=bq1_sb[:, m:m + 1], scale=scale1_sb[:, m:m + 1]), sM)
            bank_ev[bk] = ("sM", s_h)
            wait("gpsimd", "sM", s_h)
            g_h = dmaG(bass.AP(h_dram, (m * 128) * TLOC + j * CH, [[TLOC, 128], [1, CH]]),
                       stg_sb[:, hsl * CH:(hsl + 1) * CH])
            stgB.consumed(hsl, "dGP", g_h)
            wait("vector", "sM", s_h)
            v_hz = mark("vector", lambda e, hs=hsl: e.tensor_reduce(
                qs_sb[:, 22:23], stg_sb[:, hs * CH:(hs + 1) * CH], axis=AX.X,
                op=ALU.max), vM)
            wait("vector", "vM", v_hz)
            hmax_v = mark("vector", lambda e: e.tensor_max(
                xmax_sb[:, 1:2], xmax_sb[:, 1:2], qs_sb[:, 22:23]), vM)
            stgB.consumed(hsl, "vM", hmax_v)
        wbfB.consumed(sl, "tM", t_grp)

    quiesce()
    h_done = dGP.v

    emit_scan_half(RNDS - 1, 1)
    if RNDS >= 2:
        emit_gating(RNDS - 2)

    quiesce()

    wait("vector", "vM", hmax_v)
    put("vector", lambda e: e.tensor_scalar_max(xmax_sb[:, 1:2], xmax_sb[:, 1:2], 0.0))
    v_fl2 = mark("vector", lambda e: e.tensor_copy(qs_sb[:, 23:24], xmax_sb[:, 1:2]), vM)
    wait("sync", "vM", v_fl2)
    w_flip2 = dmaS(flipD.ap(), qs_sb[:, 23:24])
    wait("sync", "dSY", w_flip2)
    fl3 = dmaS(srow[0:1, 0:128], flipD.ap()[None, :])
    wait("vector", "dSY", fl3)
    v_m2 = mark("vector", lambda e: e.tensor_reduce(
        r16[0:1, 0:1], srow[0:1, 0:128], axis=AX.X, op=ALU.max), vM)
    wait("sync", "vM", v_m2)
    ar2_in = dmaS(mx_in.ap()[None, :], r16[0:1, 0:16])
    wait("gpsimd", "dSY", ar2_in)
    cc_ar2 = mark("gpsimd", lambda e: e.collective_compute(
        "AllReduce", ALU.max, ins=[mx_in.ap()], outs=[mx2_out.ap()],
        replica_groups=G8), cc)
    wait("sync", "cc", cc_ar2)
    m2_ld = dmaS(r16[0:1, 0:16], mx2_out.ap()[None, :])
    wait("vector", "dSY", m2_ld)
    put("vector", lambda e: e.tensor_scalar_max(r16[0:1, 0:1], r16[0:1, 0:1], 1e-8))
    put("vector", lambda e: e.tensor_scalar_mul(r16[0:1, 1:2], r16[0:1, 0:1], 1.0 / 255.0))
    v_hz = mark("vector", lambda e: e.reciprocal(rcp1_sb[:], r16[0:1, 0:1]), vM)
    wait("vector", "vM", v_hz)
    v_s2 = mark("vector", lambda e: e.tensor_scalar_mul(r16[0:1, 2:3], rcp1_sb[:],
                                                        255.0), vM)
    wait("sync", "vM", v_s2)
    w_scl2 = dmaS(bass.AP(sclD, 8, [[4, 1], [1, 4]]), r16[0:1, 0:4])
    wait("sync", "dSY", w_scl2)
    s2b = dmaS(scl_sb[:, 1:2], bass.AP(sclD, 10, [[0, 128], [1, 1]]))
    s2c = dmaS(scl_sb[:, 3:4], bass.AP(sclD, 9, [[0, 128], [1, 1]]))
    wait("vector", "dSY", s2c)
    v_hz = mark("vector", lambda e: e.tensor_scalar(scale2_sb[:], sw2_sb[:],
                                                    scl_sb[:, 3:4], None, ALU.mult), vM)
    wait("vector", "vM", v_hz)
    v_hz = mark("vector", lambda e: e.reciprocal(bq2_sb[:], scale2_sb[:]), vM)
    wait("vector", "vM", v_hz)
    v_bq2a = mark("vector", lambda e: e.tensor_mul(bq2_sb[:], b2_sb[:], bq2_sb[:]), vM)
    wait("vector", "vM", v_bq2a)
    wait("scalar", "vM", v_bq2a)
    s_hz2 = mark("scalar", lambda e: e.copy(wqi_sb[:, 0:KD], bq2_sb[:]), sM)
    wait("scalar", "sM", s_hz2)
    s_bq2 = mark("scalar", lambda e: e.copy(bq2_sb[:], wqi_sb[:, 0:KD]), sM)
    wait("vector", "sM", s_bq2)
    v_bq2 = mark("vector", lambda e: e.tensor_mul(bq2_sb[:], bq2_sb[:], scale2_sb[:]), vM)

    def hq_tile(idx):
        base = dR_sb if idx < KI else xcR_sb
        i = idx % KI
        return base[:, i * CH:(i + 1) * CH]

    quiesce()
    last_cm = cm_last[(RNDS - 1, 1, DST - 1)]
    last_yf = yfin_last[(RNDS - 1, 1)]
    wait("scalar", "vM", max(last_cm, last_yf))
    wait("scalar", "vM", v_bq2)
    wait("scalar", "dSY", s2b)
    wait("sync", "dGP", h_done)
    for j in range(RNDS):
        hq_last = 0
        for k in range(KH):
            hsl = k % 2
            stg2B.acquire("sync", hsl)
            ldh = dmaS(stg2_sb[:, hsl * CH:(hsl + 1) * CH],
                       bass.AP(h_dram, (k * 128) * TLOC + j * CH, [[TLOC, 128], [1, CH]]))
            wait("scalar", "dSY", ldh)
            iqB.acquire("scalar", hsl)
            s_hz = mark("scalar", lambda e, hs=hsl: e.activation(
                iq_sb[:, hs * CH:(hs + 1) * CH], stg2_sb[:, hs * CH:(hs + 1) * CH],
                AF.Copy, scale=scl_sb[:, 1:2]), sM)
            wait("scalar", "sM", s_hz)
            s_hq = mark("scalar", lambda e, k=k, hs=hsl: e.copy(
                hq_tile(k), iq_sb[:, hs * CH:(hs + 1) * CH]), sM)
            stg2B.consumed(hsl, "sM", s_hq)
            wait("vector", "sM", s_hq)
            hq_last = mark("vector", lambda e, k=k: e.tensor_scalar_max(
                hq_tile(k), hq_tile(k), 0.0), vM)
        if debug and j == 0:
            wait("gpsimd", "vM", hq_last)
            dmaG(bass.AP(dbg["hq0"].ap().tensor, 0, [[4 * CH, 128], [1, 4 * CH]]),
                 dR_sb[:, 0:4 * CH])
        wait("tensor", "vM", hq_last)
        for m in range(KD):
            sl = m % 2
            wbfB.acquire("sync", 0)
            wbfB.acquire("sync", 1)
            for r in range(8):
                lc2, lv2 = dma_slot("sync", wbfB, 0, wbf[:, r * 512:(r + 1) * 512],
                                    bass.AP(wq_out, r * WQBLK + 128 * 4096 + m * 128,
                                            [[DM, 128], [128 * DM, 4], [1, 128]]))
            te_take(m)
            wait("tensor", lc2, lv2)
            t_mm = matgroup([
                (lambda e, k=k, m=m: e.matmul(
                    ps[m][:, 0:CH], wbf[:, k * 128:(k + 1) * 128],
                    hq_tile(k), start=(k == 0), stop=(k == KH - 1)))
                for k in range(KH)])
            wbfB.consumed(0, "tM", t_mm)
            wbfB.consumed(1, "tM", t_mm)
            wait("scalar", "tM", t_mm)
            gsl = m % 2
            stgB.acquire("scalar", gsl)
            s_g = mark("scalar", lambda e, m=m, gs=gsl: e.activation(
                stg_sb[:, gs * CH:(gs + 1) * CH], ps[m][:, 0:CH], AF.Silu,
                bias=bq2_sb[:, m:m + 1], scale=scale2_sb[:, m:m + 1]), sM)
            bank_ev[m] = ("sM", s_g)
            wait("gpsimd", "sM", s_g)
            g_g = dmaG(bass.AP(g_dram, (m * 128) * TLOC + j * CH, [[TLOC, 128], [1, CH]]),
                       stg_sb[:, gsl * CH:(gsl + 1) * CH])
            stgB.consumed(gsl, "dGP", g_g)
    g_done = dGP.v

    quiesce()

    emit_gating(RNDS - 1)
    quiesce()
    s_done = dGP.v

    wait("sync", "dGP", max(g_done, s_done))
    out_last = 0
    for j in range(RNDS):
        for kt in range(KD):
            osl = kt % 2
            xroB.acquire("sync", 0)
            ldx = dmaS(xro_sb[:],
                       x_fm.ap()[kt * 128:(kt + 1) * 128, j * CHH + HALO:(j + 1) * CHH])
            stgB.acquire("sync", osl)
            lds = dmaS(stg_sb[:, osl * CH:(osl + 1) * CH],
                       bass.AP(s_dram, (j * DM + kt * 128) * CH, [[CH, 128], [1, CH]]))
            stg2B.acquire("sync", osl)
            ldg = dmaS(stg2_sb[:, osl * CH:(osl + 1) * CH],
                       bass.AP(g_dram, (kt * 128) * TLOC + j * CH, [[TLOC, 128], [1, CH]]))
            wait("vector", "dSY", ldg)
            v_p = mark("vector", lambda e, os=osl: e.tensor_mul(
                stg_sb[:, os * CH:(os + 1) * CH], stg_sb[:, os * CH:(os + 1) * CH],
                stg2_sb[:, os * CH:(os + 1) * CH]), vM)
            stg2B.consumed(osl, "vM", v_p)
            outB.acquire("vector", 0)
            v_o = mark("vector", lambda e, os=osl: e.tensor_add(
                out_sb[:], xro_sb[:],
                stg_sb[:, os * CH:(os + 1) * CH]), vM)
            stgB.consumed(osl, "vM", v_o)
            xroB.consumed(0, "vM", v_o)
            wait("sync", "vM", v_o)
            out_last = dmaS(out_fm.ap()[kt * 128:(kt + 1) * 128, j * CH:(j + 1) * CH],
                            out_sb[:])
            outB.consumed(0, "dSY", out_last)

    if debug:
        wait("gpsimd", "dGP", dGP.v)
        wait("gpsimd", "dSY", out_last)
        dmaG(dbg["y0"].ap(), ySend.ap())
        dmaG(dbg["h0"].ap(), h_dram.ap())
        dmaG(dbg["g0"].ap(), g_dram.ap())
        dmaG(dbg["s0"].ap(), s_dram.ap())
        dmaG(dbg["xn0"].ap(), xn_dram.ap())
        dmaG(bass.AP(dbg["sc"].ap().tensor, 0, [[1, 4]]), bass.AP(sclD, 0, [[1, 4]]))
        dmaG(bass.AP(dbg["sc"].ap().tensor, 8, [[1, 4]]), bass.AP(sclD, 8, [[1, 4]]))
        dmaG(dbg["wq"].ap(), wq_out.ap())
        dmaG(bass.AP(dbg["carry"].ap().tensor, 0, [[NDT * DST, 128], [1, NDT * DST]]),
             carry[:])
        dmaG(bass.AP(dbg["xm"].ap().tensor, 0, [[16, 128], [1, 16]]), dbgx_sb[:])
        dmaG(dbg["wqin"].ap(), wq_in.ap())
        dmaG(dbg["swq"].ap(), swq_out.ap())
        dmaG(bass.AP(dbg["xq"].ap().tensor, 0, [[KI * CH, 128], [1, KI * CH]]), xcv_sb[:])
        dmaG(bass.AP(dbg["xq"].ap().tensor, DI * CH, [[KI * CH, 128], [1, KI * CH]]), drw_sb[:])
        dmaG(bass.AP(dbg["q1"].ap().tensor, 0, [[2 * MF, 128], [1, MF]]), scale1_sb[:])
        dmaG(bass.AP(dbg["q1"].ap().tensor, MF, [[2 * MF, 128], [1, MF]]), bq1_sb[:])
        dmaG(bass.AP(dbg["q2"].ap().tensor, 0, [[2 * KD, 128], [1, KD]]), scale2_sb[:])
        dmaG(bass.AP(dbg["q2"].ap().tensor, KD, [[2 * KD, 128], [1, KD]]), bq2_sb[:])


    wait("sync", "dSY", out_last)
    wait("gpsimd", "dSY", out_last)
    wait("gpsimd", "dGP", dGP.v)

    if vec_limit is not None:
        if isinstance(vec_limit, dict):
            for k, n in vec_limit.items():
                plan[k] = plan[k][:n]
        else:
            plan["vector"] = plan["vector"][:vec_limit]

    with nc.Block() as block:
        @block.sync
        def _(e):
            for f in plan["sync"]:
                f(e)

        @block.tensor
        def _(e):
            for f in plan["tensor"]:
                f(e)

        @block.vector
        def _(e):
            for f in plan["vector"]:
                f(e)

        @block.scalar
        def _(e):
            for f in plan["scalar"]:
                f(e)

        @block.gpsimd
        def _(e):
            for f in plan["gpsimd"]:
                f(e)

    stack.close()
    return nc


def _layout_inputs(inputs, T_total, CH):
    NCHUNK = T_total // CH
    RNDS = NCHUNK // GRP
    CHH = CH + HALO
    x = np.asarray(inputs["L0_attn_x"], np.float32)
    WinT = np.ascontiguousarray(np.asarray(inputs["L0_attn_W_in"], np.float32).T)
    WxT = np.ascontiguousarray(np.asarray(inputs["L0_attn_W_x"], np.float32).T)
    WdtT = np.ascontiguousarray(np.asarray(inputs["L0_attn_W_dt"], np.float32).T)
    WoutT = np.ascontiguousarray(np.asarray(inputs["L0_attn_W_out"], np.float32).T)
    w1 = np.asarray(inputs["L0_ffn_w1"], np.float32)
    w2 = np.asarray(inputs["L0_ffn_w2"], np.float32)
    w1T = np.ascontiguousarray(w1.T)
    w2T = np.ascontiguousarray(w2.T)
    bconv = np.ascontiguousarray(
        np.asarray(inputs["L0_attn_b_conv"], np.float32).reshape(KI, 128).T)
    bdt = np.ascontiguousarray(
        np.asarray(inputs["L0_attn_b_dt"], np.float32).reshape(KI, 128).T)
    wcv = np.ascontiguousarray(
        np.asarray(inputs["L0_attn_w_conv"], np.float32).reshape(KI, 128, 4)
        .transpose(1, 0, 2).reshape(128, KI * 4))
    Alog = np.asarray(inputs["L0_attn_A_log"], np.float32)
    Dv = np.asarray(inputs["L0_attn_D"], np.float32)
    b1 = np.ascontiguousarray(np.asarray(inputs["L0_ffn_b1"], np.float32).reshape(MF, 128).T)
    b2 = np.ascontiguousarray(np.asarray(inputs["L0_ffn_b2"], np.float32).reshape(KD, 128).T)

    xp = np.pad(x, ((0, 0), (HALO, 0), (0, 0)))
    DQ = DI // GRP
    NDT = DQ // 128
    in_maps = []
    for c in range(8):
        b, q = divmod(c, GRP)
        cols = []
        for j in range(RNDS):
            g = q + GRP * j
            cols.append(xp[b, g * CH:g * CH + CHH, :].T)
        x_c = np.ascontiguousarray(np.concatenate(cols, axis=1), np.float32)
        dq0 = q * DQ
        Alog_my = np.ascontiguousarray(
            Alog[dq0:dq0 + DQ].reshape(NDT, 128, DST).transpose(1, 0, 2)
            .reshape(128, NDT * DST))
        D_my = np.ascontiguousarray(Dv[dq0:dq0 + DQ].reshape(NDT, 128).T)
        in_maps.append({
            "x_fm": x_c,
            "WinT": WinT, "WxT": WxT, "WdtT": WdtT, "WoutT": WoutT,
            "w1T_myk": np.ascontiguousarray(w1T[c * 128:(c + 1) * 128]),
            "w2T_myk": np.ascontiguousarray(w2T[c * 512:(c + 1) * 512]),
            "w1_rows": np.ascontiguousarray(w1[c * 512:(c + 1) * 512]),
            "w2_rows": np.ascontiguousarray(w2[c * 128:(c + 1) * 128]),
            "bconv_t": bconv, "bdt_t": bdt, "wcv_t": wcv,
            "Alog_my": Alog_my, "D_my": D_my,
            "b1_t": b1, "b2_t": b2,
            "rank_in": np.array([[q, q * CH, q * (DI // GRP // 128), 0]], np.int32),
        })
    return in_maps


def kernel(**inputs):
    T_total = int(np.asarray(inputs["L0_attn_x"]).shape[1])
    CH = 512 if T_total >= GRP * 512 else T_total // (GRP * 4)
    nc = build_program(T_total=T_total, CH=CH)
    in_maps = _layout_inputs(inputs, T_total, CH)
    res = run_bass_kernel_spmd(nc, in_maps, core_ids=list(range(8)))
    RNDS = T_total // CH // GRP
    out = np.empty((BATCH, T_total, DM), np.float32)
    for c in range(8):
        b, q = divmod(c, GRP)
        o = np.asarray(res.results[c]["out_fm"])
        for j in range(RNDS):
            g = q + GRP * j
            out[b, g * CH:(g + 1) * CH, :] = o[:, j * CH:(j + 1) * CH].T
    return out
